# version 02
# baseline (speedup 1.0000x reference)
import contextlib

import numpy as np
import ml_dtypes

import concourse.bass as bass
import concourse.mybir as mybir
from concourse.bass_utils import run_bass_kernel_spmd

F32 = mybir.dt.float32
BF16 = mybir.dt.bfloat16
AF = mybir.ActivationFunctionType
ALU = mybir.AluOpType
AX = mybir.AxisListType

BF = ml_dtypes.bfloat16

D = 1024
T = 2048
NB = 8
DC = D // 128
NT = T // 512
TC = T // 128

ENGINES = ("tensor", "vector", "scalar", "gpsimd", "sync")
DMA_QUEUES = ("sync", "gpsimd", "scalar")
DSEM_K = 24

GROUPS = [(0, 1, 2, 3)]


class _Op:
    __slots__ = ("fn", "waits", "inc", "dma", "tag")

    def __init__(self, fn, waits, dma, tag):
        self.fn = fn
        self.waits = waits
        self.inc = False
        self.dma = dma
        self.tag = tag


class PsumBuf:
    def __init__(self, nc, handle):
        self.h = handle
        m = nc.lookup_mloc(handle)
        nbytes = int(m.dims[1]) if len(m.dims) > 1 else int(m.dims[0])
        b0 = int(m.bank)
        nbanks = max(1, (nbytes + 2047) // 2048)
        self.keys = tuple(("psum", b) for b in range(b0, b0 + nbanks))

    def __getitem__(self, key):
        return self.h[key]


class PsumView:

    def __init__(self, buf, col0, ncols, elsz=4):
        self.buf = buf
        self.col0 = col0
        self.ncols = ncols
        b0 = buf.keys[0][1]
        lo = b0 + (col0 * elsz) // 2048
        hi = b0 + ((col0 + ncols) * elsz - 1) // 2048
        self.keys = tuple(("psum", b) for b in range(lo, hi + 1))

    def __getitem__(self, key):
        rows, cols = key
        if isinstance(cols, slice):
            start = (cols.start or 0) + self.col0
            stop = (cols.stop if cols.stop is not None else self.ncols) + self.col0
            cols = slice(start, stop)
        else:
            raise TypeError(cols)
        return self.buf.h[rows, cols]


class SbView:

    def __init__(self, h, col0, ncols, nrows=128, keys=None):
        self.h = h
        self.col0 = col0
        self.ncols = ncols
        self.nrows = nrows
        self.keys = keys

    def __getitem__(self, key):
        rows, cols = key
        if isinstance(rows, slice) and rows == slice(None):
            rows = slice(0, self.nrows)
        if isinstance(cols, slice):
            start = (cols.start or 0) + self.col0
            stop = (cols.stop if cols.stop is not None else self.ncols) + self.col0
            cols = slice(start, stop)
        else:
            raise TypeError(cols)
        return self.h[rows, cols]


_PINS = []


class P:

    def __init__(self, nc):
        self.nc = nc
        self.stack = contextlib.ExitStack()
        self.cur_stack = self.stack
        self._kcache = {}
        self.stream = {e: [] for e in ENGINES}
        self.lastw = {}
        self.readers = {}
        self.dma_n = {e: 0 for e in ENGINES}
        self.waited = {e: {} for e in ENGINES}

    def sb(self, name, shape, dtype):
        return self.cur_stack.enter_context(self.nc.sbuf_tensor(name, list(shape), dtype))

    @contextlib.contextmanager
    def scope(self):
        st = contextlib.ExitStack()
        old = self.cur_stack
        self.cur_stack = st
        try:
            yield
        finally:
            st.close()
            self.cur_stack = old

    def _keys(self, objs):
        out = []
        for o in objs:
            if isinstance(o, (PsumBuf, PsumView, SbView)):
                out.extend(o.keys)
            elif isinstance(o, tuple):
                out.append(o)
            else:
                kk = self._kcache.get(id(o))
                if kk is None:
                    _PINS.append(o)
                    try:
                        space = getattr(o, "space", None)
                        is_sb = space is not None and "SB" in str(space)
                    except Exception:
                        is_sb = False
                    if is_sb:
                        m = self.nc.lookup_mloc(o)
                        addr = int(m.addr)
                        nbytes = int(m.dims[1]) if len(m.dims) > 1 else int(m.dims[0])
                        g0 = addr // 512
                        g1 = (addr + nbytes - 1) // 512
                        kk = tuple(("sb", g) for g in range(g0, g1 + 1))
                    else:
                        kk = (("t", id(o)),)
                    self._kcache[id(o)] = kk
                out.extend(kk)
        return out

    def sbpool(self, name, ncols, dtype):
        h = self.sb(name, [128, ncols], dtype)
        elsz = 4 if dtype == F32 else 2
        m = self.nc.lookup_mloc(h)
        addr = int(m.addr)
        return h, addr, elsz

    def view(self, pool, col0, ncols, nrows=128):
        h, addr, elsz = pool
        g0 = (addr + col0 * elsz) // 512
        g1 = (addr + (col0 + ncols) * elsz - 1) // 512
        keys = tuple(("sb", g) for g in range(g0, g1 + 1))
        return SbView(h, col0, ncols, nrows, keys)

    def ps(self, name, shape, dtype=F32):
        h = self.stack.enter_context(self.nc.psum_tensor(name, list(shape), dtype))
        return PsumBuf(self.nc, h)

    def dram(self, name, shape, dtype, kind="Internal"):
        return self.nc.dram_tensor(name, list(shape), dtype, kind=kind)

    def _deps(self, reads, writes):
        deps = []
        for b in reads:
            t = self.lastw.get(b)
            if t is not None:
                deps.append(t)
        for b in writes:
            deps.extend(self.readers.get(b, ()))
            t = self.lastw.get(b)
            if t is not None:
                deps.append(t)
        return deps

    def _resolve(self, eng, deps):
        out = []
        w = self.waited[eng]
        for t in deps:
            if t[0] == "E":
                _, peng, idx = t
                if peng == eng and peng == "tensor":
                    continue
                k = ("E", peng)
                if w.get(k, -1) >= idx:
                    continue
                w[k] = idx
                out.append(t)
            else:
                _, qeng, cnt = t
                slot = (cnt - 1) % DSEM_K
                use = (cnt - 1) // DSEM_K + 1
                k = ("D", qeng, slot)
                if w.get(k, 0) >= use:
                    continue
                w[k] = use
                out.append(t)
        return out

    def _record(self, eng, fn, reads, writes, dma, tag):
        reads = self._keys(reads)
        writes = self._keys(writes)
        deps = self._deps(reads, writes)
        waits = self._resolve(eng, deps)
        for t in waits:
            if t[0] == "E":
                self.stream[t[1]][t[2]].inc = True
        if dma:
            self.dma_n[eng] += 1
            tok = ("D", eng, self.dma_n[eng])
        else:
            tok = ("E", eng, len(self.stream[eng]))
        self.stream[eng].append(_Op(fn, waits, dma, tag))
        for b in reads:
            self.readers.setdefault(b, []).append(tok)
        for b in writes:
            self.lastw[b] = tok
            self.readers[b] = []
        return tok

    def op(self, eng, fn, reads=(), writes=(), tag=""):
        return self._record(eng, fn, reads, writes, False, tag)

    def dma(self, qeng, out_ap, in_ap, reads=(), writes=(), tag="", transpose=False):
        if transpose:
            fn = lambda e: e.dma_start_transpose(out_ap, in_ap)
        else:
            fn = lambda e: e.dma_start(out=out_ap, in_=in_ap)
        return self._record(qeng, fn, reads, writes, True, tag)

    def emit_and_close(self):
        nc = self.nc
        sems = {e: self.stack.enter_context(nc.semaphore("sem_" + e)) for e in ENGINES}
        dsems = {q: [self.stack.enter_context(nc.semaphore(f"dsem_{q}{i}"))
                     for i in range(DSEM_K)] for q in DMA_QUEUES}
        rank = {}
        for e in ENGINES:
            c = 0
            r = []
            for o in self.stream[e]:
                if o.inc and not o.dma:
                    c += 1
                r.append(c)
            rank[e] = r

        def make_body(e):
            def body(eng):
                nd = 0
                for o in self.stream[e]:
                    for t in o.waits:
                        if t[0] == "E":
                            eng.wait_ge(sems[t[1]], rank[t[1]][t[2]])
                        else:
                            _, q, cnt = t
                            slot = (cnt - 1) % DSEM_K
                            use = (cnt - 1) // DSEM_K + 1
                            eng.wait_ge(dsems[q][slot], 16 * use)
                    if o.dma:
                        slot = nd % DSEM_K
                        use = nd // DSEM_K + 1
                        if use > 1:
                            eng.wait_ge(dsems[e][slot], 16 * (use - 1))
                        ins = o.fn(eng)
                        ins.then_inc(dsems[e][slot], 16)
                        nd += 1
                    else:
                        ins = o.fn(eng)
                        if o.inc:
                            ins.then_inc(sems[e], 1)
                if e == "sync":
                    for q in DMA_QUEUES:
                        n = self.dma_n[q]
                        for slot in range(min(n, DSEM_K)):
                            last_use = (n - 1 - slot) // DSEM_K + 1
                            eng.wait_ge(dsems[q][slot], 16 * last_use)
            return body

        with nc.Block() as block:
            block.tensor(make_body("tensor"))
            block.vector(make_body("vector"))
            block.scalar(make_body("scalar"))
            block.gpsimd(make_body("gpsimd"))
            block.sync(make_body("sync"))
        self.stack.close()

    def n_ops(self):
        d = {e: len(self.stream[e]) for e in ENGINES}
        d["total"] = sum(d.values())
        return d



def host_constants():
    c = {}
    c["c_ones"] = np.ones((128, 128), np.float32)
    c["c_ones_bf"] = np.ones((128, 256), BF)
    c["c_ident_bf"] = np.eye(128, dtype=BF)
    c["c_ident_f32"] = np.eye(128, dtype=np.float32)
    tsel = np.zeros((4, 512), np.float32)
    for k in range(4):
        tsel[k, k * 128:(k + 1) * 128] = 1.0
    c["c_tsel"] = tsel
    c["c_zero16"] = np.zeros((1, 16), np.float32)
    c["c_zrow"] = np.zeros((1, 512), np.float32)
    offs = [0] + [1 << k for k in range(11)]
    sa = np.zeros((128, 12 * 24), BF)
    for o in range(12):
        for pp in range(128):
            sa[pp, 24 * o + (pp // 64) * 12 + o] = 1.0
    c["c_l2sa"] = sa
    lm = np.zeros((24, 2048), np.float32)
    for h in range(2):
        for o in range(12):
            lm[h * 12 + o, 0:offs[o]] = -1e30
    c["c_l2mask"] = lm.astype(BF)
    pb = np.zeros((24, 12 * 128), BF)
    for o in range(12):
        for j in range(128):
            pb[(j // 64) * 12 + o, 128 * o + j] = 1.0
    c["c_l2pb"] = pb
    dn = np.zeros((24, 2), BF)
    dn[0:12, 0] = 1.0
    dn[12:24, 1] = 1.0
    c["c_l2den"] = dn
    rb2 = np.zeros((2, 128), BF)
    rb2[0, 0:64] = 1.0
    rb2[1, 64:128] = 1.0
    c["c_l2rb"] = rb2
    hs = np.zeros((128, 2), np.float32)
    hs[0:64, 0] = 1.0
    hs[64:128, 1] = 1.0
    c["c_l3hs"] = hs
    GRID, DECAY = 12, 0.3
    i = np.arange(T)
    xi = (i // GRID) % GRID
    yi = i % GRID
    dxa = np.abs(xi[:, None] - xi[None, :])
    dya = np.abs(yi[:, None] - yi[None, :])
    dx = np.minimum(dxa, GRID - dxa).astype(np.float32)
    dy = np.minimum(dya, GRID - dya).astype(np.float32)
    dist = np.sqrt(dx * dx + dy * dy)
    mask = np.exp(-DECAY * dist) * np.tril(np.ones((T, T), np.float32))
    c["c_l3expb"] = (mask + 1e-10).T.astype(BF)
    esel = np.zeros((8, 1024), BF)
    for e in range(8):
        esel[e, e * 128:(e + 1) * 128] = 1.0
    c["c_esel"] = esel
    sel16 = np.zeros((16, 2048), BF)
    for n in range(16):
        sel16[n, n * 128:(n + 1) * 128] = 1.0
    c["c_sel16"] = sel16
    u = np.arange(896)[None, :]
    pp = np.arange(128)[:, None]
    c["c_l0mask"] = (u >= pp + 384).astype(BF)
    return c


CONSTS = host_constants()



class Core:
    def __init__(self, p):
        self.p = p
        self.x = [p.sb(f"x{i}", [128, T], F32) for i in range(DC)]
        self.POOL1 = p.sbpool("POOL1", DC * T, BF16)
        self.POOL2 = p.sbpool("POOL2", DC * T, BF16)
        self.POOL3 = p.sbpool("POOL3", DC * T, BF16)
        self.pb1 = [p.view(self.POOL1, T * i, T) for i in range(DC)]
        self.pb2 = [p.view(self.POOL2, T * i, T) for i in range(DC)]
        self.pb3 = [p.view(self.POOL3, T * i, T) for i in range(DC)]
        self.s512 = [p.sb(f"s512_{i}", [128, 512], F32) for i in range(4)]
        self.colA = p.sb("colA", [128, TC], F32)
        self.colB = p.sb("colB", [128, TC], F32)
        self.colC = p.sb("colC", [128, TC], F32)
        self.sr4 = p.sb("sr4", [4, 128], F32)
        self.ones = p.sb("ones", [128, 128], F32)
        self.ones_bf = p.sb("ones_bf", [128, 256], BF16)
        self.ident_bf = p.sb("ident_bf", [128, 128], BF16)
        self.ident_f32 = p.sb("ident_f32", [128, 128], F32)
        self.tsel = p.sb("tsel", [4, 512], F32)
        self.zero16 = p.sb("zero16", [1, 16], F32)
        self.esel = p.sb("esel", [8, 1024], BF16)
        self.sel16 = p.sb("sel16", [16, 2048], BF16)
        self.WPOOL = p.sbpool("WPOOL", 8 * 512, BF16)
        self.wtb = [p.view(self.WPOOL, 512 * i, 512) for i in range(8)]
        self.wsb = [p.view(self.WPOOL, 512 * i, 256) for i in range(8)]
        self.cvt = p.sb("cvt", [128, 32], F32)
        self.zrow = p.sb("zrow", [1, 512], F32)
        self.cvals = {}
        self.lazy_consts = {}
        self.l0mask = p.sb("l0mask", [128, 896], BF16)
        self.PA = p.ps("PA", [128, 2048])
        self.PB = p.ps("PB", [128, 1536])
        self.PT = p.ps("PT", [128, 512], BF16)
        self.pm = [PsumView(self.PA, 512 * i, 512) for i in range(4)]
        self.pm.append(PsumView(self.PB, 0, 512))
        self.paux = PsumView(self.PB, 512, 512)
        self.prow = PsumView(self.PB, 1024, 512)
        self.ptb = PsumView(self.PT, 0, 512, elsz=2)
        self.ext = {}

    def cv(self, v, n=128):
        v = float(v)
        if v not in self.cvals:
            assert len(self.cvals) < 32
            self.cvals[v] = len(self.cvals)
        return self.cvt[0:n, self.cvals[v]:self.cvals[v] + 1]

    def ein(self, name, shape, dtype=F32):
        if name not in self.ext:
            self.ext[name] = self.p.dram(name, list(shape), dtype, kind="ExternalInput")
        return self.ext[name]

    def load_consts(self):
        p = self.p
        pairs = [("c_ones", self.ones), ("c_ones_bf", self.ones_bf),
                 ("c_ident_bf", self.ident_bf), ("c_ident_f32", self.ident_f32),
                 ("c_tsel", self.tsel), ("c_zero16", self.zero16),
                 ("c_esel", self.esel), ("c_sel16", self.sel16),
                 ("c_l0mask", self.l0mask)]
        for name, tile in pairs:
            arr = CONSTS[name]
            d = self.ein(name, list(arr.shape),
                         F32 if arr.dtype == np.float32 else BF16)
            p.dma("sync", tile[:, :], d.ap(), reads=[d], writes=[tile], tag="const")
        d = self.ein("c_cvals", [128, 32], F32)
        p.dma("sync", self.cvt[:, :], d.ap(), reads=[d], writes=[self.cvt], tag="cvals")
        dz = self.ein("c_zrow", [1, 512], F32)
        p.dma("sync", self.zrow[:, :], dz.ap(), reads=[dz], writes=[self.zrow], tag="zrow")

    def cload(self, name):
        if name in self.lazy_consts:
            return self.lazy_consts[name]
        arr = CONSTS[name]
        dt_ = F32 if arr.dtype == np.float32 else BF16
        d = self.ein(name, list(arr.shape), dt_)
        t = self.p.sb("sb" + name, list(arr.shape), dt_)
        self.p.dma("sync", t[:, :], d.ap(), reads=[d], writes=[t], tag="lconst")
        self.lazy_consts[name] = t
        return t


def vec_cols(p, co, in_name, n, name):
    d = co.ein(in_name, [n * 128], F32)
    t = p.sb(name, [128, n], F32)
    for c in range(n):
        p.dma("sync", t[:, c:c + 1],
              d.ap()[c * 128:(c + 1) * 128].rearrange("(p o) -> p o", o=1),
              reads=[d], writes=[t], tag="veccol")
    return t


def stage_bf16(p, co, in_name, shape, sname=None):
    src = co.ein(in_name, list(shape), F32)
    dst = p.dram((sname or ("s_" + in_name)), list(shape), BF16, kind="Internal")
    p.dma("gpsimd", dst.ap(), src.ap(), reads=[src], writes=[dst], tag="stage")
    return dst



def psum_zero(p, co, psum, ncols, tag="pz"):
    p.op("tensor", lambda t: t.matmul(psum[0:128, 0:ncols], co.ones[0:1, :],
                                      co.zero16[:, 0:ncols], start=True, stop=True),
         reads=[co.ones, co.zero16], writes=[psum], tag=tag)


def emit_stats(p, co, src_tiles, src_bf16):
    ones_x = co.ones_bf[:, 0:1] if src_bf16 else co.ones[:, 0:1]
    ones_x_buf = co.ones_bf if src_bf16 else co.ones
    psum_zero(p, co, co.pm[0], 16, "st_z1")
    psum_zero(p, co, co.pm[1], 16, "st_z2")
    for j in range(NT):
        for d in range(DC):
            xt = src_tiles[d]
            sq = co.s512[d % 2]
            p.op("scalar", lambda e, xt=xt, sq=sq, j=j: e.activation(
                sq[:, :], xt[:, j * 512:(j + 1) * 512], AF.Square),
                 reads=[xt], writes=[sq], tag="st_sq")
            for u in range(4):
                tc = 4 * j + u
                p.op("tensor", lambda t, xt=xt, tc=tc: t.matmul(
                    co.pm[0][0:128, tc:tc + 1], xt[:, tc * 128:(tc + 1) * 128],
                    ones_x, start=False, stop=False),
                     reads=[ones_x_buf, xt], writes=[co.pm[0]], tag="st_sum")
                p.op("tensor", lambda t, sq=sq, tc=tc, u=u: t.matmul(
                    co.pm[1][0:128, tc:tc + 1], sq[:, u * 128:(u + 1) * 128],
                    co.ones[:, 0:1], start=False, stop=False),
                     reads=[co.ones, sq], writes=[co.pm[1]], tag="st_sumsq")
    p.op("vector", lambda e: e.tensor_scalar_mul(co.colA[:, :], co.pm[0][0:128, 0:16],
                                                 1.0 / D),
         reads=[co.pm[0]], writes=[co.colA], tag="st_mu")
    p.op("vector", lambda e: e.tensor_scalar_mul(co.colB[:, :], co.pm[1][0:128, 0:16],
                                                 1.0 / D),
         reads=[co.pm[1]], writes=[co.colB], tag="st_msq")
    p.op("vector", lambda e: e.tensor_tensor(co.colC[:, :], co.colA[:, :],
                                             co.colA[:, :], ALU.mult),
         reads=[co.colA], writes=[co.colC], tag="st_mu2")
    p.op("vector", lambda e: e.tensor_sub(co.colC[:, :], co.colB[:, :], co.colC[:, :]),
         reads=[co.colB, co.colC], writes=[co.colC], tag="st_var")
    p.op("scalar", lambda e: e.activation(co.colC[:, :], co.colC[:, :], AF.Ln,
                                          bias=co.cv(1e-5)),
         reads=[co.colC, co.cvt], writes=[co.colC], tag="st_ln")
    p.op("scalar", lambda e: e.activation(co.colC[:, :], co.colC[:, :], AF.Exp,
                                          scale=co.cv(-0.5)),
         reads=[co.colC, co.cvt], writes=[co.colC], tag="st_r")
    p.op("vector", lambda e: e.tensor_tensor(co.colA[:, :], co.colA[:, :],
                                             co.colC[:, :], ALU.mult),
         reads=[co.colA, co.colC], writes=[co.colA], tag="st_mr")


def col_broadcast(p, co, col_ap, col_buf, out_tile, w=512, tag="cb"):
    nchunks = w // 128
    p.op("tensor", lambda t: t.transpose(
        co.paux[0:nchunks, 0:128], col_ap, co.ident_f32[:, :]),
         reads=[col_buf, co.ident_f32], writes=[co.paux], tag=tag + "_tr")
    p.op("vector", lambda e: e.tensor_copy(co.sr4[0:nchunks, :],
                                           co.paux[0:nchunks, 0:128]),
         reads=[co.paux], writes=[co.sr4], tag=tag + "_ev")
    for c in range(nchunks):
        p.op("tensor", lambda t, c=c: t.matmul(
            co.paux[0:128, c * 128:(c + 1) * 128],
            co.tsel[0:nchunks, c * 128:(c + 1) * 128],
            co.sr4[0:nchunks, 0:128], start=True, stop=True),
             reads=[co.tsel, co.sr4], writes=[co.paux], tag=tag + "_mm")
    p.op("vector", lambda e: e.tensor_copy(out_tile[:, 0:w], co.paux[0:128, 0:w]),
         reads=[co.paux], writes=[out_tile], tag=tag + "_cp")


def emit_normalize(p, co, src_tiles, out_tiles, affine=None):
    for j in range(NT):
        sl = slice(j * 512, (j + 1) * 512)
        rb, mb, tmp = co.s512[3], co.s512[0], co.s512[2]
        col_broadcast(p, co, co.colC[:, 4 * j:4 * j + 4], co.colC, rb, tag="nm_r")
        col_broadcast(p, co, co.colA[:, 4 * j:4 * j + 4], co.colA, mb, tag="nm_m")
        for d in range(DC):
            xt, ot = src_tiles[d], out_tiles[d]
            p.op("vector", lambda e, xt=xt, sl=sl, tmp=tmp, rb=rb: e.tensor_tensor(
                tmp[:, :], xt[:, sl], rb[:, :], ALU.mult),
                 reads=[xt, rb], writes=[tmp], tag="nm_n1")
            if affine is None:
                p.op("vector", lambda e, tmp=tmp, ot=ot, sl=sl, mb=mb: e.tensor_sub(
                    ot[:, sl], tmp[:, :], mb[:, :]),
                     reads=[tmp, mb], writes=[ot], tag="nm_n2")
            else:
                g, b = affine
                p.op("vector", lambda e, tmp=tmp, mb=mb: e.tensor_sub(
                    tmp[:, :], tmp[:, :], mb[:, :]),
                     reads=[tmp, mb], writes=[tmp], tag="nm_n2")
                p.op("vector", lambda e, tmp=tmp, ot=ot, sl=sl, d=d: e.tensor_scalar(
                    ot[:, sl], tmp[:, :], g[:, d:d + 1], b[:, d:d + 1],
                    ALU.mult, ALU.add),
                     reads=[tmp, g, b], writes=[ot], tag="nm_aff")



def emit_l0(p, co):
    s_qkv = stage_bf16(p, co, "L0_attn_qkv_w", [D, 3 * D])
    s_gate = stage_bf16(p, co, "L0_attn_gate_w", [D, D])
    s_router = stage_bf16(p, co, "L0_ffn_w_router", [D, 8])
    s_w1 = stage_bf16(p, co, "L0_ffn_w1", [8, D, 4096])
    s_w2 = stage_bf16(p, co, "L0_ffn_w2", [8, 2048, D])

    emit_stats(p, co, co.x, False)
    emit_normalize(p, co, co.x, co.pb1)
    xn = co.pb1

    emit_l0_ffn(p, co, xn, s_router, s_w1, s_w2)

    ln_g = vec_cols(p, co, "L0_attn_ln_g", DC, "l0_lng")
    ln_b = vec_cols(p, co, "L0_attn_ln_b", DC, "l0_lnb")
    qkv_b = vec_cols(p, co, "L0_attn_qkv_b", 3 * DC, "l0_qkvb")
    gate_b = vec_cols(p, co, "L0_attn_gate_b", DC, "l0_gateb")

    emit_stats(p, co, xn, True)
    emit_normalize(p, co, xn, co.pb2, affine=(ln_g, ln_b))
    xn2 = co.pb2

    kT = co.pb1
    for c in range(DC):
        for d in range(DC):
            p.dma("sync", co.wsb[d][:, 0:128],
                  s_gate.ap()[d * 128:(d + 1) * 128, c * 128:(c + 1) * 128],
                  reads=[s_gate], writes=[co.wsb[d]], tag="l0_wgl")
            p.dma("sync", co.wsb[d][:, 128:256],
                  s_qkv.ap()[d * 128:(d + 1) * 128, D + c * 128:D + (c + 1) * 128],
                  reads=[s_qkv], writes=[co.wsb[d]], tag="l0_wkl")
        for j in range(NT):
            sl = slice(j * 512, (j + 1) * 512)
            for d in range(DC):
                p.op("tensor", lambda t, d=d, sl=sl: t.matmul(
                    co.pm[0][0:128, 0:512], co.wsb[d][:, 0:128], xn2[d][:, sl],
                    start=(d == 0), stop=(d == DC - 1)),
                     reads=[co.wsb[d], xn2[d]], writes=[co.pm[0]], tag="l0_mmg")
            for d in range(DC):
                p.op("tensor", lambda t, d=d, sl=sl: t.matmul(
                    co.pm[1][0:128, 0:512], co.wsb[d][:, 128:256], xn2[d][:, sl],
                    start=(d == 0), stop=(d == DC - 1)),
                     reads=[co.wsb[d], xn2[d]], writes=[co.pm[1]], tag="l0_mmk")
            sg, k1, km = co.s512[0], co.s512[1], co.s512[2]
            p.op("scalar", lambda e, c=c, sg=sg: e.activation(
                sg[:, :], co.pm[0][0:128, 0:512], AF.Sigmoid, bias=gate_b[:, c:c + 1]),
                 reads=[co.pm[0], gate_b], writes=[sg], tag="l0_sig")
            p.op("vector", lambda e, c=c, k1=k1, sg=sg: e.scalar_tensor_tensor(
                k1[:, :], co.pm[1][0:128, 0:512], qkv_b[:, DC + c:DC + c + 1], sg[:, :],
                ALU.add, ALU.mult),
                 reads=[co.pm[1], qkv_b, sg], writes=[k1], tag="l0_k1")
            p.op("vector", lambda e, k1=k1, km=km: e.tensor_scalar_min(km[:, :], k1[:, :], 0.0),
                 reads=[k1], writes=[km], tag="l0_kmin")
            p.op("scalar", lambda e, km=km: e.activation(km[:, :], km[:, :], AF.Exp),
                 reads=[km], writes=[km], tag="l0_kexp")
            p.op("vector", lambda e, k1=k1: e.tensor_scalar_max(k1[:, :], k1[:, :], 0.0),
                 reads=[k1], writes=[k1], tag="l0_krelu")
            p.op("vector", lambda e, c=c, sl=sl, k1=k1, km=km: e.tensor_add(
                kT[c][:, sl], km[:, :], k1[:, :]),
                 reads=[km, k1], writes=[kT[c]], tag="l0_kout")

    vtok = co.pb3
    bv = p.sb("l0_bv", [1, 1024], BF16)
    p.dma("gpsimd", bv[:, :],
          co.ext["L0_attn_qkv_b"].ap()[2 * D:3 * D].rearrange("(o n) -> o n", o=1),
          reads=[co.ext["L0_attn_qkv_b"]], writes=[bv], tag="l0_bvf")
    for g in range(4):
        gsl = slice(2 * D + g * 256, 2 * D + (g + 1) * 256)
        for d in range(DC):
            p.dma("sync", co.wsb[d][:, :],
                  s_qkv.ap()[d * 128:(d + 1) * 128, gsl],
                  reads=[s_qkv], writes=[co.wsb[d]], tag="l0_wvl")
        for s in range(TC):
            ssl = slice(s * 128, (s + 1) * 128)
            for d in range(DC):
                p.op("tensor", lambda t, d=d, ssl=ssl: t.matmul(
                    co.pm[2][0:128, 0:256], xn2[d][:, ssl], co.wsb[d][:, :],
                    start=(d == 0), stop=False),
                     reads=[xn2[d], co.wsb[d]], writes=[co.pm[2]], tag="l0_mmv")
            p.op("tensor", lambda t, g=g: t.matmul(
                co.pm[2][0:128, 0:256], co.ones_bf[0:1, 0:128],
                bv[:, g * 256:(g + 1) * 256], start=False, stop=True),
                 reads=[co.ones_bf, bv], writes=[co.pm[2]], tag="l0_mmvb")
            dst = vtok[s // 2]
            off = (s % 2) * 1024 + g * 256
            p.op("vector", lambda e, dst=dst, off=off: e.tensor_copy(
                dst[:, off:off + 256], co.pm[2][0:128, 0:256]),
                 reads=[co.pm[2]], writes=[dst], tag="l0_vev")

    W = 256
    NJ = T // W
    qsb = [p.view(co.WPOOL, 512 * d + 256, 256) for d in range(DC)]
    esb = [p.sb(f"l0_e{i}", [128, W], BF16) for i in range(TC)]
    kcs = p.sb("l0_kcs", [128, DC], F32)
    p.op("vector", lambda e: e.memset(kcs[:, :], 0.0), writes=[kcs], tag="l0_kcs0")
    rden_b = co.s512[3]
    for j in range(NJ):
        sl = slice(j * W, (j + 1) * W)
        for c in range(DC):
            for d in range(DC):
                p.dma("sync", co.wsb[d][:, 0:128],
                      s_qkv.ap()[d * 128:(d + 1) * 128, c * 128:(c + 1) * 128],
                      reads=[s_qkv], writes=[co.wsb[d]], tag="l0_wql")
            for d in range(DC):
                p.op("tensor", lambda t, d=d, sl=sl: t.matmul(
                    co.pm[3][0:128, 0:W], co.wsb[d][:, 0:128],
                    xn2[d][:, sl], start=(d == 0), stop=(d == DC - 1)),
                     reads=[co.wsb[d], xn2[d]], writes=[co.pm[3]], tag="l0_mmq")
            q1, qm = co.s512[0], co.s512[1]
            p.op("vector", lambda e, c=c, q1=q1: e.tensor_scalar_add(
                q1[:, 0:W], co.pm[3][0:128, 0:W], qkv_b[:, c:c + 1]),
                 reads=[co.pm[3], qkv_b], writes=[q1], tag="l0_qb")
            p.op("vector", lambda e, q1=q1, qm=qm: e.tensor_scalar_min(
                qm[:, 0:W], q1[:, 0:W], 0.0), reads=[q1], writes=[qm], tag="l0_qmin")
            p.op("scalar", lambda e, qm=qm: e.activation(qm[:, 0:W], qm[:, 0:W], AF.Exp),
                 reads=[qm], writes=[qm], tag="l0_qexp")
            p.op("vector", lambda e, q1=q1: e.tensor_scalar_max(q1[:, 0:W], q1[:, 0:W], 0.0),
                 reads=[q1], writes=[q1], tag="l0_qrelu")
            p.op("vector", lambda e, c=c, q1=q1, qm=qm: e.tensor_add(
                qsb[c][:, :], qm[:, 0:W], q1[:, 0:W]),
                 reads=[qm, q1], writes=[qsb[c]], tag="l0_qout")
        psum_zero(p, co, co.pm[4], 2, "l0_dz")
        for d in range(DC):
            kc = co.s512[2]
            p.op("vector", lambda e, d=d, sl=sl, kc=kc: e.tensor_tensor_scan(
                kc[:, 0:W], co.ones_bf[:, 0:W], kT[d][:, sl], kcs[:, d:d + 1],
                ALU.mult, ALU.add),
                 reads=[co.ones_bf, kT[d], kcs], writes=[kc], tag="l0_kscan")
            p.op("vector", lambda e, d=d, kc=kc: e.tensor_copy(
                kcs[:, d:d + 1], kc[:, W - 1:W]),
                 reads=[kc], writes=[kcs], tag="l0_kst")
            p.op("vector", lambda e, d=d, kc=kc: e.tensor_tensor(
                kc[:, 0:W], kc[:, 0:W], qsb[d][:, :], ALU.mult),
                 reads=[kc, qsb[d]], writes=[kc], tag="l0_kq")
            for h in range(2):
                p.op("tensor", lambda t, kc=kc, h=h: t.matmul(
                    co.pm[4][0:128, h:h + 1], kc[:, h * 128:(h + 1) * 128],
                    co.ones[:, 0:1], start=False, stop=False),
                     reads=[co.ones, kc], writes=[co.pm[4]], tag="l0_mmden")
        dcol = co.colB
        p.op("scalar", lambda e: e.activation(
            dcol[:, 0:2], co.pm[4][0:128, 0:2], AF.Ln, bias=co.cv(1e-6)),
             reads=[co.pm[4], co.cvt], writes=[dcol], tag="l0_dln")
        p.op("scalar", lambda e: e.activation(
            dcol[:, 0:2], dcol[:, 0:2], AF.Exp, scale=co.cv(-1.0)),
             reads=[dcol, co.cvt], writes=[dcol], tag="l0_dexp")
        col_broadcast(p, co, dcol[:, 0:2], dcol, rden_b, w=W, tag="l0_db")
        ns = 2 * j + 2
        for i in range(ns):
            isl = slice(i * 128, (i + 1) * 128)
            for d in range(DC):
                p.op("tensor", lambda t, d=d, isl=isl: t.matmul(
                    co.pm[4][0:128, 0:W], kT[d][:, isl], qsb[d][:, :],
                    start=(d == 0), stop=(d == DC - 1)),
                     reads=[kT[d], qsb[d]], writes=[co.pm[4]], tag="l0_mma")
            if i >= 2 * j:
                ko = 384 - 128 * (i - 2 * j)
                p.op("vector", lambda e, i=i, ko=ko: e.tensor_tensor(
                    esb[i][:, :], co.pm[4][0:128, 0:W],
                    co.l0mask[:, ko:ko + W], ALU.mult),
                     reads=[co.pm[4], co.l0mask], writes=[esb[i]], tag="l0_emask")
            else:
                p.op("vector", lambda e, i=i: e.tensor_copy(
                    esb[i][:, :], co.pm[4][0:128, 0:W]),
                     reads=[co.pm[4]], writes=[esb[i]], tag="l0_ecp")
        for c in range(DC):
            for i in range(ns):
                vt = vtok[i // 2]
                voff = (i % 2) * 1024 + c * 128
                p.op("tensor", lambda t, vt=vt, voff=voff, i=i, ns=ns: t.matmul(
                    co.pm[0][0:128, 0:W], vt[:, voff:voff + 128], esb[i][:, :],
                    start=(i == 0), stop=(i == ns - 1)),
                     reads=[vt, esb[i]], writes=[co.pm[0]], tag="l0_mmo")
            tmp = co.s512[2]
            p.op("vector", lambda e, tmp=tmp: e.tensor_tensor(
                tmp[:, 0:W], co.pm[0][0:128, 0:W], rden_b[:, 0:W], ALU.mult),
                 reads=[co.pm[0], rden_b], writes=[tmp], tag="l0_onorm")
            xt = co.x[c]
            p.op("vector", lambda e, xt=xt, sl=sl, tmp=tmp: e.tensor_add(
                xt[:, sl], xt[:, sl], tmp[:, 0:W]),
                 reads=[xt, tmp], writes=[xt], tag="l0_ores")


def emit_l0_ffn(p, co, xn, s_router, s_w1, s_w2):
    wr = p.sb("l0_wr", [128, 64], BF16)
    for d in range(DC):
        p.dma("sync", wr[:, d * 8:(d + 1) * 8],
              s_router.ap()[d * 128:(d + 1) * 128, 0:8],
              reads=[s_router], writes=[wr], tag="l0_wrl")
    GT = p.sb("l0_GT", [8, T], BF16)
    rtp = co.s512[0]
    for tc in range(TC):
        tsl = slice(tc * 128, (tc + 1) * 128)
        for d in range(DC):
            p.op("tensor", lambda t, d=d, tsl=tsl: t.matmul(
                co.pm[0][0:128, 0:8], xn[d][:, tsl], wr[:, d * 8:(d + 1) * 8],
                start=(d == 0), stop=(d == DC - 1)),
                 reads=[xn[d], wr], writes=[co.pm[0]], tag="l0_mmr")
        p.op("vector", lambda e: e.tensor_copy(rtp[:, 0:8], co.pm[0][0:128, 0:8]),
             reads=[co.pm[0]], writes=[rtp], tag="l0_rt")
        p.op("vector", lambda e: e.reduce_max(rtp[:, 16:17], rtp[:, 0:8], AX.X),
             reads=[rtp], writes=[rtp], tag="l0_m1")
        p.op("vector", lambda e: e.tensor_scalar(
            rtp[:, 24:32], rtp[:, 0:8], rtp[:, 16:17], None, ALU.is_equal),
             reads=[rtp], writes=[rtp], tag="l0_eq1")
        p.op("vector", lambda e: e.scalar_tensor_tensor(
            rtp[:, 8:16], rtp[:, 24:32], -1e30, rtp[:, 0:8], ALU.mult, ALU.add),
             reads=[rtp], writes=[rtp], tag="l0_mask1")
        p.op("vector", lambda e: e.reduce_max(rtp[:, 17:18], rtp[:, 8:16], AX.X),
             reads=[rtp], writes=[rtp], tag="l0_m2")
        p.op("vector", lambda e: e.tensor_scalar(
            rtp[:, 32:40], rtp[:, 8:16], rtp[:, 17:18], None, ALU.is_equal),
             reads=[rtp], writes=[rtp], tag="l0_eq2")
        p.op("vector", lambda e: e.tensor_sub(rtp[:, 18:19], rtp[:, 16:17], rtp[:, 17:18]),
             reads=[rtp], writes=[rtp], tag="l0_dm")
        p.op("scalar", lambda e: e.activation(rtp[:, 18:19], rtp[:, 18:19], AF.Sigmoid),
             reads=[rtp], writes=[rtp], tag="l0_g1")
        p.op("vector", lambda e: e.tensor_scalar(
            rtp[:, 19:20], rtp[:, 18:19], -1.0, 1.0, ALU.mult, ALU.add),
             reads=[rtp], writes=[rtp], tag="l0_g2")
        p.op("vector", lambda e: e.tensor_scalar(
            rtp[:, 24:32], rtp[:, 24:32], rtp[:, 18:19], None, ALU.mult),
             reads=[rtp], writes=[rtp], tag="l0_p1")
        p.op("vector", lambda e: e.scalar_tensor_tensor(
            rtp[:, 24:32], rtp[:, 32:40], rtp[:, 19:20], rtp[:, 24:32],
            ALU.mult, ALU.add),
             reads=[rtp], writes=[rtp], tag="l0_p2")
        gtile = p.sb(f"l0_g{tc}", [128, 8], BF16)
        p.op("vector", lambda e, gtile=gtile: e.tensor_copy(gtile[:, :], rtp[:, 24:32]),
             reads=[rtp], writes=[gtile], tag="l0_gcast")
        p.op("tensor", lambda t, gtile=gtile: t.transpose(
            co.ptb[0:8, 0:128], gtile[:, :], co.ident_bf[:, :]),
             reads=[gtile, co.ident_bf], writes=[co.ptb], tag="l0_gtr")
        p.op("vector", lambda e, tsl=tsl: e.tensor_copy(GT[0:8, tsl], co.ptb[0:8, 0:128]),
             reads=[co.ptb], writes=[GT], tag="l0_gtc")

    actbuf = co.pb3
    w2sb = co.pb2
    gbt = co.s512[3]
    for e_ in range(8):
        for i in range(16):
            p.dma("sync", w2sb[i // 2][:, (i % 2) * 1024:(i % 2) * 1024 + 1024],
                  s_w2.ap()[e_, i * 128:(i + 1) * 128, 0:D],
                  reads=[s_w2], writes=[w2sb[i // 2]], tag="l0_w2l")
        for th in range(2):
            for hc in range(16):
                for d in range(DC):
                    p.dma("sync", co.wsb[d][:, 0:128],
                          s_w1.ap()[e_, d * 128:(d + 1) * 128, hc * 128:(hc + 1) * 128],
                          reads=[s_w1], writes=[co.wsb[d]], tag="l0_w1la")
                    p.dma("sync", co.wsb[d][:, 128:256],
                          s_w1.ap()[e_, d * 128:(d + 1) * 128,
                                    2048 + hc * 128:2048 + (hc + 1) * 128],
                          reads=[s_w1], writes=[co.wsb[d]], tag="l0_w1lg")
                for tj in range(2):
                    j = th * 2 + tj
                    sl = slice(j * 512, (j + 1) * 512)
                    for d in range(DC):
                        p.op("tensor", lambda t, d=d, sl=sl: t.matmul(
                            co.pm[0][0:128, 0:512], co.wsb[d][:, 0:128], xn[d][:, sl],
                            start=(d == 0), stop=(d == DC - 1)),
                             reads=[co.wsb[d], xn[d]], writes=[co.pm[0]], tag="l0_mme1")
                    for d in range(DC):
                        p.op("tensor", lambda t, d=d, sl=sl: t.matmul(
                            co.pm[1][0:128, 0:512], co.wsb[d][:, 128:256], xn[d][:, sl],
                            start=(d == 0), stop=(d == DC - 1)),
                             reads=[co.wsb[d], xn[d]], writes=[co.pm[1]], tag="l0_mme2")
                    sg = co.s512[1]
                    p.op("scalar", lambda e, sg=sg: e.activation(
                        sg[:, :], co.pm[1][0:128, 0:512], AF.Silu),
                         reads=[co.pm[1]], writes=[sg], tag="l0_esilu")
                    dst = actbuf[hc // 2]
                    off = (hc % 2) * 1024 + tj * 512
                    p.op("vector", lambda e, dst=dst, off=off, sg=sg: e.tensor_tensor(
                        dst[:, off:off + 512], co.pm[0][0:128, 0:512], sg[:, :], ALU.mult),
                         reads=[co.pm[0], sg], writes=[dst], tag="l0_eact")
            for tj in range(2):
                j = th * 2 + tj
                sl = slice(j * 512, (j + 1) * 512)
                p.op("tensor", lambda t, e_=e_, sl=sl: t.matmul(
                    co.paux[0:128, 0:512], co.esel[:, e_ * 128:(e_ + 1) * 128],
                    GT[0:8, sl], start=True, stop=True),
                     reads=[co.esel, GT], writes=[co.paux], tag="l0_gbc")
                p.op("vector", lambda e: e.tensor_copy(gbt[:, :], co.paux[0:128, 0:512]),
                     reads=[co.paux], writes=[gbt], tag="l0_gbcc")
                for c in range(DC):
                    for hc in range(16):
                        src = w2sb[hc // 2]
                        woff = (hc % 2) * 1024 + c * 128
                        asrc = actbuf[hc // 2]
                        aoff = (hc % 2) * 1024 + tj * 512
                        p.op("tensor", lambda t, src=src, woff=woff, asrc=asrc,
                             aoff=aoff, hc=hc: t.matmul(
                                 co.pm[2][0:128, 0:512], src[:, woff:woff + 128],
                                 asrc[:, aoff:aoff + 512],
                                 start=(hc == 0), stop=(hc == 15)),
                             reads=[src, asrc], writes=[co.pm[2]], tag="l0_mmo2")
                    tmp = co.s512[2]
                    p.op("vector", lambda e, tmp=tmp: e.tensor_tensor(
                        tmp[:, :], co.pm[2][0:128, 0:512], gbt[:, :], ALU.mult),
                         reads=[co.pm[2], gbt], writes=[tmp], tag="l0_egate")
                    xt = co.x[c]
                    p.op("vector", lambda e, xt=xt, sl=sl, tmp=tmp: e.tensor_add(
                        xt[:, sl], xt[:, sl], tmp[:, :]),
                         reads=[xt, tmp], writes=[xt], tag="l0_eres")



C8 = float(1.5 * 2 ** 23 / 8)
C4 = float(1.5 * 2 ** 23 / 4)
C2 = float(1.5 * 2 ** 23 / 2)


def emit_quant(p, co, src_h, s_off, dst_h, d_off, q, sc, tag):
    q1, q2, q3, q4, q5 = q
    sv = src_h[:, s_off:s_off + 512].rearrange("p (g k) -> p g k", k=32)
    p.op("vector", lambda e: e.tensor_reduce(
        sc[:, 0:16], sv, AX.X, ALU.max, apply_absolute_value=True),
         reads=[src_h], writes=[sc], tag=tag + "_amax")
    p.op("vector", lambda e: e.tensor_scalar(
        sc[:, 0:16], sc[:, 0:16], 1.0 / 7.5, 1e-30, ALU.mult, ALU.max),
         reads=[sc], writes=[sc], tag=tag + "_sc")
    p.op("scalar", lambda e: e.activation(sc[:, 16:32], sc[:, 0:16], AF.Ln),
         reads=[sc], writes=[sc], tag=tag + "_ln")
    p.op("scalar", lambda e: e.activation(sc[:, 16:32], sc[:, 16:32], AF.Exp,
                                          scale=co.cv(-1.0)),
         reads=[sc, co.cvt], writes=[sc], tag=tag + "_inv")
    inv_b = sc[:, 16:32].rearrange("p (g o) -> p g o", o=1).broadcast_to([128, 16, 32])
    scl_b = sc[:, 0:16].rearrange("p (g o) -> p g o", o=1).broadcast_to([128, 16, 32])
    qv = lambda t: t[:, 0:512].rearrange("p (g k) -> p g k", k=32)
    p.op("vector", lambda e: e.tensor_tensor(qv(q1), sv, inv_b, ALU.mult),
         reads=[src_h, sc], writes=[q1], tag=tag + "_y")
    p.op("scalar", lambda e: e.activation(q2[:, :], q1[:, :], AF.Abs),
         reads=[q1], writes=[q2], tag=tag + "_u")
    p.op("vector", lambda e: e.tensor_scalar(q3[:, :], q1[:, :], C8, C8,
                                             ALU.add, ALU.subtract),
         reads=[q1], writes=[q3], tag=tag + "_r1")
    p.op("vector", lambda e: e.tensor_scalar(q4[:, :], q1[:, :], C4, C4,
                                             ALU.add, ALU.subtract),
         reads=[q1], writes=[q4], tag=tag + "_r2")
    p.op("vector", lambda e: e.tensor_sub(q4[:, :], q4[:, :], q3[:, :]),
         reads=[q4, q3], writes=[q4], tag=tag + "_d21")
    p.op("vector", lambda e: e.tensor_single_scalar(q5[:, :], q2[:, :], 2.0, ALU.is_ge),
         reads=[q2], writes=[q5], tag=tag + "_m2")
    p.op("vector", lambda e: e.tensor_mul(q4[:, :], q4[:, :], q5[:, :]),
         reads=[q4, q5], writes=[q4], tag=tag + "_md")
    p.op("vector", lambda e: e.tensor_add(q3[:, :], q3[:, :], q4[:, :]),
         reads=[q3, q4], writes=[q3], tag=tag + "_a1")
    p.op("vector", lambda e: e.tensor_scalar(q4[:, :], q1[:, :], C2, C2,
                                             ALU.add, ALU.subtract),
         reads=[q1], writes=[q4], tag=tag + "_r4")
    p.op("vector", lambda e: e.tensor_scalar(q1[:, :], q1[:, :], C4, C4,
                                             ALU.add, ALU.subtract),
         reads=[q1], writes=[q1], tag=tag + "_r2b")
    p.op("vector", lambda e: e.tensor_sub(q4[:, :], q4[:, :], q1[:, :]),
         reads=[q4, q1], writes=[q4], tag=tag + "_d42")
    p.op("vector", lambda e: e.tensor_single_scalar(q5[:, :], q2[:, :], 4.0, ALU.is_ge),
         reads=[q2], writes=[q5], tag=tag + "_m4")
    p.op("vector", lambda e: e.tensor_mul(q4[:, :], q4[:, :], q5[:, :]),
         reads=[q4, q5], writes=[q4], tag=tag + "_md4")
    p.op("vector", lambda e: e.tensor_add(q3[:, :], q3[:, :], q4[:, :]),
         reads=[q3, q4], writes=[q3], tag=tag + "_a2")
    dv = dst_h[:, d_off:d_off + 512].rearrange("p (g k) -> p g k", k=32)
    p.op("vector", lambda e: e.tensor_tensor(dv, qv(q3), scl_b, ALU.mult),
         reads=[q3, sc], writes=[dst_h], tag=tag + "_dq")


def emit_l1(p, co):
    s_win = stage_bf16(p, co, "L1_attn_W_in", [D, 2 * D])
    s_xprj = stage_bf16(p, co, "L1_attn_W_xproj", [D, 33])
    s_wout = stage_bf16(p, co, "L1_attn_W_out", [D, D])

    emit_stats(p, co, co.x, False)
    emit_normalize(p, co, co.x, co.pb1)
    xn = co.pb1

    emit_l1_ffn(p, co)

    convb = vec_cols(p, co, "L1_attn_conv_b", DC, "l1_convb")
    Dcol = vec_cols(p, co, "L1_attn_D_param", DC, "l1_D")
    cw_ext = co.ein("L1_attn_conv_w", [D, 1, 4], F32)
    l1_cw = p.sb("l1_cw", [128, 32], F32)
    for d in range(DC):
        p.dma("sync", l1_cw[:, 4 * d:4 * d + 4],
              cw_ext.ap()[d * 128:(d + 1) * 128, 0, :],
              reads=[cw_ext], writes=[l1_cw], tag="l1_cwl")
    la_ext = co.ein("L1_attn_log_A", [D, 16], F32)
    l1_A = p.sb("l1_A", [128, 128], F32)
    for d in range(DC):
        p.dma("sync", l1_A[:, 16 * d:16 * d + 16],
              la_ext.ap()[d * 128:(d + 1) * 128, :],
              reads=[la_ext], writes=[l1_A], tag="l1_lal")
    p.op("scalar", lambda e: e.activation(l1_A[:, :], l1_A[:, :], AF.Exp),
         reads=[l1_A], writes=[l1_A], tag="l1_Aexp")
    p.op("vector", lambda e: e.tensor_scalar_mul(l1_A[:, :], l1_A[:, :], -1.0),
         reads=[l1_A], writes=[l1_A], tag="l1_Aneg")

    xconv = co.pb2
    L1P = p.sbpool("L1P", 4096, BF16)
    xm_pad = p.view(L1P, 0, T + 3)
    p.op("vector", lambda e: e.memset(xm_pad[:, 0:3], 0.0),
         writes=[xm_pad], tag="l1_pz")
    for c in range(DC):
        for d in range(DC):
            p.dma("sync", co.wsb[d][:, 0:128],
                  s_win.ap()[d * 128:(d + 1) * 128, c * 128:(c + 1) * 128],
                  reads=[s_win], writes=[co.wsb[d]], tag="l1_wxl")
        for j in range(NT):
            sl = slice(j * 512, (j + 1) * 512)
            for d in range(DC):
                p.op("tensor", lambda t, d=d, sl=sl: t.matmul(
                    co.pm[0][0:128, 0:512], co.wsb[d][:, 0:128], xn[d][:, sl],
                    start=(d == 0), stop=(d == DC - 1)),
                     reads=[co.wsb[d], xn[d]], writes=[co.pm[0]], tag="l1_mmx")
            p.op("vector", lambda e, j=j: e.tensor_copy(
                xm_pad[:, 3 + j * 512:3 + (j + 1) * 512], co.pm[0][0:128, 0:512]),
                 reads=[co.pm[0]], writes=[xm_pad], tag="l1_xev")
        for j in range(NT):
            acc = co.s512[0]
            p.op("vector", lambda e, c=c, j=j, acc=acc: e.tensor_scalar(
                acc[:, :], xm_pad[:, j * 512:j * 512 + 512],
                l1_cw[:, 4 * c:4 * c + 1], None, ALU.mult),
                 reads=[xm_pad, l1_cw], writes=[acc], tag="l1_cv0")
            for jj in range(1, 4):
                p.op("vector", lambda e, c=c, j=j, jj=jj, acc=acc: e.scalar_tensor_tensor(
                    acc[:, :], xm_pad[:, j * 512 + jj:j * 512 + jj + 512],
                    l1_cw[:, 4 * c + jj:4 * c + jj + 1], acc[:, :], ALU.mult, ALU.add),
                     reads=[xm_pad, l1_cw, acc], writes=[acc], tag="l1_cv")
            p.op("scalar", lambda e, c=c, j=j, acc=acc: e.activation(
                xconv[c][:, j * 512:(j + 1) * 512], acc[:, :], AF.Silu,
                bias=convb[:, c:c + 1]),
                 reads=[acc, convb], writes=[xconv[c]], tag="l1_silu")

    bcdB = p.view(L1P, 0, T, nrows=16)
    bcdC = p.view(L1P, T, T, nrows=16)
    for d in range(DC):
        p.dma("sync", co.wsb[d][:, 0:33],
              s_xprj.ap()[d * 128:(d + 1) * 128, 0:33],
              reads=[s_xprj], writes=[co.wsb[d]], tag="l1_wpl")
    for j in range(NT):
        sl = slice(j * 512, (j + 1) * 512)
        for d in range(DC):
            p.op("tensor", lambda t, d=d, sl=sl: t.matmul(
                co.pm[0][0:16, 0:512], co.wsb[d][:, 0:16], xconv[d][:, sl],
                start=(d == 0), stop=(d == DC - 1)),
                 reads=[co.wsb[d], xconv[d]], writes=[co.pm[0]], tag="l1_mmB")
        for d in range(DC):
            p.op("tensor", lambda t, d=d, sl=sl: t.matmul(
                co.pm[1][0:16, 0:512], co.wsb[d][:, 16:32], xconv[d][:, sl],
                start=(d == 0), stop=(d == DC - 1)),
                 reads=[co.wsb[d], xconv[d]], writes=[co.pm[1]], tag="l1_mmC")
        p.op("vector", lambda e, sl=sl: e.tensor_copy(bcdB[:, sl], co.pm[0][0:16, 0:512]),
             reads=[co.pm[0]], writes=[bcdB], tag="l1_evB")
        p.op("vector", lambda e, sl=sl: e.tensor_copy(bcdC[:, sl], co.pm[1][0:16, 0:512]),
             reads=[co.pm[1]], writes=[bcdC], tag="l1_evC")
    psum_zero(p, co, co.pm[2], 16, "l1_dz")
    for tc in range(TC):
        for d in range(DC):
            p.op("tensor", lambda t, tc=tc, d=d: t.matmul(
                co.pm[2][0:128, tc:tc + 1], xconv[d][:, tc * 128:(tc + 1) * 128],
                co.wsb[d][:, 32:33], start=False, stop=False),
                 reads=[xconv[d], co.wsb[d]], writes=[co.pm[2]], tag="l1_mmd")
    p.op("vector", lambda e: e.tensor_copy(co.colB[:, :], co.pm[2][0:128, 0:16]),
         reads=[co.pm[2]], writes=[co.colB], tag="l1_dcp")
    p.op("scalar", lambda e: e.activation(co.colA[:, :], co.colB[:, :], AF.Abs),
         reads=[co.colB], writes=[co.colA], tag="l1_dabs")
    p.op("scalar", lambda e: e.activation(co.colA[:, :], co.colA[:, :], AF.Exp,
                                          scale=co.cv(-1.0)),
         reads=[co.colA, co.cvt], writes=[co.colA], tag="l1_dexp2")
    p.op("scalar", lambda e: e.activation(co.colA[:, :], co.colA[:, :], AF.Ln,
                                          bias=co.cv(1.0)),
         reads=[co.colA, co.cvt], writes=[co.colA], tag="l1_dln2")
    p.op("vector", lambda e: e.tensor_scalar_max(co.colC[:, :], co.colB[:, :], 0.0),
         reads=[co.colB], writes=[co.colC], tag="l1_drelu")
    p.op("vector", lambda e: e.tensor_add(co.colB[:, :], co.colA[:, :], co.colC[:, :]),
         reads=[co.colA, co.colC], writes=[co.colB], tag="l1_dsp")

    du = co.pb3
    states = p.sb("l1_states", [128, 128], F32)
    p.op("vector", lambda e: e.memset(states[:, :], 0.0), writes=[states], tag="l1_st0")
    l1_y = co.l1_qx
    for q in range(NT):
        qsl = slice(q * 512, (q + 1) * 512)
        delta_b = co.s512[3]
        col_broadcast(p, co, co.colB[:, 4 * q:4 * q + 4], co.colB, delta_b, tag="l1_db")
        for d in range(DC):
            p.op("vector", lambda e, d=d, qsl=qsl: e.tensor_tensor(
                du[d][:, qsl], xconv[d][:, qsl], delta_b[:, :], ALU.mult),
                 reads=[xconv[d], delta_b], writes=[du[d]], tag="l1_du")
        for d in range(DC):
            for n in range(16):
                p.op("tensor", lambda t, n=n, qsl=qsl: t.matmul(
                    co.pm[0][0:128, 0:512], co.sel16[:, n * 128:(n + 1) * 128],
                    bcdB[:, qsl], start=True, stop=True),
                     reads=[co.sel16, bcdB], writes=[co.pm[0]], tag="l1_bB")
                p.op("tensor", lambda t, n=n, qsl=qsl: t.matmul(
                    co.pm[1][0:128, 0:512], co.sel16[:, n * 128:(n + 1) * 128],
                    bcdC[:, qsl], start=True, stop=True),
                     reads=[co.sel16, bcdC], writes=[co.pm[1]], tag="l1_bC")
                sa, sb_, sh = co.s512[0], co.s512[1], co.s512[2]
                scol = 16 * d + n
                p.op("scalar", lambda e, scol=scol, sa=sa: e.activation(
                    sa[:, :], delta_b[:, :], AF.Exp,
                    scale=l1_A[:, scol:scol + 1]),
                     reads=[delta_b, l1_A], writes=[sa], tag="l1_dA")
                p.op("vector", lambda e, d=d, qsl=qsl, sb_=sb_: e.tensor_tensor(
                    sb_[:, :], du[d][:, qsl], co.pm[0][0:128, 0:512], ALU.mult),
                     reads=[du[d], co.pm[0]], writes=[sb_], tag="l1_bu")
                p.op("vector", lambda e, sa=sa, sb_=sb_, sh=sh, scol=scol:
                     e.tensor_tensor_scan(
                         sh[:, :], sa[:, :], sb_[:, :], states[:, scol:scol + 1],
                         ALU.mult, ALU.add),
                     reads=[sa, sb_, states], writes=[sh], tag="l1_scan")
                p.op("vector", lambda e, sh=sh, scol=scol: e.tensor_copy(
                    states[:, scol:scol + 1], sh[:, 511:512]),
                     reads=[sh], writes=[states], tag="l1_stu")
                p.op("vector", lambda e, sh=sh, sb_=sb_: e.tensor_tensor(
                    sb_[:, :], sh[:, :], co.pm[1][0:128, 0:512], ALU.mult),
                     reads=[sh, co.pm[1]], writes=[sb_], tag="l1_yt")
                if n == 0:
                    p.op("vector", lambda e, sb_=sb_: e.tensor_copy(
                        l1_y[:, :], sb_[:, :]),
                         reads=[sb_], writes=[l1_y], tag="l1_y0")
                else:
                    p.op("vector", lambda e, sb_=sb_: e.tensor_add(
                        l1_y[:, :], l1_y[:, :], sb_[:, :]),
                         reads=[l1_y, sb_], writes=[l1_y], tag="l1_ya")
            for dd in range(DC):
                p.dma("sync", co.wsb[dd][:, 128:256],
                      s_win.ap()[dd * 128:(dd + 1) * 128, D + d * 128:D + (d + 1) * 128],
                      reads=[s_win], writes=[co.wsb[dd]], tag="l1_wzl")
            for dd in range(DC):
                p.op("tensor", lambda t, dd=dd, qsl=qsl: t.matmul(
                    co.pm[2][0:128, 0:512], co.wsb[dd][:, 128:256], xn[dd][:, qsl],
                    start=(dd == 0), stop=(dd == DC - 1)),
                     reads=[co.wsb[dd], xn[dd]], writes=[co.pm[2]], tag="l1_mmz")
            zs = co.s512[0]
            p.op("scalar", lambda e, zs=zs: e.activation(
                zs[:, :], co.pm[2][0:128, 0:512], AF.Silu),
                 reads=[co.pm[2]], writes=[zs], tag="l1_zs")
            p.op("vector", lambda e, d=d, qsl=qsl: e.scalar_tensor_tensor(
                l1_y[:, :], xconv[d][:, qsl], Dcol[:, d:d + 1], l1_y[:, :],
                ALU.mult, ALU.add),
                 reads=[xconv[d], Dcol, l1_y], writes=[l1_y], tag="l1_Dt")
            p.op("vector", lambda e, d=d, qsl=qsl, zs=zs: e.tensor_tensor(
                du[d][:, qsl], l1_y[:, :], zs[:, :], ALU.mult),
                 reads=[l1_y, zs], writes=[du[d]], tag="l1_yz")
    for c in range(DC):
        for d in range(DC):
            p.dma("sync", co.wsb[d][:, 0:128],
                  s_wout.ap()[d * 128:(d + 1) * 128, c * 128:(c + 1) * 128],
                  reads=[s_wout], writes=[co.wsb[d]], tag="l1_wol")
        for j in range(NT):
            sl = slice(j * 512, (j + 1) * 512)
            for d in range(DC):
                p.op("tensor", lambda t, d=d, sl=sl: t.matmul(
                    co.pm[0][0:128, 0:512], co.wsb[d][:, 0:128], du[d][:, sl],
                    start=(d == 0), stop=(d == DC - 1)),
                     reads=[co.wsb[d], du[d]], writes=[co.pm[0]], tag="l1_mmo")
            xt = co.x[c]
            p.op("vector", lambda e, xt=xt, sl=sl: e.tensor_add(
                xt[:, sl], xt[:, sl], co.pm[0][0:128, 0:512]),
                 reads=[xt, co.pm[0]], writes=[xt], tag="l1_ores")


def emit_l1_ffn(p, co):
    w_ext = co.ein("L1_ffn_weight", [D, D], F32)
    bias = vec_cols(p, co, "L1_ffn_bias", DC, "l1_bias")
    xt_ = p.sb("l1_xt", [128, D], F32)
    wf = xt_
    xq = p.sb("l1_xq", [128, D], BF16)
    sc = p.sb("l1_sc", [128, 32], F32)
    qx = p.sb("l1_qx", [128, 512], F32)
    co.l1_qx = qx
    q = [co.s512[0], co.s512[1], co.s512[2], co.s512[3], qx]
    wq = co.pb3
    xdT = co.pb2
    for oc in range(DC):
        p.dma("sync", wf[:, :], w_ext.ap()[oc * 128:(oc + 1) * 128, :],
              reads=[w_ext], writes=[wf], tag="l1f_wl")
        for h in range(2):
            emit_quant(p, co, wf, h * 512, wq[oc], h * 512, q, sc, f"l1f_w{oc}{h}")
    for oc in range(DC):
        for ic in range(0, DC, 2):
            for k in range(2):
                p.op("tensor", lambda t, oc=oc, ic=ic, k=k: t.transpose(
                    co.ptb[0:128, k * 128:(k + 1) * 128],
                    wq[oc][:, (ic + k) * 128:(ic + k + 1) * 128], co.ident_bf[:, :]),
                     reads=[wq[oc], co.ident_bf], writes=[co.ptb], tag="l1f_wtr")
            for k in range(2):
                p.op("vector", lambda e, oc=oc, ic=ic, k=k: e.tensor_copy(
                    wq[ic + k][:, 1024 + oc * 128:1024 + (oc + 1) * 128],
                    co.ptb[0:128, k * 128:(k + 1) * 128]),
                     reads=[co.ptb], writes=[wq[ic + k]], tag="l1f_wte")
    for tc in range(TC):
        tsl = slice(tc * 128, (tc + 1) * 128)
        for g in range(2):
            for k in range(4):
                d = g * 4 + k
                p.op("tensor", lambda t, d=d, k=k, tsl=tsl: t.transpose(
                    co.paux[0:128, k * 128:(k + 1) * 128], co.x[d][:, tsl],
                    co.ident_f32[:, :]),
                     reads=[co.x[d], co.ident_f32], writes=[co.paux], tag="l1f_xtr")
            p.op("vector", lambda e, g=g, tc=tc: e.tensor_scalar(
                xt_[:, g * 512:(g + 1) * 512], co.paux[0:128, 0:512],
                co.colC[:, tc:tc + 1], co.colA[:, tc:tc + 1],
                ALU.mult, ALU.subtract),
                 reads=[co.paux, co.colC, co.colA], writes=[xt_], tag="l1f_xn")
        for h in range(2):
            emit_quant(p, co, xt_, h * 512, xq, h * 512, q, sc, f"l1f_x{tc}{h}")
        for ic in range(0, DC, 2):
            for k in range(2):
                p.op("tensor", lambda t, ic=ic, k=k: t.transpose(
                    co.ptb[0:128, k * 128:(k + 1) * 128],
                    xq[:, (ic + k) * 128:(ic + k + 1) * 128], co.ident_bf[:, :]),
                     reads=[xq, co.ident_bf], writes=[co.ptb], tag="l1f_btr")
            for k in range(2):
                p.op("vector", lambda e, ic=ic, k=k, tsl=tsl: e.tensor_copy(
                    xdT[ic + k][:, tsl], co.ptb[0:128, k * 128:(k + 1) * 128]),
                     reads=[co.ptb], writes=[xdT[ic + k]], tag="l1f_bte")
    for oc in range(DC):
        for j in range(NT):
            sl = slice(j * 512, (j + 1) * 512)
            for ic in range(DC):
                p.op("tensor", lambda t, oc=oc, ic=ic, sl=sl: t.matmul(
                    co.pm[0][0:128, 0:512],
                    wq[ic][:, 1024 + oc * 128:1024 + (oc + 1) * 128],
                    xdT[ic][:, sl], start=(ic == 0), stop=(ic == DC - 1)),
                     reads=[wq[ic], xdT[ic]], writes=[co.pm[0]], tag="l1f_mm")
            xt2 = co.x[oc]
            p.op("vector", lambda e, xt2=xt2, sl=sl, oc=oc: e.scalar_tensor_tensor(
                xt2[:, sl], co.pm[0][0:128, 0:512], bias[:, oc:oc + 1], xt2[:, sl],
                ALU.add, ALU.add),
                 reads=[co.pm[0], bias, xt2], writes=[xt2], tag="l1f_res")



L2_OFFS = [0] + [1 << k for k in range(11)]


def emit_l2(p, co):
    s_qkv = stage_bf16(p, co, "L2_attn_W_qkv", [D, 3 * D])
    s_proj = stage_bf16(p, co, "L2_attn_W_proj", [D, D])
    s_w1 = stage_bf16(p, co, "L2_ffn_w1_w", [4096, D])
    s_w2 = stage_bf16(p, co, "L2_ffn_w2_w", [4096, D])
    s_h1 = stage_bf16(p, co, "L2_ffn_h1_w", [4096, 4096])
    s_h2 = stage_bf16(p, co, "L2_ffn_h2_w", [4096, 4096])
    s_w3 = stage_bf16(p, co, "L2_ffn_w3_w", [D, 4096])

    emit_stats(p, co, co.x, False)
    emit_normalize(p, co, co.x, co.pb1)
    xn = co.pb1

    import os
    if not os.environ.get("L2_SKIP_FFN"):
        emit_l2_ffn(p, co, xn, s_w1, s_w2, s_h1, s_h2, s_w3)
    if not os.environ.get("L2_SKIP_ATTN"):
        emit_l2_attn(p, co, xn, s_qkv, s_proj)


def emit_l2_ffn(p, co, xn, s_w1, s_w2, s_h1, s_h2, s_w3):
    w1s = vec_cols(p, co, "L2_ffn_w1_s", 32, "l2_w1s")
    w2s = vec_cols(p, co, "L2_ffn_w2_s", 32, "l2_w2s")
    h1s = vec_cols(p, co, "L2_ffn_h1_s", 32, "l2_h1s")
    h2s = vec_cols(p, co, "L2_ffn_h2_s", 32, "l2_h2s")
    w3s = vec_cols(p, co, "L2_ffn_w3_s", DC, "l2_w3s")
    l2w = co.wtb

    def hslot(pool_sel, hh):
        t = pool_sel[hh // 4]
        off = (hh % 4) * 512
        return t, off

    for q in range(NT):
        qsl = slice(q * 512, (q + 1) * 512)
        for hb in range(8):
            for ic in range(DC):
                p.dma("sync", l2w[ic][:, :],
                      s_w1.ap()[hb * 512:(hb + 1) * 512, ic * 128:(ic + 1) * 128],
                      reads=[s_w1], writes=[l2w[ic]], tag="l2_w1l", transpose=True)
            for hc in range(4):
                for ic in range(DC):
                    p.op("tensor", lambda t, ic=ic, hc=hc, qsl=qsl: t.matmul(
                        co.pm[hc][0:128, 0:512], l2w[ic][:, hc * 128:(hc + 1) * 128],
                        xn[ic][:, qsl], start=(ic == 0), stop=(ic == DC - 1)),
                         reads=[l2w[ic], xn[ic]], writes=[co.pm[hc]], tag="l2_mm1")
            for ic in range(DC):
                p.dma("sync", l2w[ic][:, :],
                      s_w2.ap()[hb * 512:(hb + 1) * 512, ic * 128:(ic + 1) * 128],
                      reads=[s_w2], writes=[l2w[ic]], tag="l2_w2l", transpose=True)
            for hc in range(4):
                hh = 4 * hb + hc
                for ic in range(DC):
                    p.op("tensor", lambda t, ic=ic, hc=hc, qsl=qsl: t.matmul(
                        co.pm[4][0:128, 0:512], l2w[ic][:, hc * 128:(hc + 1) * 128],
                        xn[ic][:, qsl], start=(ic == 0), stop=(ic == DC - 1)),
                         reads=[l2w[ic], xn[ic]], writes=[co.pm[4]], tag="l2_mm2")
                sg = co.s512[0]
                p.op("scalar", lambda e, hh=hh, hc=hc, sg=sg: e.activation(
                    sg[:, :], co.pm[hc][0:128, 0:512], AF.Silu,
                    scale=w1s[:, hh:hh + 1]),
                     reads=[co.pm[hc], w1s], writes=[sg], tag="l2_silu")
                dstt, doff = hslot(co.pb2, hh)
                p.op("vector", lambda e, hh=hh, dstt=dstt, doff=doff, sg=sg:
                     e.scalar_tensor_tensor(
                         dstt[:, doff:doff + 512], co.pm[4][0:128, 0:512],
                         w2s[:, hh:hh + 1], sg[:, :], ALU.mult, ALU.mult),
                     reads=[co.pm[4], w2s, sg], writes=[dstt], tag="l2_h")
        for (srcp, dstp, s_h, scol) in ((co.pb2, co.pb3, s_h1, h1s),
                                        (co.pb3, co.pb2, s_h2, h2s)):
            for ob in range(8):
                for hh in range(32):
                    wt = l2w[hh % 8]
                    p.dma("sync", wt[:, :],
                          s_h.ap()[ob * 512:(ob + 1) * 512, hh * 128:(hh + 1) * 128],
                          reads=[s_h], writes=[wt], tag="l2_hl", transpose=True)
                    st, soff = hslot(srcp, hh)
                    for oc in range(4):
                        p.op("tensor", lambda t, wt=wt, oc=oc, st=st, soff=soff, hh=hh:
                             t.matmul(
                                 co.pm[oc][0:128, 0:512], wt[:, oc * 128:(oc + 1) * 128],
                                 st[:, soff:soff + 512],
                                 start=(hh == 0), stop=(hh == 31)),
                             reads=[wt, st], writes=[co.pm[oc]], tag="l2_hmm")
                for oc in range(4):
                    oo = 4 * ob + oc
                    dstt, doff = hslot(dstp, oo)
                    p.op("scalar", lambda e, oc=oc, oo=oo, dstt=dstt, doff=doff:
                         e.activation(
                             dstt[:, doff:doff + 512], co.pm[oc][0:128, 0:512],
                             AF.Gelu, scale=scol[:, oo:oo + 1]),
                         reads=[co.pm[oc], scol], writes=[dstt], tag="l2_gelu")
        for cb in range(2):
            for hh in range(32):
                wt = l2w[hh % 8]
                p.dma("sync", wt[:, :],
                      s_w3.ap()[cb * 512:(cb + 1) * 512, hh * 128:(hh + 1) * 128],
                      reads=[s_w3], writes=[wt], tag="l2_w3l", transpose=True)
                st, soff = hslot(co.pb2, hh)
                for oc in range(4):
                    p.op("tensor", lambda t, wt=wt, oc=oc, st=st, soff=soff, hh=hh:
                         t.matmul(
                             co.pm[oc][0:128, 0:512], wt[:, oc * 128:(oc + 1) * 128],
                             st[:, soff:soff + 512],
                             start=(hh == 0), stop=(hh == 31)),
                         reads=[wt, st], writes=[co.pm[oc]], tag="l2_w3mm")
            for oc in range(4):
                cc = 4 * cb + oc
                xt = co.x[cc]
                p.op("vector", lambda e, oc=oc, cc=cc, xt=xt, qsl=qsl:
                     e.scalar_tensor_tensor(
                         xt[:, qsl], co.pm[oc][0:128, 0:512], w3s[:, cc:cc + 1],
                         xt[:, qsl], ALU.mult, ALU.add),
                     reads=[co.pm[oc], w3s, xt], writes=[xt], tag="l2_fres")


def emit_l2_attn(p, co, xn, s_qkv, s_proj):
    qkvb = vec_cols(p, co, "L2_attn_b_qkv", 24, "l2_qkvb")
    projb = vec_cols(p, co, "L2_attn_b_proj", DC, "l2_projb")
    sa = co.cload("c_l2sa")
    lmask = co.cload("c_l2mask")
    lpb = co.cload("c_l2pb")
    lden = co.cload("c_l2den")
    lrb = co.cload("c_l2rb")
    l2q = p.view(co.POOL3, 0, T)
    l2k = p.view(co.POOL3, T, D + T)
    l2v = p.view(co.POOL3, T + D + T, D + T)
    l2E = p.view(co.POOL3, 2 * (D + T) + T, T, nrows=24)
    l2es = p.view(co.POOL3, 2 * (D + T) + 2 * T, 512)
    l2rd = p.view(co.POOL3, 2 * (D + T) + 2 * T + 512, 512, nrows=2)
    l2nc = p.sb("l2nc", [24, 1], F32)
    l2r1 = p.sb("l2r1", [1, 32], F32)
    p.op("vector", lambda e: e.memset(l2k[:, 0:D], 0.0), writes=[l2k], tag="l2_kz")
    p.op("vector", lambda e: e.memset(l2v[:, 0:D], 0.0), writes=[l2v], tag="l2_vz")
    attnout = co.pb2
    for hp in range(8):
        for part, dst, doff, bbase in ((0, l2q, 0, 0), (1, l2k, D, 8), (2, l2v, D, 16)):
            for d in range(DC):
                p.dma("sync", co.wsb[d][:, 0:128],
                      s_qkv.ap()[d * 128:(d + 1) * 128,
                                 part * D + hp * 128:part * D + (hp + 1) * 128],
                      reads=[s_qkv], writes=[co.wsb[d]], tag="l2_wql")
            for j in range(NT):
                sl = slice(j * 512, (j + 1) * 512)
                for d in range(DC):
                    p.op("tensor", lambda t, d=d, sl=sl: t.matmul(
                        co.pm[0][0:128, 0:512], co.wsb[d][:, 0:128], xn[d][:, sl],
                        start=(d == 0), stop=(d == DC - 1)),
                         reads=[co.wsb[d], xn[d]], writes=[co.pm[0]], tag="l2_mmqkv")
                p.op("vector", lambda e, dst=dst, doff=doff, j=j, bbase=bbase, hp=hp:
                     e.tensor_scalar_add(
                         dst[:, doff + j * 512:doff + (j + 1) * 512],
                         co.pm[0][0:128, 0:512], qkvb[:, bbase + hp:bbase + hp + 1]),
                     reads=[co.pm[0], qkvb], writes=[dst], tag="l2_qkve")
        for j in range(4):
            p.op("tensor", lambda t, j=j: t.matmul(
                co.PA.h[0:24, j * 512:(j + 1) * 512], co.ones[0:1, 0:24],
                co.zrow[0:1, 0:512], start=True, stop=False),
                 reads=[co.ones, co.zrow], writes=[co.PA], tag="l2_sz")
        for o in range(12):
            off = L2_OFFS[o]
            for j in range(4):
                sl = slice(j * 512, (j + 1) * 512)
                p.op("vector", lambda e, off=off, j=j: e.tensor_tensor(
                    l2es[:, :], l2q[:, j * 512:(j + 1) * 512],
                    l2k[:, D - off + j * 512:D - off + (j + 1) * 512], ALU.mult),
                     reads=[l2q, l2k], writes=[l2es], tag="l2_eo")
                p.op("tensor", lambda t, o=o, j=j: t.matmul(
                    co.PA.h[0:24, j * 512:(j + 1) * 512], sa[:, 24 * o:24 * (o + 1)],
                    l2es[:, :], start=False, stop=False),
                     reads=[sa, l2es], writes=[co.PA], tag="l2_smm")
        for j in range(4):
            p.op("tensor", lambda t, j=j: t.matmul(
                co.PA.h[0:24, j * 512:(j + 1) * 512], co.ident_bf[0:24, 0:24],
                lmask[0:24, j * 512:(j + 1) * 512], start=False, stop=(j == 3)),
                 reads=[co.ident_bf, lmask], writes=[co.PA], tag="l2_maskmm")
        p.op("vector", lambda e: e.reduce_max(l2nc[:, 0:1], co.PA.h[0:24, 0:T], AX.X),
             reads=[co.PA], writes=[l2nc], tag="l2_m24")
        p.op("tensor", lambda t: t.transpose(
            co.paux[0:1, 0:24], l2nc[:, 0:1], co.ident_f32[0:24, 0:24]),
             reads=[l2nc, co.ident_f32], writes=[co.paux], tag="l2_mtr")
        p.op("vector", lambda e: e.tensor_copy(l2r1[:, 0:24], co.paux[0:1, 0:24]),
             reads=[co.paux], writes=[l2r1], tag="l2_mev")
        p.op("vector", lambda e: e.reduce_max(l2r1[:, 24:25], l2r1[:, 0:24], AX.X),
             reads=[l2r1], writes=[l2r1], tag="l2_mred")
        p.op("vector", lambda e: e.tensor_scalar_mul(l2r1[:, 25:26], l2r1[:, 24:25],
                                                     -0.125),
             reads=[l2r1], writes=[l2r1], tag="l2_mneg")
        p.op("tensor", lambda t: t.matmul(
            co.paux[0:24, 0:1], co.ones[0:1, 0:24], l2r1[:, 25:26],
            start=True, stop=True),
             reads=[co.ones, l2r1], writes=[co.paux], tag="l2_mbc")
        p.op("vector", lambda e: e.tensor_copy(l2nc[:, 0:1], co.paux[0:24, 0:1]),
             reads=[co.paux], writes=[l2nc], tag="l2_mbe")
        p.op("scalar", lambda e: e.activation(
            l2E[:, :], co.PA.h[0:24, 0:T], AF.Exp, scale=co.cv(0.125, 24),
            bias=l2nc[:, 0:1]),
             reads=[co.PA, co.cvt, l2nc], writes=[l2E], tag="l2_exp")
        for j in range(4):
            sl = slice(j * 512, (j + 1) * 512)
            p.op("tensor", lambda t, j=j: t.matmul(
                co.pm[4][0:2, 0:512], lden[0:24, 0:2], l2E[:, j * 512:(j + 1) * 512],
                start=True, stop=True),
                 reads=[lden, l2E], writes=[co.pm[4]], tag="l2_dmm")
            p.op("scalar", lambda e, j=j: e.activation(
                l2rd[:, 0:512], co.pm[4][0:2, 0:512], AF.Ln),
                 reads=[co.pm[4]], writes=[l2rd], tag="l2_dln")
            p.op("scalar", lambda e, j=j: e.activation(
                l2rd[:, 0:512], l2rd[:, 0:512],
                AF.Exp, scale=co.cv(-1.0, 2)),
                 reads=[l2rd, co.cvt], writes=[l2rd], tag="l2_dexp")
            acc, tmp = co.s512[1], co.s512[2]
            for o in range(12):
                off = L2_OFFS[o]
                p.op("tensor", lambda t, o=o, j=j: t.matmul(
                    co.paux[0:128, 0:512], lpb[0:24, 128 * o:128 * (o + 1)],
                    l2E[:, j * 512:(j + 1) * 512], start=True, stop=True),
                     reads=[lpb, l2E], writes=[co.paux], tag="l2_pbm")
                if o == 0:
                    p.op("vector", lambda e, off=off, j=j, acc=acc: e.tensor_tensor(
                        acc[:, :], l2v[:, D - off + j * 512:D - off + (j + 1) * 512],
                        co.paux[0:128, 0:512], ALU.mult),
                         reads=[l2v, co.paux], writes=[acc], tag="l2_oa0")
                else:
                    p.op("vector", lambda e, off=off, j=j, tmp=tmp: e.tensor_tensor(
                        tmp[:, :], l2v[:, D - off + j * 512:D - off + (j + 1) * 512],
                        co.paux[0:128, 0:512], ALU.mult),
                         reads=[l2v, co.paux], writes=[tmp], tag="l2_oam")
                    p.op("vector", lambda e, acc=acc, tmp=tmp: e.tensor_add(
                        acc[:, :], acc[:, :], tmp[:, :]),
                         reads=[acc, tmp], writes=[acc], tag="l2_oaa")
            p.op("tensor", lambda t, j=j: t.matmul(
                co.paux[0:128, 0:512], lrb[0:2, 0:128], l2rd[0:2, 0:512],
                start=True, stop=True),
                 reads=[lrb, l2rd], writes=[co.paux], tag="l2_rbm")
            p.op("vector", lambda e, hp=hp, j=j, acc=acc: e.tensor_tensor(
                attnout[hp][:, j * 512:(j + 1) * 512], acc[:, :],
                co.paux[0:128, 0:512], ALU.mult),
                 reads=[acc, co.paux], writes=[attnout[hp]], tag="l2_onorm")
    for c in range(DC):
        for d in range(DC):
            p.dma("sync", co.wsb[d][:, 0:128],
                  s_proj.ap()[d * 128:(d + 1) * 128, c * 128:(c + 1) * 128],
                  reads=[s_proj], writes=[co.wsb[d]], tag="l2_wpl")
        for j in range(NT):
            sl = slice(j * 512, (j + 1) * 512)
            for d in range(DC):
                p.op("tensor", lambda t, d=d, sl=sl: t.matmul(
                    co.pm[0][0:128, 0:512], co.wsb[d][:, 0:128], attnout[d][:, sl],
                    start=(d == 0), stop=(d == DC - 1)),
                     reads=[co.wsb[d], attnout[d]], writes=[co.pm[0]], tag="l2_mmp")
            xt = co.x[c]
            p.op("vector", lambda e, xt=xt, sl=sl, c=c: e.scalar_tensor_tensor(
                xt[:, sl], co.pm[0][0:128, 0:512], projb[:, c:c + 1], xt[:, sl],
                ALU.add, ALU.add),
                 reads=[co.pm[0], projb, xt], writes=[xt], tag="l2_pres")



def emit_l3(p, co):
    s_wq = stage_bf16(p, co, "L3_attn_Wq", [D, D])
    s_wk = stage_bf16(p, co, "L3_attn_Wk", [D, D])
    s_wv = stage_bf16(p, co, "L3_attn_Wv", [D, D])
    s_wo = stage_bf16(p, co, "L3_attn_Wo", [D, D])
    s_sh1 = stage_bf16(p, co, "L3_ffn_shared_w1", [4096, D])
    s_sh2 = stage_bf16(p, co, "L3_ffn_shared_w2", [D, 4096])
    s_ew1 = stage_bf16(p, co, "L3_ffn_experts_w1", [8, 4096, D])
    s_ew2 = stage_bf16(p, co, "L3_ffn_experts_w2", [8, D, 4096])
    s_gw = stage_bf16(p, co, "L3_ffn_gate_w", [8, D])

    emit_stats(p, co, co.x, False)
    emit_normalize(p, co, co.x, co.pb1)
    xn = co.pb1

    import os
    if not os.environ.get("L3_SKIP_FFN"):
        emit_l3_ffn(p, co, xn, s_sh1, s_sh2, s_ew1, s_ew2, s_gw)
    if not os.environ.get("L3_SKIP_ATTN"):
        emit_l3_attn(p, co, xn, s_wq, s_wk, s_wv, s_wo)


def emit_l3_ffn(p, co, xn, s_sh1, s_sh2, s_ew1, s_ew2, s_gw):
    wr8 = p.sb("l3_wr8", [8, D], BF16)
    p.dma("sync", wr8[:, :], s_gw.ap(), reads=[s_gw], writes=[wr8], tag="l3_gwl")
    wrT = p.sb("l3_wrT", [128, 64], BF16)
    for d in range(DC):
        p.op("tensor", lambda t, d=d: t.transpose(
            co.ptb[0:128, 0:8], wr8[0:8, d * 128:(d + 1) * 128],
            co.ident_bf[0:8, 0:8]),
             reads=[wr8, co.ident_bf], writes=[co.ptb], tag="l3_gtr")
        p.op("vector", lambda e, d=d: e.tensor_copy(
            wrT[:, d * 8:(d + 1) * 8], co.ptb[0:128, 0:8]),
             reads=[co.ptb], writes=[wrT], tag="l3_gte")
    GT = emit_router(p, co, xn, wrT, "l3")

    def hslot(pool_sel, hh):
        t = pool_sel[hh // 4]
        off = (hh % 4) * 512
        return t, off

    gbt = co.s512[3]
    for q in range(NT):
        qsl = slice(q * 512, (q + 1) * 512)
        for hb in range(8):
            for ic in range(DC):
                p.dma("sync", co.wtb[ic][:, :],
                      s_sh1.ap()[hb * 512:(hb + 1) * 512, ic * 128:(ic + 1) * 128],
                      reads=[s_sh1], writes=[co.wtb[ic]], tag="l3_s1l", transpose=True)
            for hc in range(4):
                hh = 4 * hb + hc
                for ic in range(DC):
                    p.op("tensor", lambda t, ic=ic, hc=hc, qsl=qsl: t.matmul(
                        co.pm[hc][0:128, 0:512], co.wtb[ic][:, hc * 128:(hc + 1) * 128],
                        xn[ic][:, qsl], start=(ic == 0), stop=(ic == DC - 1)),
                         reads=[co.wtb[ic], xn[ic]], writes=[co.pm[hc]], tag="l3_s1mm")
            for hc in range(4):
                hh = 4 * hb + hc
                dstt, doff = hslot(co.pb2, hh)
                p.op("scalar", lambda e, hc=hc, dstt=dstt, doff=doff: e.activation(
                    dstt[:, doff:doff + 512], co.pm[hc][0:128, 0:512], AF.Silu),
                     reads=[co.pm[hc]], writes=[dstt], tag="l3_s1act")
        for cb in range(2):
            for hh in range(32):
                wt = co.wtb[hh % 8]
                p.dma("sync", wt[:, :],
                      s_sh2.ap()[cb * 512:(cb + 1) * 512, hh * 128:(hh + 1) * 128],
                      reads=[s_sh2], writes=[wt], tag="l3_s2l", transpose=True)
                st, soff = hslot(co.pb2, hh)
                for oc in range(4):
                    p.op("tensor", lambda t, wt=wt, oc=oc, st=st, soff=soff, hh=hh:
                         t.matmul(
                             co.pm[oc][0:128, 0:512], wt[:, oc * 128:(oc + 1) * 128],
                             st[:, soff:soff + 512],
                             start=(hh == 0), stop=(hh == 31)),
                         reads=[wt, st], writes=[co.pm[oc]], tag="l3_s2mm")
            for oc in range(4):
                cc = 4 * cb + oc
                xt = co.x[cc]
                p.op("vector", lambda e, oc=oc, xt=xt, qsl=qsl: e.tensor_add(
                    xt[:, qsl], xt[:, qsl], co.pm[oc][0:128, 0:512]),
                     reads=[xt, co.pm[oc]], writes=[xt], tag="l3_s2res")
        for e_ in range(8):
            for hb in range(8):
                for ic in range(DC):
                    p.dma("sync", co.wtb[ic][:, :],
                          s_ew1.ap()[e_, hb * 512:(hb + 1) * 512, ic * 128:(ic + 1) * 128],
                          reads=[s_ew1], writes=[co.wtb[ic]], tag="l3_e1l", transpose=True)
                for hc in range(4):
                    for ic in range(DC):
                        p.op("tensor", lambda t, ic=ic, hc=hc, qsl=qsl: t.matmul(
                            co.pm[hc][0:128, 0:512],
                            co.wtb[ic][:, hc * 128:(hc + 1) * 128],
                            xn[ic][:, qsl], start=(ic == 0), stop=(ic == DC - 1)),
                             reads=[co.wtb[ic], xn[ic]], writes=[co.pm[hc]], tag="l3_e1mm")
                for hc in range(4):
                    hh = 4 * hb + hc
                    dstt, doff = hslot(co.pb3, hh)
                    p.op("scalar", lambda e, hc=hc, dstt=dstt, doff=doff: e.activation(
                        dstt[:, doff:doff + 512], co.pm[hc][0:128, 0:512], AF.Silu),
                         reads=[co.pm[hc]], writes=[dstt], tag="l3_e1act")
            p.op("tensor", lambda t, e_=e_, qsl=qsl: t.matmul(
                co.paux[0:128, 0:512], co.esel[:, e_ * 128:(e_ + 1) * 128],
                GT[0:8, qsl], start=True, stop=True),
                 reads=[co.esel, GT], writes=[co.paux], tag="l3_gbc")
            p.op("vector", lambda e: e.tensor_copy(gbt[:, :], co.paux[0:128, 0:512]),
                 reads=[co.paux], writes=[gbt], tag="l3_gbcc")
            for cb in range(2):
                for hh in range(32):
                    wt = co.wtb[hh % 8]
                    p.dma("sync", wt[:, :],
                          s_ew2.ap()[e_, cb * 512:(cb + 1) * 512, hh * 128:(hh + 1) * 128],
                          reads=[s_ew2], writes=[wt], tag="l3_e2l", transpose=True)
                    st, soff = hslot(co.pb3, hh)
                    for oc in range(4):
                        p.op("tensor", lambda t, wt=wt, oc=oc, st=st, soff=soff, hh=hh:
                             t.matmul(
                                 co.pm[oc][0:128, 0:512], wt[:, oc * 128:(oc + 1) * 128],
                                 st[:, soff:soff + 512],
                                 start=(hh == 0), stop=(hh == 31)),
                             reads=[wt, st], writes=[co.pm[oc]], tag="l3_e2mm")
                for oc in range(4):
                    cc = 4 * cb + oc
                    xt = co.x[cc]
                    tmp = co.s512[2]
                    p.op("vector", lambda e, oc=oc, tmp=tmp: e.tensor_tensor(
                        tmp[:, :], co.pm[oc][0:128, 0:512], gbt[:, :], ALU.mult),
                         reads=[co.pm[oc], gbt], writes=[tmp], tag="l3_egate")
                    p.op("vector", lambda e, xt=xt, qsl=qsl, tmp=tmp: e.tensor_add(
                        xt[:, qsl], xt[:, qsl], tmp[:, :]),
                         reads=[xt, tmp], writes=[xt], tag="l3_eres")


def emit_router(p, co, xn, wrT, pref):
    GT = p.sb(pref + "_GT", [8, T], BF16)
    rtp = co.s512[0]
    for tc in range(TC):
        tsl = slice(tc * 128, (tc + 1) * 128)
        for d in range(DC):
            p.op("tensor", lambda t, d=d, tsl=tsl: t.matmul(
                co.pm[0][0:128, 0:8], xn[d][:, tsl], wrT[:, d * 8:(d + 1) * 8],
                start=(d == 0), stop=(d == DC - 1)),
                 reads=[xn[d], wrT], writes=[co.pm[0]], tag=pref + "_mmr")
        p.op("vector", lambda e: e.tensor_copy(rtp[:, 0:8], co.pm[0][0:128, 0:8]),
             reads=[co.pm[0]], writes=[rtp], tag=pref + "_rt")
        p.op("vector", lambda e: e.reduce_max(rtp[:, 16:17], rtp[:, 0:8], AX.X),
             reads=[rtp], writes=[rtp], tag=pref + "_m1")
        p.op("vector", lambda e: e.tensor_scalar(
            rtp[:, 24:32], rtp[:, 0:8], rtp[:, 16:17], None, ALU.is_equal),
             reads=[rtp], writes=[rtp], tag=pref + "_eq1")
        p.op("vector", lambda e: e.scalar_tensor_tensor(
            rtp[:, 8:16], rtp[:, 24:32], -1e30, rtp[:, 0:8], ALU.mult, ALU.add),
             reads=[rtp], writes=[rtp], tag=pref + "_mask1")
        p.op("vector", lambda e: e.reduce_max(rtp[:, 17:18], rtp[:, 8:16], AX.X),
             reads=[rtp], writes=[rtp], tag=pref + "_m2")
        p.op("vector", lambda e: e.tensor_scalar(
            rtp[:, 32:40], rtp[:, 8:16], rtp[:, 17:18], None, ALU.is_equal),
             reads=[rtp], writes=[rtp], tag=pref + "_eq2")
        p.op("vector", lambda e: e.tensor_sub(rtp[:, 18:19], rtp[:, 16:17], rtp[:, 17:18]),
             reads=[rtp], writes=[rtp], tag=pref + "_dm")
        p.op("scalar", lambda e: e.activation(rtp[:, 18:19], rtp[:, 18:19], AF.Sigmoid),
             reads=[rtp], writes=[rtp], tag=pref + "_g1")
        p.op("vector", lambda e: e.tensor_scalar(
            rtp[:, 19:20], rtp[:, 18:19], -1.0, 1.0, ALU.mult, ALU.add),
             reads=[rtp], writes=[rtp], tag=pref + "_g2")
        p.op("vector", lambda e: e.tensor_scalar(
            rtp[:, 24:32], rtp[:, 24:32], rtp[:, 18:19], None, ALU.mult),
             reads=[rtp], writes=[rtp], tag=pref + "_p1")
        p.op("vector", lambda e: e.scalar_tensor_tensor(
            rtp[:, 24:32], rtp[:, 32:40], rtp[:, 19:20], rtp[:, 24:32],
            ALU.mult, ALU.add),
             reads=[rtp], writes=[rtp], tag=pref + "_p2")
        gtile = p.sb(f"{pref}_g{tc}", [128, 8], BF16)
        p.op("vector", lambda e, gtile=gtile: e.tensor_copy(gtile[:, :], rtp[:, 24:32]),
             reads=[rtp], writes=[gtile], tag=pref + "_gcast")
        p.op("tensor", lambda t, gtile=gtile: t.transpose(
            co.ptb[0:8, 0:128], gtile[:, :], co.ident_bf[:, :]),
             reads=[gtile, co.ident_bf], writes=[co.ptb], tag=pref + "_gtr2")
        p.op("vector", lambda e, tsl=tsl: e.tensor_copy(GT[0:8, tsl], co.ptb[0:8, 0:128]),
             reads=[co.ptb], writes=[GT], tag=pref + "_gtc")
    return GT


def emit_l3_attn(p, co, xn, s_wq, s_wk, s_wv, s_wo):
    bq = vec_cols(p, co, "L3_attn_bq", DC, "l3_bq")
    bk = vec_cols(p, co, "L3_attn_bk", DC, "l3_bk")
    bo = vec_cols(p, co, "L3_attn_bo", DC, "l3_bo")
    bv_ext = co.ein("L3_attn_bv", [D], F32)
    bvr = p.sb("l3_bvr", [1, D], BF16)
    p.dma("gpsimd", bvr[:, :], bv_ext.ap().rearrange("(o n) -> o n", o=1),
          reads=[bv_ext], writes=[bvr], tag="l3_bvl")
    hsel = co.cload("c_l3hs")
    expb_ext = co.ein("c_l3expb", [T, T], BF16)
    l3q = [p.view(co.POOL3, 0, T), p.view(co.POOL3, T, T)]
    l3k = [p.view(co.POOL3, 2 * T, T), p.view(co.POOL3, 3 * T, T)]
    l3v = p.view(co.POOL3, 4 * T, 4096)
    l3es = p.view(co.POOL3, 4 * T + 4096, 1024)
    l3eb = [p.view(co.POOL3, 4 * T + 4096 + 1024 + i * 1024, 1024) for i in range(2)]
    l3rb = p.sb("l3_rb", [64, 1024], F32)
    l3nc = p.sb("l3_nc", [128, 8], F32)
    l3sm = p.sb("l3_sm", [2, 32], F32)
    attnout = co.pb2
    pm01 = PsumView(co.PA, 0, 1024)
    pm23 = PsumView(co.PA, 1024, 1024)
    pden = PsumView(co.PB, 0, 1024)
    for quad in range(4):
        for part, s_w, bias in ((0, s_wq, bq), (1, s_wk, bk)):
            dstp = l3q if part == 0 else l3k
            for hp in range(2):
                for d in range(DC):
                    p.dma("sync", co.wsb[d][:, 0:128],
                          s_w.ap()[d * 128:(d + 1) * 128,
                                   quad * 256 + hp * 128:quad * 256 + (hp + 1) * 128],
                          reads=[s_w], writes=[co.wsb[d]], tag="l3_wl")
                for j in range(NT):
                    sl = slice(j * 512, (j + 1) * 512)
                    for d in range(DC):
                        p.op("tensor", lambda t, d=d, sl=sl: t.matmul(
                            co.pm[0][0:128, 0:512], co.wsb[d][:, 0:128], xn[d][:, sl],
                            start=(d == 0), stop=(d == DC - 1)),
                             reads=[co.wsb[d], xn[d]], writes=[co.pm[0]], tag="l3_mmqk")
                    p.op("vector", lambda e, hp=hp, sl=sl, quad=quad, bias=bias,
                         dstp=dstp: e.tensor_scalar_add(
                             dstp[hp][:, sl], co.pm[0][0:128, 0:512],
                             bias[:, 2 * quad + hp:2 * quad + hp + 1]),
                         reads=[co.pm[0], bias], writes=[dstp[hp]], tag="l3_qke")
        for d in range(DC):
            p.dma("sync", co.wsb[d][:, 0:256],
                  s_wv.ap()[d * 128:(d + 1) * 128, quad * 256:(quad + 1) * 256],
                  reads=[s_wv], writes=[co.wsb[d]], tag="l3_wvl")
        for s in range(TC):
            ssl = slice(s * 128, (s + 1) * 128)
            for d in range(DC):
                p.op("tensor", lambda t, d=d, ssl=ssl: t.matmul(
                    co.pm[1][0:128, 0:256], xn[d][:, ssl], co.wsb[d][:, 0:256],
                    start=(d == 0), stop=False),
                     reads=[xn[d], co.wsb[d]], writes=[co.pm[1]], tag="l3_mmv")
            p.op("tensor", lambda t, quad=quad: t.matmul(
                co.pm[1][0:128, 0:256], co.ones_bf[0:1, 0:128],
                bvr[:, quad * 256:(quad + 1) * 256], start=False, stop=True),
                 reads=[co.ones_bf, bvr], writes=[co.pm[1]], tag="l3_mmvb")
            p.op("vector", lambda e, s=s: e.tensor_copy(
                l3v[:, s * 256:(s + 1) * 256], co.pm[1][0:128, 0:256]),
                 reads=[co.pm[1]], writes=[l3v], tag="l3_vev")
        for hp in range(2):
            for (src, col) in ((l3q[hp], 0), (l3k[hp], 2)):
                sq = co.s512[1]
                for j in range(NT):
                    p.op("scalar", lambda e, src=src, j=j, sq=sq: e.activation(
                        sq[:, :], src[:, j * 512:(j + 1) * 512], AF.Square),
                         reads=[src], writes=[sq], tag="l3_nsq")
                    p.op("tensor", lambda t, j=j, sq=sq: t.matmul(
                        co.pm[4][0:2, 0:512], hsel[0:128, 0:2], sq[:, :],
                        start=True, stop=True),
                         reads=[hsel, sq], writes=[co.pm[4]], tag="l3_nmm")
                    p.op("vector", lambda e, j=j, col=col: e.reduce_max(
                        l3sm[:, col + 16 + j:col + 17 + j], co.pm[4][0:2, 0:512], AX.X),
                         reads=[co.pm[4]], writes=[l3sm], tag="l3_nred")
                p.op("vector", lambda e, col=col: e.reduce_max(
                    l3sm[:, col:col + 1], l3sm[:, col + 16:col + 20], AX.X),
                     reads=[l3sm], writes=[l3sm], tag="l3_nred2")
            p.op("vector", lambda e: e.tensor_mul(l3sm[:, 4:5], l3sm[:, 0:1],
                                                  l3sm[:, 2:3]),
                 reads=[l3sm], writes=[l3sm], tag="l3_nm")
            p.op("scalar", lambda e: e.activation(l3sm[:, 4:5], l3sm[:, 4:5], AF.Ln),
                 reads=[l3sm], writes=[l3sm], tag="l3_nln")
            p.op("scalar", lambda e: e.activation(l3sm[:, 4:5], l3sm[:, 4:5], AF.Exp,
                                                  scale=co.cv(0.5, 2)),
                 reads=[l3sm, co.cvt], writes=[l3sm], tag="l3_nexp")
            p.op("vector", lambda e: e.tensor_scalar_mul(l3sm[:, 5:6], l3sm[:, 4:5],
                                                         -0.125),
                 reads=[l3sm], writes=[l3sm], tag="l3_nneg")
            p.op("tensor", lambda t: t.transpose(
                co.paux[0:1, 0:2], l3sm[0:2, 5:6], co.ident_f32[0:2, 0:2]),
                 reads=[l3sm, co.ident_f32], writes=[co.paux], tag="l3_ntr")
            p.op("vector", lambda e: e.tensor_copy(l3sm[0:1, 6:8], co.paux[0:1, 0:2]),
                 reads=[co.paux], writes=[l3sm], tag="l3_nte")
            for hh in range(2):
                p.op("tensor", lambda t, hh=hh: t.matmul(
                    co.paux[0:128, 0:1], co.ones[0:1, :], l3sm[0:1, 6 + hh:7 + hh],
                    start=True, stop=True),
                     reads=[co.ones, l3sm], writes=[co.paux], tag="l3_nbc")
                p.op("vector", lambda e, hp=hp, hh=hh: e.tensor_copy(
                    l3nc[:, 2 * hp + hh:2 * hp + hh + 1], co.paux[0:128, 0:1]),
                     reads=[co.paux], writes=[l3nc], tag="l3_nbe")
        for h in range(4):
            hp, hq = h // 2, h % 2
            base = hq * 64
            for qh in range(2):
                qof = qh * 1024
                for s in range(TC):
                    ebuf = l3eb[s % 2]
                    p.dma("sync", ebuf[:, 0:1024],
                          expb_ext.ap()[s * 128:(s + 1) * 128, qof:qof + 1024],
                          reads=[expb_ext], writes=[ebuf], tag="l3_ebl")
                    for u in range(2):
                        p.op("tensor", lambda t, s=s, u=u, qof=qof, hp=hp, base=base:
                             t.matmul(
                                 pm23[0 + 0:128, u * 512:(u + 1) * 512] if False else
                                 co.PA.h[0:128, 1024 + u * 512:1024 + (u + 1) * 512],
                                 l3k[hp][base:base + 64, s * 128:(s + 1) * 128],
                                 l3q[hp][base:base + 64, qof + u * 512:qof + (u + 1) * 512],
                                 start=True, stop=True),
                             reads=[l3k[hp], l3q[hp]], writes=[pm23], tag="l3_smm")
                    p.op("scalar", lambda e, hp=hp, hq=hq, h=h: e.activation(
                        l3es[:, 0:1024], co.PA.h[0:128, 1024:2048], AF.Exp,
                        scale=co.cv(0.125), bias=l3nc[:, h:h + 1]),
                         reads=[pm23, co.cvt, l3nc], writes=[l3es], tag="l3_sexp")
                    p.op("vector", lambda e, ebuf=ebuf: e.tensor_mul(
                        l3es[:, 0:1024], l3es[:, 0:1024], ebuf[:, 0:1024]),
                         reads=[l3es, ebuf], writes=[l3es], tag="l3_eb")
                    for u in range(2):
                        p.op("tensor", lambda t, s=s, u=u, h=h: t.matmul(
                            co.PA.h[0:64, u * 512:(u + 1) * 512],
                            l3v[:, s * 256 + h * 64:s * 256 + h * 64 + 64],
                            l3es[:, u * 512:(u + 1) * 512],
                            start=(s == 0), stop=(s == TC - 1)),
                             reads=[l3v, l3es], writes=[pm01], tag="l3_omm")
                        p.op("tensor", lambda t, s=s, u=u: t.matmul(
                            co.PB.h[0:1, u * 512:(u + 1) * 512], co.ones_bf[:, 0:1],
                            l3es[:, u * 512:(u + 1) * 512],
                            start=(s == 0), stop=(s == TC - 1)),
                             reads=[co.ones_bf, l3es], writes=[pden], tag="l3_dmm")
                p.op("scalar", lambda e: e.activation(
                    co.s512[1][0:1, 0:512], co.PB.h[0:1, 0:512], AF.Ln),
                     reads=[pden], writes=[co.s512[1]], tag="l3_dln1")
                p.op("scalar", lambda e: e.activation(
                    co.s512[2][0:1, 0:512], co.PB.h[0:1, 512:1024], AF.Ln),
                     reads=[pden], writes=[co.s512[2]], tag="l3_dln2")
                p.op("scalar", lambda e: e.activation(
                    co.s512[1][0:1, 0:512], co.s512[1][0:1, 0:512], AF.Exp,
                    scale=co.cv(-1.0, 1)),
                     reads=[co.s512[1], co.cvt], writes=[co.s512[1]], tag="l3_dex1")
                p.op("scalar", lambda e: e.activation(
                    co.s512[2][0:1, 0:512], co.s512[2][0:1, 0:512], AF.Exp,
                    scale=co.cv(-1.0, 1)),
                     reads=[co.s512[2], co.cvt], writes=[co.s512[2]], tag="l3_dex2")
                for u in range(2):
                    rsrc = co.s512[1] if u == 0 else co.s512[2]
                    p.op("tensor", lambda t, u=u, rsrc=rsrc: t.matmul(
                        co.PA.h[0:64, 1024 + u * 512:1024 + (u + 1) * 512],
                        co.ones[0:1, 0:64], rsrc[0:1, 0:512],
                        start=True, stop=True),
                         reads=[co.ones, rsrc], writes=[pm23], tag="l3_rbm")
                p.op("vector", lambda e: e.tensor_copy(
                    l3rb[:, 0:1024], co.PA.h[0:64, 1024:2048]),
                     reads=[pm23], writes=[l3rb], tag="l3_rbe")
                dst = attnout[h // 2 + 2 * (quad // 2)] if False else attnout[(quad * 4 + h) // 2]
                rbase = ((quad * 4 + h) % 2) * 64
                p.op("vector", lambda e, dst=dst, rbase=rbase, qof=qof: e.tensor_tensor(
                    dst[rbase:rbase + 64, qof:qof + 1024], co.PA.h[0:64, 0:1024],
                    l3rb[0:64, 0:1024], ALU.mult),
                     reads=[pm01, l3rb], writes=[dst], tag="l3_onorm")
    for c in range(DC):
        for d in range(DC):
            p.dma("sync", co.wsb[d][:, 0:128],
                  s_wo.ap()[d * 128:(d + 1) * 128, c * 128:(c + 1) * 128],
                  reads=[s_wo], writes=[co.wsb[d]], tag="l3_wol")
        for j in range(NT):
            sl = slice(j * 512, (j + 1) * 512)
            for d in range(DC):
                p.op("tensor", lambda t, d=d, sl=sl: t.matmul(
                    co.pm[0][0:128, 0:512], co.wsb[d][:, 0:128], attnout[d][:, sl],
                    start=(d == 0), stop=(d == DC - 1)),
                     reads=[co.wsb[d], attnout[d]], writes=[co.pm[0]], tag="l3_mmo")
            xt = co.x[c]
            p.op("vector", lambda e, xt=xt, sl=sl, c=c: e.scalar_tensor_tensor(
                xt[:, sl], co.pm[0][0:128, 0:512], bo[:, c:c + 1], xt[:, sl],
                ALU.add, ALU.add),
                 reads=[co.pm[0], bo, xt], writes=[xt], tag="l3_pres")



_PROG_CACHE = {}


def build_program(layers):
    nc = bass.Bass("TRN2", target_bir_lowering=False, debug=False)
    p = P(nc)
    co = Core(p)
    x_in = p.dram("x_in", [D, T], F32, kind="ExternalInput")
    x_out = p.dram("x_out", [D, T], F32, kind="ExternalOutput")
    co.ext["x_in"] = x_in
    co.load_consts()
    for d in range(DC):
        p.dma("sync", co.x[d][:, :], x_in.ap()[d * 128:(d + 1) * 128, :],
              reads=[x_in], writes=[co.x[d]], tag="xload")
    for li in layers:
        co.lazy_consts = {}
        with p.scope():
            if li == 0:
                emit_l0(p, co)
            elif li == 1:
                emit_l1(p, co)
            elif li == 2:
                emit_l2(p, co)
            elif li == 3:
                emit_l3(p, co)
            else:
                raise NotImplementedError(f"layer {li}")
    for d in range(DC):
        p.dma("sync", x_out.ap()[d * 128:(d + 1) * 128, :], co.x[d][:, :],
              reads=[co.x[d]], writes=[x_out], tag="xstore")
    names = set(co.ext.keys())
    counts = p.n_ops()
    p.emit_and_close()
    cvarr = np.zeros((128, 32), np.float32)
    for v, col in co.cvals.items():
        cvarr[:, col] = v
    return nc, names, counts, cvarr


def _get_program(layers):
    key = tuple(layers)
    if key not in _PROG_CACHE:
        _PROG_CACHE[key] = build_program(layers)
    return _PROG_CACHE[key]


def kernel(**inputs):
    x = np.asarray(inputs["L0_attn_x"])
    xT = [np.ascontiguousarray(x[b].T) for b in range(NB)]
    for group in GROUPS:
        nc, names, counts, cvarr = _get_program(group)
        in_maps = []
        for b in range(NB):
            m = {"x_in": xT[b]}
            for n in names:
                if n == "x_in":
                    continue
                if n == "c_cvals":
                    m[n] = cvarr
                elif n.startswith("c_"):
                    m[n] = CONSTS[n]
                else:
                    m[n] = np.asarray(inputs[n])
            in_maps.append(m)
        res = run_bass_kernel_spmd(nc, in_maps, core_ids=list(range(NB)))
        xT = [np.asarray(r["x_out"]) for r in res.results]
    return np.stack([t.T for t in xT], axis=0)


# version 03
# speedup vs baseline: 1.1213x; 1.1213x over previous
import contextlib

import numpy as np
import ml_dtypes

import concourse.bass as bass
import concourse.mybir as mybir
from concourse.bass_utils import run_bass_kernel_spmd

F32 = mybir.dt.float32
BF16 = mybir.dt.bfloat16
AF = mybir.ActivationFunctionType
ALU = mybir.AluOpType
AX = mybir.AxisListType

BF = ml_dtypes.bfloat16

D = 1024
T = 2048
NB = 8
DC = D // 128
NT = T // 512
TC = T // 128

ENGINES = ("tensor", "vector", "scalar", "gpsimd", "sync")
DMA_QUEUES = ("sync", "gpsimd", "scalar")
DSEM_K = 24

GROUPS = [(0, 1, 2, 3)]


class _Op:
    __slots__ = ("fn", "waits", "inc", "dma", "tag")

    def __init__(self, fn, waits, dma, tag):
        self.fn = fn
        self.waits = waits
        self.inc = False
        self.dma = dma
        self.tag = tag


class PsumBuf:
    def __init__(self, nc, handle):
        self.h = handle
        m = nc.lookup_mloc(handle)
        nbytes = int(m.dims[1]) if len(m.dims) > 1 else int(m.dims[0])
        b0 = int(m.bank)
        nbanks = max(1, (nbytes + 2047) // 2048)
        self.keys = tuple(("psum", b) for b in range(b0, b0 + nbanks))

    def __getitem__(self, key):
        return self.h[key]


class PsumView:

    def __init__(self, buf, col0, ncols, elsz=4):
        self.buf = buf
        self.col0 = col0
        self.ncols = ncols
        b0 = buf.keys[0][1]
        lo = b0 + (col0 * elsz) // 2048
        hi = b0 + ((col0 + ncols) * elsz - 1) // 2048
        self.keys = tuple(("psum", b) for b in range(lo, hi + 1))

    def __getitem__(self, key):
        rows, cols = key
        if isinstance(cols, slice):
            start = (cols.start or 0) + self.col0
            stop = (cols.stop if cols.stop is not None else self.ncols) + self.col0
            cols = slice(start, stop)
        else:
            raise TypeError(cols)
        return self.buf.h[rows, cols]


class SbView:

    def __init__(self, h, col0, ncols, nrows=128, keys=None):
        self.h = h
        self.col0 = col0
        self.ncols = ncols
        self.nrows = nrows
        self.keys = keys

    def __getitem__(self, key):
        rows, cols = key
        if isinstance(rows, slice) and rows == slice(None):
            rows = slice(0, self.nrows)
        if isinstance(cols, slice):
            start = (cols.start or 0) + self.col0
            stop = (cols.stop if cols.stop is not None else self.ncols) + self.col0
            cols = slice(start, stop)
        else:
            raise TypeError(cols)
        return self.h[rows, cols]


_PINS = []


class P:

    def __init__(self, nc):
        self.nc = nc
        self.stack = contextlib.ExitStack()
        self.cur_stack = self.stack
        self._kcache = {}
        self.stream = {e: [] for e in ENGINES}
        self.lastw = {}
        self.readers = {}
        self.dma_n = {e: 0 for e in ENGINES}
        self.waited = {e: {} for e in ENGINES}

    def sb(self, name, shape, dtype):
        return self.cur_stack.enter_context(self.nc.sbuf_tensor(name, list(shape), dtype))

    @contextlib.contextmanager
    def scope(self):
        st = contextlib.ExitStack()
        old = self.cur_stack
        self.cur_stack = st
        try:
            yield
        finally:
            st.close()
            self.cur_stack = old

    def _keys(self, objs):
        out = []
        for o in objs:
            if isinstance(o, (PsumBuf, PsumView, SbView)):
                out.extend(o.keys)
            elif isinstance(o, tuple):
                out.append(o)
            else:
                kk = self._kcache.get(id(o))
                if kk is None:
                    _PINS.append(o)
                    try:
                        space = getattr(o, "space", None)
                        is_sb = space is not None and "SB" in str(space)
                    except Exception:
                        is_sb = False
                    if is_sb:
                        m = self.nc.lookup_mloc(o)
                        addr = int(m.addr)
                        nbytes = int(m.dims[1]) if len(m.dims) > 1 else int(m.dims[0])
                        g0 = addr // 512
                        g1 = (addr + nbytes - 1) // 512
                        kk = tuple(("sb", g) for g in range(g0, g1 + 1))
                    else:
                        kk = (("t", id(o)),)
                    self._kcache[id(o)] = kk
                out.extend(kk)
        return out

    def sbpool(self, name, ncols, dtype):
        h = self.sb(name, [128, ncols], dtype)
        elsz = 4 if dtype == F32 else 2
        m = self.nc.lookup_mloc(h)
        addr = int(m.addr)
        return h, addr, elsz

    def view(self, pool, col0, ncols, nrows=128):
        h, addr, elsz = pool
        g0 = (addr + col0 * elsz) // 512
        g1 = (addr + (col0 + ncols) * elsz - 1) // 512
        keys = tuple(("sb", g) for g in range(g0, g1 + 1))
        return SbView(h, col0, ncols, nrows, keys)

    def ps(self, name, shape, dtype=F32):
        h = self.stack.enter_context(self.nc.psum_tensor(name, list(shape), dtype))
        return PsumBuf(self.nc, h)

    def dram(self, name, shape, dtype, kind="Internal"):
        return self.nc.dram_tensor(name, list(shape), dtype, kind=kind)

    def _deps(self, reads, writes):
        deps = []
        for b in reads:
            t = self.lastw.get(b)
            if t is not None:
                deps.append(t)
        for b in writes:
            deps.extend(self.readers.get(b, ()))
            t = self.lastw.get(b)
            if t is not None:
                deps.append(t)
        return deps

    def _resolve(self, eng, deps):
        out = []
        w = self.waited[eng]
        for t in deps:
            if t[0] == "E":
                _, peng, idx = t
                if peng == eng and peng == "tensor":
                    continue
                k = ("E", peng)
                if w.get(k, -1) >= idx:
                    continue
                w[k] = idx
                out.append(t)
            else:
                _, qeng, cnt = t
                slot = (cnt - 1) % DSEM_K
                use = (cnt - 1) // DSEM_K + 1
                k = ("D", qeng, slot)
                if w.get(k, 0) >= use:
                    continue
                w[k] = use
                out.append(t)
        return out

    def _record(self, eng, fn, reads, writes, dma, tag):
        reads = self._keys(reads)
        writes = self._keys(writes)
        deps = self._deps(reads, writes)
        waits = self._resolve(eng, deps)
        for t in waits:
            if t[0] == "E":
                self.stream[t[1]][t[2]].inc = True
        if dma:
            self.dma_n[eng] += 1
            tok = ("D", eng, self.dma_n[eng])
        else:
            tok = ("E", eng, len(self.stream[eng]))
        self.stream[eng].append(_Op(fn, waits, dma, tag))
        for b in reads:
            self.readers.setdefault(b, []).append(tok)
        for b in writes:
            self.lastw[b] = tok
            self.readers[b] = []
        return tok

    def op(self, eng, fn, reads=(), writes=(), tag=""):
        return self._record(eng, fn, reads, writes, False, tag)

    def dma(self, qeng, out_ap, in_ap, reads=(), writes=(), tag="", transpose=False):
        if transpose:
            fn = lambda e: e.dma_start_transpose(out_ap, in_ap)
        else:
            fn = lambda e: e.dma_start(out=out_ap, in_=in_ap)
        return self._record(qeng, fn, reads, writes, True, tag)

    def emit_and_close(self):
        nc = self.nc
        sems = {e: self.stack.enter_context(nc.semaphore("sem_" + e)) for e in ENGINES}
        dsems = {q: [self.stack.enter_context(nc.semaphore(f"dsem_{q}{i}"))
                     for i in range(DSEM_K)] for q in DMA_QUEUES}
        rank = {}
        for e in ENGINES:
            c = 0
            r = []
            for o in self.stream[e]:
                if o.inc and not o.dma:
                    c += 1
                r.append(c)
            rank[e] = r

        def make_body(e):
            def body(eng):
                nd = 0
                for o in self.stream[e]:
                    for t in o.waits:
                        if t[0] == "E":
                            eng.wait_ge(sems[t[1]], rank[t[1]][t[2]])
                        else:
                            _, q, cnt = t
                            slot = (cnt - 1) % DSEM_K
                            use = (cnt - 1) // DSEM_K + 1
                            eng.wait_ge(dsems[q][slot], 16 * use)
                    if o.dma:
                        slot = nd % DSEM_K
                        use = nd // DSEM_K + 1
                        if use > 1:
                            eng.wait_ge(dsems[e][slot], 16 * (use - 1))
                        ins = o.fn(eng)
                        ins.then_inc(dsems[e][slot], 16)
                        nd += 1
                    else:
                        ins = o.fn(eng)
                        if o.inc:
                            ins.then_inc(sems[e], 1)
                if e == "sync":
                    for q in DMA_QUEUES:
                        n = self.dma_n[q]
                        for slot in range(min(n, DSEM_K)):
                            last_use = (n - 1 - slot) // DSEM_K + 1
                            eng.wait_ge(dsems[q][slot], 16 * last_use)
            return body

        with nc.Block() as block:
            block.tensor(make_body("tensor"))
            block.vector(make_body("vector"))
            block.scalar(make_body("scalar"))
            block.gpsimd(make_body("gpsimd"))
            block.sync(make_body("sync"))
        self.stack.close()

    def n_ops(self):
        d = {e: len(self.stream[e]) for e in ENGINES}
        d["total"] = sum(d.values())
        return d



def host_constants():
    c = {}
    c["c_ones"] = np.ones((128, 128), np.float32)
    c["c_ones_bf"] = np.ones((128, 256), BF)
    c["c_ident_bf"] = np.eye(128, dtype=BF)
    c["c_ident_f32"] = np.eye(128, dtype=np.float32)
    tsel = np.zeros((4, 512), np.float32)
    for k in range(4):
        tsel[k, k * 128:(k + 1) * 128] = 1.0
    c["c_tsel"] = tsel
    c["c_zero16"] = np.zeros((1, 16), np.float32)
    c["c_zrow"] = np.zeros((1, 512), BF)
    offs = [0] + [1 << k for k in range(11)]
    sa = np.zeros((128, 12 * 24), BF)
    for o in range(12):
        for pp in range(128):
            sa[pp, 24 * o + (pp // 64) * 12 + o] = 1.0
    c["c_l2sa"] = sa
    lm = np.zeros((24, 2048), np.float32)
    for h in range(2):
        for o in range(12):
            lm[h * 12 + o, 0:offs[o]] = -1e30
    c["c_l2mask"] = lm.astype(BF)
    pb = np.zeros((24, 12 * 128), BF)
    for o in range(12):
        for j in range(128):
            pb[(j // 64) * 12 + o, 128 * o + j] = 1.0
    c["c_l2pb"] = pb
    dn = np.zeros((24, 2), BF)
    dn[0:12, 0] = 1.0
    dn[12:24, 1] = 1.0
    c["c_l2den"] = dn
    rb2 = np.zeros((2, 128), BF)
    rb2[0, 0:64] = 1.0
    rb2[1, 64:128] = 1.0
    c["c_l2rb"] = rb2
    hs = np.zeros((128, 2), np.float32)
    hs[0:64, 0] = 1.0
    hs[64:128, 1] = 1.0
    c["c_l3hs"] = hs
    GRID, DECAY = 12, 0.3
    i = np.arange(T)
    xi = (i // GRID) % GRID
    yi = i % GRID
    dxa = np.abs(xi[:, None] - xi[None, :])
    dya = np.abs(yi[:, None] - yi[None, :])
    dx = np.minimum(dxa, GRID - dxa).astype(np.float32)
    dy = np.minimum(dya, GRID - dya).astype(np.float32)
    dist = np.sqrt(dx * dx + dy * dy)
    mask = np.exp(-DECAY * dist) * np.tril(np.ones((T, T), np.float32))
    c["c_l3expb"] = (mask + 1e-10).T.astype(BF)
    esel = np.zeros((8, 1024), BF)
    for e in range(8):
        esel[e, e * 128:(e + 1) * 128] = 1.0
    c["c_esel"] = esel
    sel16 = np.zeros((16, 2048), BF)
    for n in range(16):
        sel16[n, n * 128:(n + 1) * 128] = 1.0
    c["c_sel16"] = sel16
    u = np.arange(896)[None, :]
    pp = np.arange(128)[:, None]
    c["c_l0mask"] = (u >= pp + 384).astype(BF)
    return c


CONSTS = host_constants()



class Core:
    def __init__(self, p):
        self.p = p
        self.x = [p.sb(f"x{i}", [128, T], F32) for i in range(DC)]
        self.POOL1 = p.sbpool("POOL1", DC * T, BF16)
        self.POOL2 = p.sbpool("POOL2", DC * T, BF16)
        self.POOL3 = p.sbpool("POOL3", DC * T, BF16)
        self.pb1 = [p.view(self.POOL1, T * i, T) for i in range(DC)]
        self.pb2 = [p.view(self.POOL2, T * i, T) for i in range(DC)]
        self.pb3 = [p.view(self.POOL3, T * i, T) for i in range(DC)]
        self.s512 = [p.sb(f"s512_{i}", [128, 512], F32) for i in range(4)]
        self.colA = p.sb("colA", [128, TC], F32)
        self.colB = p.sb("colB", [128, TC], F32)
        self.colC = p.sb("colC", [128, TC], F32)
        self.sr4 = p.sb("sr4", [4, 128], F32)
        self.ones = p.sb("ones", [128, 128], F32)
        self.ones_bf = p.sb("ones_bf", [128, 256], BF16)
        self.ident_bf = p.sb("ident_bf", [128, 128], BF16)
        self.ident_f32 = p.sb("ident_f32", [128, 128], F32)
        self.tsel = p.sb("tsel", [4, 512], F32)
        self.WPOOL = p.sbpool("WPOOL", 8 * 512, BF16)
        self.WPOOL2 = p.sbpool("WPOOL2", 8 * 512, BF16)
        self.wtb = [p.view(self.WPOOL, 512 * i, 512) for i in range(8)]
        self.wtb2 = [p.view(self.WPOOL2, 512 * i, 512) for i in range(8)]
        self.wsb = [p.view(self.WPOOL, 512 * i, 256) for i in range(8)]
        self.wsb2 = [p.view(self.WPOOL2, 512 * i, 256) for i in range(8)]
        self.WS = (self.wsb, self.wsb2)
        self.WT = (self.wtb, self.wtb2)
        self.cvt = p.sb("cvt", [128, 16], F32)
        self.zrow = p.sb("zrow", [1, 512], BF16)
        self.cvals = {}
        self.lazy_consts = {}
        self.PA = p.ps("PA", [128, 2048])
        self.PB = p.ps("PB", [128, 1536])
        self.PT = p.ps("PT", [128, 512], BF16)
        self.pm = [PsumView(self.PA, 512 * i, 512) for i in range(4)]
        self.pm.append(PsumView(self.PB, 0, 512))
        self.paux = PsumView(self.PB, 512, 512)
        self.prow = PsumView(self.PB, 1024, 512)
        self.ptb = PsumView(self.PT, 0, 512, elsz=2)
        self.ext = {}

    def cv(self, v, n=128):
        v = float(v)
        if v not in self.cvals:
            assert len(self.cvals) < 16
            self.cvals[v] = len(self.cvals)
        return self.cvt[0:n, self.cvals[v]:self.cvals[v] + 1]

    def ein(self, name, shape, dtype=F32):
        if name not in self.ext:
            self.ext[name] = self.p.dram(name, list(shape), dtype, kind="ExternalInput")
        return self.ext[name]

    def load_consts(self):
        p = self.p
        pairs = [("c_ones", self.ones), ("c_ones_bf", self.ones_bf),
                 ("c_ident_bf", self.ident_bf), ("c_ident_f32", self.ident_f32),
                 ("c_tsel", self.tsel)]
        for name, tile in pairs:
            arr = CONSTS[name]
            d = self.ein(name, list(arr.shape),
                         F32 if arr.dtype == np.float32 else BF16)
            p.dma("sync", tile[:, :], d.ap(), reads=[d], writes=[tile], tag="const")
        d = self.ein("c_cvals", [128, 16], F32)
        p.dma("sync", self.cvt[:, :], d.ap(), reads=[d], writes=[self.cvt], tag="cvals")
        dz = self.ein("c_zrow", [1, 512], BF16)
        p.dma("sync", self.zrow[:, :], dz.ap(), reads=[dz], writes=[self.zrow], tag="zrow")

    def cload(self, name):
        if name in self.lazy_consts:
            return self.lazy_consts[name]
        arr = CONSTS[name]
        dt_ = F32 if arr.dtype == np.float32 else BF16
        d = self.ein(name, list(arr.shape), dt_)
        self._cload_n = getattr(self, "_cload_n", 0) + 1
        t = self.p.sb(f"sb{self._cload_n}" + name, list(arr.shape), dt_)
        self.p.dma("sync", t[:, :], d.ap(), reads=[d], writes=[t], tag="lconst")
        self.lazy_consts[name] = t
        return t


def vec_cols(p, co, in_name, n, name):
    d = co.ein(in_name, [n * 128], F32)
    t = p.sb(name, [128, n], F32)
    for c in range(n):
        p.dma("sync", t[:, c:c + 1],
              d.ap()[c * 128:(c + 1) * 128].rearrange("(p o) -> p o", o=1),
              reads=[d], writes=[t], tag="veccol")
    return t


def stage_bf16(p, co, in_name, shape, sname=None):
    src = co.ein(in_name, list(shape), F32)
    dst = p.dram((sname or ("s_" + in_name)), list(shape), BF16, kind="Internal")
    p.dma("gpsimd", dst.ap(), src.ap(), reads=[src], writes=[dst], tag="stage")
    return dst



def psum_zero(p, co, psum, ncols, tag="pz"):
    p.op("tensor", lambda t: t.matmul(psum[0:128, 0:ncols], co.ones_bf[0:1, 0:128],
                                      co.zrow[:, 0:ncols], start=True, stop=True),
         reads=[co.ones_bf, co.zrow], writes=[psum], tag=tag)


def emit_stats(p, co, src_tiles, src_bf16):
    ones_x = co.ones_bf[:, 0:1] if src_bf16 else co.ones[:, 0:1]
    ones_x_buf = co.ones_bf if src_bf16 else co.ones
    psum_zero(p, co, co.pm[0], 16, "st_z1")
    psum_zero(p, co, co.pm[1], 16, "st_z2")
    for j in range(NT):
        for d in range(DC):
            xt = src_tiles[d]
            sq = co.s512[d % 2]
            p.op("scalar", lambda e, xt=xt, sq=sq, j=j: e.activation(
                sq[:, :], xt[:, j * 512:(j + 1) * 512], AF.Square),
                 reads=[xt], writes=[sq], tag="st_sq")
            for u in range(4):
                tc = 4 * j + u
                p.op("tensor", lambda t, xt=xt, tc=tc: t.matmul(
                    co.pm[0][0:128, tc:tc + 1], xt[:, tc * 128:(tc + 1) * 128],
                    ones_x, start=False, stop=False),
                     reads=[ones_x_buf, xt], writes=[co.pm[0]], tag="st_sum")
                p.op("tensor", lambda t, sq=sq, tc=tc, u=u: t.matmul(
                    co.pm[1][0:128, tc:tc + 1], sq[:, u * 128:(u + 1) * 128],
                    co.ones[:, 0:1], start=False, stop=False),
                     reads=[co.ones, sq], writes=[co.pm[1]], tag="st_sumsq")
    p.op("vector", lambda e: e.tensor_scalar_mul(co.colA[:, :], co.pm[0][0:128, 0:16],
                                                 1.0 / D),
         reads=[co.pm[0]], writes=[co.colA], tag="st_mu")
    p.op("vector", lambda e: e.tensor_scalar_mul(co.colB[:, :], co.pm[1][0:128, 0:16],
                                                 1.0 / D),
         reads=[co.pm[1]], writes=[co.colB], tag="st_msq")
    p.op("vector", lambda e: e.tensor_tensor(co.colC[:, :], co.colA[:, :],
                                             co.colA[:, :], ALU.mult),
         reads=[co.colA], writes=[co.colC], tag="st_mu2")
    p.op("vector", lambda e: e.tensor_sub(co.colC[:, :], co.colB[:, :], co.colC[:, :]),
         reads=[co.colB, co.colC], writes=[co.colC], tag="st_var")
    p.op("scalar", lambda e: e.activation(co.colC[:, :], co.colC[:, :], AF.Ln,
                                          bias=co.cv(1e-5)),
         reads=[co.colC, co.cvt], writes=[co.colC], tag="st_ln")
    p.op("scalar", lambda e: e.activation(co.colC[:, :], co.colC[:, :], AF.Exp,
                                          scale=co.cv(-0.5)),
         reads=[co.colC, co.cvt], writes=[co.colC], tag="st_r")
    p.op("vector", lambda e: e.tensor_tensor(co.colA[:, :], co.colA[:, :],
                                             co.colC[:, :], ALU.mult),
         reads=[co.colA, co.colC], writes=[co.colA], tag="st_mr")


def col_broadcast(p, co, col_ap, col_buf, out_tile, w=512, tag="cb"):
    nchunks = w // 128
    p.op("tensor", lambda t: t.transpose(
        co.paux[0:nchunks, 0:128], col_ap, co.ident_f32[:, :]),
         reads=[col_buf, co.ident_f32], writes=[co.paux], tag=tag + "_tr")
    p.op("vector", lambda e: e.tensor_copy(co.sr4[0:nchunks, :],
                                           co.paux[0:nchunks, 0:128]),
         reads=[co.paux], writes=[co.sr4], tag=tag + "_ev")
    for c in range(nchunks):
        p.op("tensor", lambda t, c=c: t.matmul(
            co.paux[0:128, c * 128:(c + 1) * 128],
            co.tsel[0:nchunks, c * 128:(c + 1) * 128],
            co.sr4[0:nchunks, 0:128], start=True, stop=True),
             reads=[co.tsel, co.sr4], writes=[co.paux], tag=tag + "_mm")
    p.op("vector", lambda e: e.tensor_copy(out_tile[:, 0:w], co.paux[0:128, 0:w]),
         reads=[co.paux], writes=[out_tile], tag=tag + "_cp")


def emit_normalize(p, co, src_tiles, out_tiles, affine=None):
    for j in range(NT):
        sl = slice(j * 512, (j + 1) * 512)
        rb, mb, tmp = co.s512[3], co.s512[0], co.s512[2]
        col_broadcast(p, co, co.colC[:, 4 * j:4 * j + 4], co.colC, rb, tag="nm_r")
        col_broadcast(p, co, co.colA[:, 4 * j:4 * j + 4], co.colA, mb, tag="nm_m")
        for d in range(DC):
            xt, ot = src_tiles[d], out_tiles[d]
            p.op("vector", lambda e, xt=xt, sl=sl, tmp=tmp, rb=rb: e.tensor_tensor(
                tmp[:, :], xt[:, sl], rb[:, :], ALU.mult),
                 reads=[xt, rb], writes=[tmp], tag="nm_n1")
            if affine is None:
                p.op("vector", lambda e, tmp=tmp, ot=ot, sl=sl, mb=mb: e.tensor_sub(
                    ot[:, sl], tmp[:, :], mb[:, :]),
                     reads=[tmp, mb], writes=[ot], tag="nm_n2")
            else:
                g, b = affine
                p.op("vector", lambda e, tmp=tmp, mb=mb: e.tensor_sub(
                    tmp[:, :], tmp[:, :], mb[:, :]),
                     reads=[tmp, mb], writes=[tmp], tag="nm_n2")
                p.op("vector", lambda e, tmp=tmp, ot=ot, sl=sl, d=d: e.tensor_scalar(
                    ot[:, sl], tmp[:, :], g[:, d:d + 1], b[:, d:d + 1],
                    ALU.mult, ALU.add),
                     reads=[tmp, g, b], writes=[ot], tag="nm_aff")



def emit_l0(p, co):
    co.l0mask = co.cload("c_l0mask")
    s_qkv = stage_bf16(p, co, "L0_attn_qkv_w", [D, 3 * D])
    s_gate = stage_bf16(p, co, "L0_attn_gate_w", [D, D])
    s_router = stage_bf16(p, co, "L0_ffn_w_router", [D, 8])
    s_w1 = stage_bf16(p, co, "L0_ffn_w1", [8, D, 4096])
    s_w2 = stage_bf16(p, co, "L0_ffn_w2", [8, 2048, D])

    emit_stats(p, co, co.x, False)
    emit_normalize(p, co, co.x, co.pb1)
    xn = co.pb1

    emit_l0_ffn(p, co, xn, s_router, s_w1, s_w2)

    ln_g = vec_cols(p, co, "L0_attn_ln_g", DC, "l0_lng")
    ln_b = vec_cols(p, co, "L0_attn_ln_b", DC, "l0_lnb")
    qkv_b = vec_cols(p, co, "L0_attn_qkv_b", 3 * DC, "l0_qkvb")
    gate_b = vec_cols(p, co, "L0_attn_gate_b", DC, "l0_gateb")

    emit_stats(p, co, xn, True)
    emit_normalize(p, co, xn, co.pb2, affine=(ln_g, ln_b))
    xn2 = co.pb2

    kT = co.pb1
    for c in range(DC):
        ws = co.WS[c % 2]
        for d in range(DC):
            p.dma("sync", ws[d][:, 0:128],
                  s_gate.ap()[d * 128:(d + 1) * 128, c * 128:(c + 1) * 128],
                  reads=[s_gate], writes=[ws[d]], tag="l0_wgl")
            p.dma("sync", ws[d][:, 128:256],
                  s_qkv.ap()[d * 128:(d + 1) * 128, D + c * 128:D + (c + 1) * 128],
                  reads=[s_qkv], writes=[ws[d]], tag="l0_wkl")
        for j in range(NT):
            sl = slice(j * 512, (j + 1) * 512)
            for d in range(DC):
                p.op("tensor", lambda t, d=d, sl=sl, ws=ws: t.matmul(
                    co.pm[0][0:128, 0:512], ws[d][:, 0:128], xn2[d][:, sl],
                    start=(d == 0), stop=(d == DC - 1)),
                     reads=[ws[d], xn2[d]], writes=[co.pm[0]], tag="l0_mmg")
            for d in range(DC):
                p.op("tensor", lambda t, d=d, sl=sl, ws=ws: t.matmul(
                    co.pm[1][0:128, 0:512], ws[d][:, 128:256], xn2[d][:, sl],
                    start=(d == 0), stop=(d == DC - 1)),
                     reads=[ws[d], xn2[d]], writes=[co.pm[1]], tag="l0_mmk")
            sg, k1, km = co.s512[0], co.s512[1], co.s512[2]
            p.op("scalar", lambda e, c=c, sg=sg: e.activation(
                sg[:, :], co.pm[0][0:128, 0:512], AF.Sigmoid, bias=gate_b[:, c:c + 1]),
                 reads=[co.pm[0], gate_b], writes=[sg], tag="l0_sig")
            p.op("vector", lambda e, c=c, k1=k1, sg=sg: e.scalar_tensor_tensor(
                k1[:, :], co.pm[1][0:128, 0:512], qkv_b[:, DC + c:DC + c + 1], sg[:, :],
                ALU.add, ALU.mult),
                 reads=[co.pm[1], qkv_b, sg], writes=[k1], tag="l0_k1")
            p.op("vector", lambda e, k1=k1, km=km: e.tensor_scalar_min(km[:, :], k1[:, :], 0.0),
                 reads=[k1], writes=[km], tag="l0_kmin")
            p.op("scalar", lambda e, km=km: e.activation(km[:, :], km[:, :], AF.Exp),
                 reads=[km], writes=[km], tag="l0_kexp")
            p.op("vector", lambda e, k1=k1: e.tensor_scalar_max(k1[:, :], k1[:, :], 0.0),
                 reads=[k1], writes=[k1], tag="l0_krelu")
            p.op("vector", lambda e, c=c, sl=sl, k1=k1, km=km: e.tensor_add(
                kT[c][:, sl], km[:, :], k1[:, :]),
                 reads=[km, k1], writes=[kT[c]], tag="l0_kout")

    vtok = co.pb3
    bv = p.sb("l0_bv", [1, 1024], BF16)
    p.dma("gpsimd", bv[:, :],
          co.ext["L0_attn_qkv_b"].ap()[2 * D:3 * D].rearrange("(o n) -> o n", o=1),
          reads=[co.ext["L0_attn_qkv_b"]], writes=[bv], tag="l0_bvf")
    for g in range(4):
        gsl = slice(2 * D + g * 256, 2 * D + (g + 1) * 256)
        ws = co.WS[g % 2]
        for d in range(DC):
            p.dma("sync", ws[d][:, :],
                  s_qkv.ap()[d * 128:(d + 1) * 128, gsl],
                  reads=[s_qkv], writes=[ws[d]], tag="l0_wvl")
        for s in range(TC):
            ssl = slice(s * 128, (s + 1) * 128)
            for d in range(DC):
                p.op("tensor", lambda t, d=d, ssl=ssl, ws=ws: t.matmul(
                    co.pm[2][0:128, 0:256], xn2[d][:, ssl], ws[d][:, :],
                    start=(d == 0), stop=False),
                     reads=[xn2[d], ws[d]], writes=[co.pm[2]], tag="l0_mmv")
            p.op("tensor", lambda t, g=g: t.matmul(
                co.pm[2][0:128, 0:256], co.ones_bf[0:1, 0:128],
                bv[:, g * 256:(g + 1) * 256], start=False, stop=True),
                 reads=[co.ones_bf, bv], writes=[co.pm[2]], tag="l0_mmvb")
            dst = vtok[s // 2]
            off = (s % 2) * 1024 + g * 256
            p.op("vector", lambda e, dst=dst, off=off: e.tensor_copy(
                dst[:, off:off + 256], co.pm[2][0:128, 0:256]),
                 reads=[co.pm[2]], writes=[dst], tag="l0_vev")

    W = 256
    NJ = T // W
    qsb = [p.view(co.WPOOL, 512 * d + 256, 256) for d in range(DC)]
    esb = [p.sb(f"l0_e{i}", [128, W], BF16) for i in range(8)] + \
          [p.view(co.WPOOL2, 512 * i + 256, 256) for i in range(8)]
    kcs = p.sb("l0_kcs", [128, DC], F32)
    p.op("vector", lambda e: e.memset(kcs[:, :], 0.0), writes=[kcs], tag="l0_kcs0")
    rden_b = co.s512[3]
    for j in range(NJ):
        sl = slice(j * W, (j + 1) * W)
        for c in range(DC):
            ws = co.WS[c % 2]
            for d in range(DC):
                p.dma("sync", ws[d][:, 0:128],
                      s_qkv.ap()[d * 128:(d + 1) * 128, c * 128:(c + 1) * 128],
                      reads=[s_qkv], writes=[ws[d]], tag="l0_wql")
            for d in range(DC):
                p.op("tensor", lambda t, d=d, sl=sl, ws=ws: t.matmul(
                    co.pm[3][0:128, 0:W], ws[d][:, 0:128],
                    xn2[d][:, sl], start=(d == 0), stop=(d == DC - 1)),
                     reads=[ws[d], xn2[d]], writes=[co.pm[3]], tag="l0_mmq")
            q1, qm = co.s512[0], co.s512[1]
            p.op("vector", lambda e, c=c, q1=q1: e.tensor_scalar_add(
                q1[:, 0:W], co.pm[3][0:128, 0:W], qkv_b[:, c:c + 1]),
                 reads=[co.pm[3], qkv_b], writes=[q1], tag="l0_qb")
            p.op("vector", lambda e, q1=q1, qm=qm: e.tensor_scalar_min(
                qm[:, 0:W], q1[:, 0:W], 0.0), reads=[q1], writes=[qm], tag="l0_qmin")
            p.op("scalar", lambda e, qm=qm: e.activation(qm[:, 0:W], qm[:, 0:W], AF.Exp),
                 reads=[qm], writes=[qm], tag="l0_qexp")
            p.op("vector", lambda e, q1=q1: e.tensor_scalar_max(q1[:, 0:W], q1[:, 0:W], 0.0),
                 reads=[q1], writes=[q1], tag="l0_qrelu")
            p.op("vector", lambda e, c=c, q1=q1, qm=qm: e.tensor_add(
                qsb[c][:, :], qm[:, 0:W], q1[:, 0:W]),
                 reads=[qm, q1], writes=[qsb[c]], tag="l0_qout")
        psum_zero(p, co, co.pm[4], 2, "l0_dz")
        for d in range(DC):
            kc = co.s512[2]
            p.op("vector", lambda e, d=d, sl=sl, kc=kc: e.tensor_tensor_scan(
                kc[:, 0:W], co.ones_bf[:, 0:W], kT[d][:, sl], kcs[:, d:d + 1],
                ALU.mult, ALU.add),
                 reads=[co.ones_bf, kT[d], kcs], writes=[kc], tag="l0_kscan")
            p.op("vector", lambda e, d=d, kc=kc: e.tensor_copy(
                kcs[:, d:d + 1], kc[:, W - 1:W]),
                 reads=[kc], writes=[kcs], tag="l0_kst")
            p.op("vector", lambda e, d=d, kc=kc: e.tensor_tensor(
                kc[:, 0:W], kc[:, 0:W], qsb[d][:, :], ALU.mult),
                 reads=[kc, qsb[d]], writes=[kc], tag="l0_kq")
            for h in range(2):
                p.op("tensor", lambda t, kc=kc, h=h: t.matmul(
                    co.pm[4][0:128, h:h + 1], kc[:, h * 128:(h + 1) * 128],
                    co.ones[:, 0:1], start=False, stop=False),
                     reads=[co.ones, kc], writes=[co.pm[4]], tag="l0_mmden")
        dcol = co.colB
        p.op("scalar", lambda e: e.activation(
            dcol[:, 0:2], co.pm[4][0:128, 0:2], AF.Ln, bias=co.cv(1e-6)),
             reads=[co.pm[4], co.cvt], writes=[dcol], tag="l0_dln")
        p.op("scalar", lambda e: e.activation(
            dcol[:, 0:2], dcol[:, 0:2], AF.Exp, scale=co.cv(-1.0)),
             reads=[dcol, co.cvt], writes=[dcol], tag="l0_dexp")
        col_broadcast(p, co, dcol[:, 0:2], dcol, rden_b, w=W, tag="l0_db")
        ns = 2 * j + 2
        for i in range(ns):
            isl = slice(i * 128, (i + 1) * 128)
            for d in range(DC):
                p.op("tensor", lambda t, d=d, isl=isl: t.matmul(
                    co.pm[4][0:128, 0:W], kT[d][:, isl], qsb[d][:, :],
                    start=(d == 0), stop=(d == DC - 1)),
                     reads=[kT[d], qsb[d]], writes=[co.pm[4]], tag="l0_mma")
            if i >= 2 * j:
                ko = 384 - 128 * (i - 2 * j)
                p.op("vector", lambda e, i=i, ko=ko: e.tensor_tensor(
                    esb[i][:, :], co.pm[4][0:128, 0:W],
                    co.l0mask[:, ko:ko + W], ALU.mult),
                     reads=[co.pm[4], co.l0mask], writes=[esb[i]], tag="l0_emask")
            else:
                p.op("vector", lambda e, i=i: e.tensor_copy(
                    esb[i][:, :], co.pm[4][0:128, 0:W]),
                     reads=[co.pm[4]], writes=[esb[i]], tag="l0_ecp")
        for c in range(DC):
            for i in range(ns):
                vt = vtok[i // 2]
                voff = (i % 2) * 1024 + c * 128
                p.op("tensor", lambda t, vt=vt, voff=voff, i=i, ns=ns: t.matmul(
                    co.pm[0][0:128, 0:W], vt[:, voff:voff + 128], esb[i][:, :],
                    start=(i == 0), stop=(i == ns - 1)),
                     reads=[vt, esb[i]], writes=[co.pm[0]], tag="l0_mmo")
            tmp = co.s512[2]
            p.op("vector", lambda e, tmp=tmp: e.tensor_tensor(
                tmp[:, 0:W], co.pm[0][0:128, 0:W], rden_b[:, 0:W], ALU.mult),
                 reads=[co.pm[0], rden_b], writes=[tmp], tag="l0_onorm")
            xt = co.x[c]
            p.op("vector", lambda e, xt=xt, sl=sl, tmp=tmp: e.tensor_add(
                xt[:, sl], xt[:, sl], tmp[:, 0:W]),
                 reads=[xt, tmp], writes=[xt], tag="l0_ores")


def emit_l0_ffn(p, co, xn, s_router, s_w1, s_w2):
    esel = co.cload("c_esel")
    wr = p.sb("l0_wr", [128, 64], BF16)
    for d in range(DC):
        p.dma("sync", wr[:, d * 8:(d + 1) * 8],
              s_router.ap()[d * 128:(d + 1) * 128, 0:8],
              reads=[s_router], writes=[wr], tag="l0_wrl")
    GT = p.sb("l0_GT", [8, T], BF16)
    rtp = co.s512[0]
    for tc in range(TC):
        tsl = slice(tc * 128, (tc + 1) * 128)
        for d in range(DC):
            p.op("tensor", lambda t, d=d, tsl=tsl: t.matmul(
                co.pm[0][0:128, 0:8], xn[d][:, tsl], wr[:, d * 8:(d + 1) * 8],
                start=(d == 0), stop=(d == DC - 1)),
                 reads=[xn[d], wr], writes=[co.pm[0]], tag="l0_mmr")
        p.op("vector", lambda e: e.tensor_copy(rtp[:, 0:8], co.pm[0][0:128, 0:8]),
             reads=[co.pm[0]], writes=[rtp], tag="l0_rt")
        p.op("vector", lambda e: e.reduce_max(rtp[:, 16:17], rtp[:, 0:8], AX.X),
             reads=[rtp], writes=[rtp], tag="l0_m1")
        p.op("vector", lambda e: e.tensor_scalar(
            rtp[:, 24:32], rtp[:, 0:8], rtp[:, 16:17], None, ALU.is_equal),
             reads=[rtp], writes=[rtp], tag="l0_eq1")
        p.op("vector", lambda e: e.scalar_tensor_tensor(
            rtp[:, 8:16], rtp[:, 24:32], -1e30, rtp[:, 0:8], ALU.mult, ALU.add),
             reads=[rtp], writes=[rtp], tag="l0_mask1")
        p.op("vector", lambda e: e.reduce_max(rtp[:, 17:18], rtp[:, 8:16], AX.X),
             reads=[rtp], writes=[rtp], tag="l0_m2")
        p.op("vector", lambda e: e.tensor_scalar(
            rtp[:, 32:40], rtp[:, 8:16], rtp[:, 17:18], None, ALU.is_equal),
             reads=[rtp], writes=[rtp], tag="l0_eq2")
        p.op("vector", lambda e: e.tensor_sub(rtp[:, 18:19], rtp[:, 16:17], rtp[:, 17:18]),
             reads=[rtp], writes=[rtp], tag="l0_dm")
        p.op("scalar", lambda e: e.activation(rtp[:, 18:19], rtp[:, 18:19], AF.Sigmoid),
             reads=[rtp], writes=[rtp], tag="l0_g1")
        p.op("vector", lambda e: e.tensor_scalar(
            rtp[:, 19:20], rtp[:, 18:19], -1.0, 1.0, ALU.mult, ALU.add),
             reads=[rtp], writes=[rtp], tag="l0_g2")
        p.op("vector", lambda e: e.tensor_scalar(
            rtp[:, 24:32], rtp[:, 24:32], rtp[:, 18:19], None, ALU.mult),
             reads=[rtp], writes=[rtp], tag="l0_p1")
        p.op("vector", lambda e: e.scalar_tensor_tensor(
            rtp[:, 24:32], rtp[:, 32:40], rtp[:, 19:20], rtp[:, 24:32],
            ALU.mult, ALU.add),
             reads=[rtp], writes=[rtp], tag="l0_p2")
        gtile = p.sb(f"l0_g{tc}", [128, 8], BF16)
        p.op("vector", lambda e, gtile=gtile: e.tensor_copy(gtile[:, :], rtp[:, 24:32]),
             reads=[rtp], writes=[gtile], tag="l0_gcast")
        p.op("tensor", lambda t, gtile=gtile: t.transpose(
            co.ptb[0:8, 0:128], gtile[:, :], co.ident_bf[:, :]),
             reads=[gtile, co.ident_bf], writes=[co.ptb], tag="l0_gtr")
        p.op("vector", lambda e, tsl=tsl: e.tensor_copy(GT[0:8, tsl], co.ptb[0:8, 0:128]),
             reads=[co.ptb], writes=[GT], tag="l0_gtc")

    actbuf = co.pb3
    w2sb = co.pb2
    gbt = co.s512[3]
    for e_ in range(8):
        for i in range(16):
            p.dma("sync", w2sb[i // 2][:, (i % 2) * 1024:(i % 2) * 1024 + 1024],
                  s_w2.ap()[e_, i * 128:(i + 1) * 128, 0:D],
                  reads=[s_w2], writes=[w2sb[i // 2]], tag="l0_w2l")
        for th in range(2):
            for hc in range(16):
                ws = co.WS[hc % 2]
                for d in range(DC):
                    p.dma("sync", ws[d][:, 0:128],
                          s_w1.ap()[e_, d * 128:(d + 1) * 128, hc * 128:(hc + 1) * 128],
                          reads=[s_w1], writes=[ws[d]], tag="l0_w1la")
                    p.dma("sync", ws[d][:, 128:256],
                          s_w1.ap()[e_, d * 128:(d + 1) * 128,
                                    2048 + hc * 128:2048 + (hc + 1) * 128],
                          reads=[s_w1], writes=[ws[d]], tag="l0_w1lg")
                for tj in range(2):
                    j = th * 2 + tj
                    sl = slice(j * 512, (j + 1) * 512)
                    for d in range(DC):
                        p.op("tensor", lambda t, d=d, sl=sl, ws=ws: t.matmul(
                            co.pm[0][0:128, 0:512], ws[d][:, 0:128], xn[d][:, sl],
                            start=(d == 0), stop=(d == DC - 1)),
                             reads=[ws[d], xn[d]], writes=[co.pm[0]], tag="l0_mme1")
                    for d in range(DC):
                        p.op("tensor", lambda t, d=d, sl=sl, ws=ws: t.matmul(
                            co.pm[1][0:128, 0:512], ws[d][:, 128:256], xn[d][:, sl],
                            start=(d == 0), stop=(d == DC - 1)),
                             reads=[ws[d], xn[d]], writes=[co.pm[1]], tag="l0_mme2")
                    sg = co.s512[1]
                    p.op("scalar", lambda e, sg=sg: e.activation(
                        sg[:, :], co.pm[1][0:128, 0:512], AF.Silu),
                         reads=[co.pm[1]], writes=[sg], tag="l0_esilu")
                    dst = actbuf[hc // 2]
                    off = (hc % 2) * 1024 + tj * 512
                    p.op("vector", lambda e, dst=dst, off=off, sg=sg: e.tensor_tensor(
                        dst[:, off:off + 512], co.pm[0][0:128, 0:512], sg[:, :], ALU.mult),
                         reads=[co.pm[0], sg], writes=[dst], tag="l0_eact")
            for tj in range(2):
                j = th * 2 + tj
                sl = slice(j * 512, (j + 1) * 512)
                p.op("tensor", lambda t, e_=e_, sl=sl: t.matmul(
                    co.paux[0:128, 0:512], esel[0:8, e_ * 128:(e_ + 1) * 128],
                    GT[0:8, sl], start=True, stop=True),
                     reads=[esel, GT], writes=[co.paux], tag="l0_gbc")
                p.op("vector", lambda e: e.tensor_copy(gbt[:, :], co.paux[0:128, 0:512]),
                     reads=[co.paux], writes=[gbt], tag="l0_gbcc")
                for c in range(DC):
                    for hc in range(16):
                        src = w2sb[hc // 2]
                        woff = (hc % 2) * 1024 + c * 128
                        asrc = actbuf[hc // 2]
                        aoff = (hc % 2) * 1024 + tj * 512
                        p.op("tensor", lambda t, src=src, woff=woff, asrc=asrc,
                             aoff=aoff, hc=hc: t.matmul(
                                 co.pm[2][0:128, 0:512], src[:, woff:woff + 128],
                                 asrc[:, aoff:aoff + 512],
                                 start=(hc == 0), stop=(hc == 15)),
                             reads=[src, asrc], writes=[co.pm[2]], tag="l0_mmo2")
                    tmp = co.s512[2]
                    p.op("vector", lambda e, tmp=tmp: e.tensor_tensor(
                        tmp[:, :], co.pm[2][0:128, 0:512], gbt[:, :], ALU.mult),
                         reads=[co.pm[2], gbt], writes=[tmp], tag="l0_egate")
                    xt = co.x[c]
                    p.op("vector", lambda e, xt=xt, sl=sl, tmp=tmp: e.tensor_add(
                        xt[:, sl], xt[:, sl], tmp[:, :]),
                         reads=[xt, tmp], writes=[xt], tag="l0_eres")



C8 = float(1.5 * 2 ** 23 / 8)
C4 = float(1.5 * 2 ** 23 / 4)
C2 = float(1.5 * 2 ** 23 / 2)


def emit_quant(p, co, src_h, s_off, dst_h, d_off, q, sc, tag):
    q1, q2, q3, q4, q5 = q
    sv = src_h[:, s_off:s_off + 512].rearrange("p (g k) -> p g k", k=32)
    p.op("vector", lambda e: e.tensor_reduce(
        sc[:, 0:16], sv, AX.X, ALU.max, apply_absolute_value=True),
         reads=[src_h], writes=[sc], tag=tag + "_amax")
    p.op("vector", lambda e: e.tensor_scalar(
        sc[:, 0:16], sc[:, 0:16], 1.0 / 7.5, 1e-30, ALU.mult, ALU.max),
         reads=[sc], writes=[sc], tag=tag + "_sc")
    p.op("scalar", lambda e: e.activation(sc[:, 16:32], sc[:, 0:16], AF.Ln),
         reads=[sc], writes=[sc], tag=tag + "_ln")
    p.op("scalar", lambda e: e.activation(sc[:, 16:32], sc[:, 16:32], AF.Exp,
                                          scale=co.cv(-1.0)),
         reads=[sc, co.cvt], writes=[sc], tag=tag + "_inv")
    inv_b = sc[:, 16:32].rearrange("p (g o) -> p g o", o=1).broadcast_to([128, 16, 32])
    scl_b = sc[:, 0:16].rearrange("p (g o) -> p g o", o=1).broadcast_to([128, 16, 32])
    qv = lambda t: t[:, 0:512].rearrange("p (g k) -> p g k", k=32)
    p.op("vector", lambda e: e.tensor_tensor(qv(q1), sv, inv_b, ALU.mult),
         reads=[src_h, sc], writes=[q1], tag=tag + "_y")
    p.op("scalar", lambda e: e.activation(q2[:, :], q1[:, :], AF.Abs),
         reads=[q1], writes=[q2], tag=tag + "_u")
    p.op("vector", lambda e: e.tensor_scalar(q3[:, :], q1[:, :], C8, C8,
                                             ALU.add, ALU.subtract),
         reads=[q1], writes=[q3], tag=tag + "_r1")
    p.op("vector", lambda e: e.tensor_scalar(q4[:, :], q1[:, :], C4, C4,
                                             ALU.add, ALU.subtract),
         reads=[q1], writes=[q4], tag=tag + "_r2")
    p.op("vector", lambda e: e.tensor_sub(q4[:, :], q4[:, :], q3[:, :]),
         reads=[q4, q3], writes=[q4], tag=tag + "_d21")
    p.op("vector", lambda e: e.tensor_single_scalar(q5[:, :], q2[:, :], 2.0, ALU.is_ge),
         reads=[q2], writes=[q5], tag=tag + "_m2")
    p.op("vector", lambda e: e.tensor_mul(q4[:, :], q4[:, :], q5[:, :]),
         reads=[q4, q5], writes=[q4], tag=tag + "_md")
    p.op("vector", lambda e: e.tensor_add(q3[:, :], q3[:, :], q4[:, :]),
         reads=[q3, q4], writes=[q3], tag=tag + "_a1")
    p.op("vector", lambda e: e.tensor_scalar(q4[:, :], q1[:, :], C2, C2,
                                             ALU.add, ALU.subtract),
         reads=[q1], writes=[q4], tag=tag + "_r4")
    p.op("vector", lambda e: e.tensor_scalar(q1[:, :], q1[:, :], C4, C4,
                                             ALU.add, ALU.subtract),
         reads=[q1], writes=[q1], tag=tag + "_r2b")
    p.op("vector", lambda e: e.tensor_sub(q4[:, :], q4[:, :], q1[:, :]),
         reads=[q4, q1], writes=[q4], tag=tag + "_d42")
    p.op("vector", lambda e: e.tensor_single_scalar(q5[:, :], q2[:, :], 4.0, ALU.is_ge),
         reads=[q2], writes=[q5], tag=tag + "_m4")
    p.op("vector", lambda e: e.tensor_mul(q4[:, :], q4[:, :], q5[:, :]),
         reads=[q4, q5], writes=[q4], tag=tag + "_md4")
    p.op("vector", lambda e: e.tensor_add(q3[:, :], q3[:, :], q4[:, :]),
         reads=[q3, q4], writes=[q3], tag=tag + "_a2")
    dv = dst_h[:, d_off:d_off + 512].rearrange("p (g k) -> p g k", k=32)
    p.op("vector", lambda e: e.tensor_tensor(dv, qv(q3), scl_b, ALU.mult),
         reads=[q3, sc], writes=[dst_h], tag=tag + "_dq")


def emit_l1(p, co):
    s_win = stage_bf16(p, co, "L1_attn_W_in", [D, 2 * D])
    s_xprj = stage_bf16(p, co, "L1_attn_W_xproj", [D, 33])
    s_wout = stage_bf16(p, co, "L1_attn_W_out", [D, D])

    emit_stats(p, co, co.x, False)
    emit_normalize(p, co, co.x, co.pb1)
    xn = co.pb1

    with p.scope():
        emit_l1_ffn(p, co)

    convb = vec_cols(p, co, "L1_attn_conv_b", DC, "l1_convb")
    Dcol = vec_cols(p, co, "L1_attn_D_param", DC, "l1_D")
    cw_ext = co.ein("L1_attn_conv_w", [D, 1, 4], F32)
    l1_cw = p.sb("l1_cw", [128, 32], F32)
    for d in range(DC):
        p.dma("sync", l1_cw[:, 4 * d:4 * d + 4],
              cw_ext.ap()[d * 128:(d + 1) * 128, 0, :],
              reads=[cw_ext], writes=[l1_cw], tag="l1_cwl")
    la_ext = co.ein("L1_attn_log_A", [D, 16], F32)
    l1_A = p.sb("l1_A", [128, 128], F32)
    for d in range(DC):
        p.dma("sync", l1_A[:, 16 * d:16 * d + 16],
              la_ext.ap()[d * 128:(d + 1) * 128, :],
              reads=[la_ext], writes=[l1_A], tag="l1_lal")
    p.op("scalar", lambda e: e.activation(l1_A[:, :], l1_A[:, :], AF.Exp),
         reads=[l1_A], writes=[l1_A], tag="l1_Aexp")
    p.op("vector", lambda e: e.tensor_scalar_mul(l1_A[:, :], l1_A[:, :], -1.0),
         reads=[l1_A], writes=[l1_A], tag="l1_Aneg")

    xconv = co.pb2
    L1P = p.sbpool("L1P", 4096, BF16)
    xm_pad = p.view(L1P, 0, T + 3)
    p.op("vector", lambda e: e.memset(xm_pad[:, 0:3], 0.0),
         writes=[xm_pad], tag="l1_pz")
    for c in range(DC):
        ws = co.WS[c % 2]
        for d in range(DC):
            p.dma("sync", ws[d][:, 0:128],
                  s_win.ap()[d * 128:(d + 1) * 128, c * 128:(c + 1) * 128],
                  reads=[s_win], writes=[ws[d]], tag="l1_wxl")
        for j in range(NT):
            sl = slice(j * 512, (j + 1) * 512)
            for d in range(DC):
                p.op("tensor", lambda t, d=d, sl=sl, ws=ws: t.matmul(
                    co.pm[0][0:128, 0:512], ws[d][:, 0:128], xn[d][:, sl],
                    start=(d == 0), stop=(d == DC - 1)),
                     reads=[ws[d], xn[d]], writes=[co.pm[0]], tag="l1_mmx")
            p.op("vector", lambda e, j=j: e.tensor_copy(
                xm_pad[:, 3 + j * 512:3 + (j + 1) * 512], co.pm[0][0:128, 0:512]),
                 reads=[co.pm[0]], writes=[xm_pad], tag="l1_xev")
        for j in range(NT):
            acc = co.s512[0]
            p.op("vector", lambda e, c=c, j=j, acc=acc: e.tensor_scalar(
                acc[:, :], xm_pad[:, j * 512:j * 512 + 512],
                l1_cw[:, 4 * c:4 * c + 1], None, ALU.mult),
                 reads=[xm_pad, l1_cw], writes=[acc], tag="l1_cv0")
            for jj in range(1, 4):
                p.op("vector", lambda e, c=c, j=j, jj=jj, acc=acc: e.scalar_tensor_tensor(
                    acc[:, :], xm_pad[:, j * 512 + jj:j * 512 + jj + 512],
                    l1_cw[:, 4 * c + jj:4 * c + jj + 1], acc[:, :], ALU.mult, ALU.add),
                     reads=[xm_pad, l1_cw, acc], writes=[acc], tag="l1_cv")
            p.op("scalar", lambda e, c=c, j=j, acc=acc: e.activation(
                xconv[c][:, j * 512:(j + 1) * 512], acc[:, :], AF.Silu,
                bias=convb[:, c:c + 1]),
                 reads=[acc, convb], writes=[xconv[c]], tag="l1_silu")

    bcdB = p.view(L1P, 0, T, nrows=16)
    bcdC = p.view(L1P, T, T, nrows=16)
    l1xp = p.sb("l1_xp", [128, 264], BF16)
    for d in range(DC):
        p.dma("sync", l1xp[:, 33 * d:33 * d + 33],
              s_xprj.ap()[d * 128:(d + 1) * 128, 0:33],
              reads=[s_xprj], writes=[l1xp], tag="l1_wpl")
    for j in range(NT):
        sl = slice(j * 512, (j + 1) * 512)
        for d in range(DC):
            p.op("tensor", lambda t, d=d, sl=sl: t.matmul(
                co.pm[0][0:16, 0:512], l1xp[:, 33 * d:33 * d + 16], xconv[d][:, sl],
                start=(d == 0), stop=(d == DC - 1)),
                 reads=[l1xp, xconv[d]], writes=[co.pm[0]], tag="l1_mmB")
        for d in range(DC):
            p.op("tensor", lambda t, d=d, sl=sl: t.matmul(
                co.pm[1][0:16, 0:512], l1xp[:, 33 * d + 16:33 * d + 32], xconv[d][:, sl],
                start=(d == 0), stop=(d == DC - 1)),
                 reads=[l1xp, xconv[d]], writes=[co.pm[1]], tag="l1_mmC")
        p.op("vector", lambda e, sl=sl: e.tensor_copy(bcdB[:, sl], co.pm[0][0:16, 0:512]),
             reads=[co.pm[0]], writes=[bcdB], tag="l1_evB")
        p.op("vector", lambda e, sl=sl: e.tensor_copy(bcdC[:, sl], co.pm[1][0:16, 0:512]),
             reads=[co.pm[1]], writes=[bcdC], tag="l1_evC")
    psum_zero(p, co, co.pm[2], 16, "l1_dz")
    for tc in range(TC):
        for d in range(DC):
            p.op("tensor", lambda t, tc=tc, d=d: t.matmul(
                co.pm[2][0:128, tc:tc + 1], xconv[d][:, tc * 128:(tc + 1) * 128],
                l1xp[:, 33 * d + 32:33 * d + 33], start=False, stop=False),
                 reads=[xconv[d], l1xp], writes=[co.pm[2]], tag="l1_mmd")
    p.op("vector", lambda e: e.tensor_copy(co.colB[:, :], co.pm[2][0:128, 0:16]),
         reads=[co.pm[2]], writes=[co.colB], tag="l1_dcp")
    p.op("scalar", lambda e: e.activation(co.colA[:, :], co.colB[:, :], AF.Abs),
         reads=[co.colB], writes=[co.colA], tag="l1_dabs")
    p.op("scalar", lambda e: e.activation(co.colA[:, :], co.colA[:, :], AF.Exp,
                                          scale=co.cv(-1.0)),
         reads=[co.colA, co.cvt], writes=[co.colA], tag="l1_dexp2")
    p.op("scalar", lambda e: e.activation(co.colA[:, :], co.colA[:, :], AF.Ln,
                                          bias=co.cv(1.0)),
         reads=[co.colA, co.cvt], writes=[co.colA], tag="l1_dln2")
    p.op("vector", lambda e: e.tensor_scalar_max(co.colC[:, :], co.colB[:, :], 0.0),
         reads=[co.colB], writes=[co.colC], tag="l1_drelu")
    p.op("vector", lambda e: e.tensor_add(co.colB[:, :], co.colA[:, :], co.colC[:, :]),
         reads=[co.colA, co.colC], writes=[co.colB], tag="l1_dsp")

    du = co.pb3
    sel16 = co.cload("c_sel16")
    states = p.sb("l1_states", [128, 128], F32)
    p.op("vector", lambda e: e.memset(states[:, :], 0.0), writes=[states], tag="l1_st0")
    l1_y = p.sb("l1_y", [128, 512], F32)
    for q in range(NT):
        qsl = slice(q * 512, (q + 1) * 512)
        delta_b = co.s512[3]
        col_broadcast(p, co, co.colB[:, 4 * q:4 * q + 4], co.colB, delta_b, tag="l1_db")
        for d in range(DC):
            p.op("vector", lambda e, d=d, qsl=qsl: e.tensor_tensor(
                du[d][:, qsl], xconv[d][:, qsl], delta_b[:, :], ALU.mult),
                 reads=[xconv[d], delta_b], writes=[du[d]], tag="l1_du")
        for d in range(DC):
            for n in range(16):
                p.op("tensor", lambda t, n=n, qsl=qsl: t.matmul(
                    co.pm[0][0:128, 0:512], sel16[0:16, n * 128:(n + 1) * 128],
                    bcdB[:, qsl], start=True, stop=True),
                     reads=[sel16, bcdB], writes=[co.pm[0]], tag="l1_bB")
                p.op("tensor", lambda t, n=n, qsl=qsl: t.matmul(
                    co.pm[1][0:128, 0:512], sel16[0:16, n * 128:(n + 1) * 128],
                    bcdC[:, qsl], start=True, stop=True),
                     reads=[sel16, bcdC], writes=[co.pm[1]], tag="l1_bC")
                sa, sb_, sh = co.s512[0], co.s512[1], co.s512[2]
                scol = 16 * d + n
                p.op("scalar", lambda e, scol=scol, sa=sa: e.activation(
                    sa[:, :], delta_b[:, :], AF.Exp,
                    scale=l1_A[:, scol:scol + 1]),
                     reads=[delta_b, l1_A], writes=[sa], tag="l1_dA")
                p.op("vector", lambda e, d=d, qsl=qsl, sb_=sb_: e.tensor_tensor(
                    sb_[:, :], du[d][:, qsl], co.pm[0][0:128, 0:512], ALU.mult),
                     reads=[du[d], co.pm[0]], writes=[sb_], tag="l1_bu")
                p.op("vector", lambda e, sa=sa, sb_=sb_, sh=sh, scol=scol:
                     e.tensor_tensor_scan(
                         sh[:, :], sa[:, :], sb_[:, :], states[:, scol:scol + 1],
                         ALU.mult, ALU.add),
                     reads=[sa, sb_, states], writes=[sh], tag="l1_scan")
                p.op("vector", lambda e, sh=sh, scol=scol: e.tensor_copy(
                    states[:, scol:scol + 1], sh[:, 511:512]),
                     reads=[sh], writes=[states], tag="l1_stu")
                p.op("vector", lambda e, sh=sh, sb_=sb_: e.tensor_tensor(
                    sb_[:, :], sh[:, :], co.pm[1][0:128, 0:512], ALU.mult),
                     reads=[sh, co.pm[1]], writes=[sb_], tag="l1_yt")
                if n == 0:
                    p.op("vector", lambda e, sb_=sb_: e.tensor_copy(
                        l1_y[:, :], sb_[:, :]),
                         reads=[sb_], writes=[l1_y], tag="l1_y0")
                else:
                    p.op("vector", lambda e, sb_=sb_: e.tensor_add(
                        l1_y[:, :], l1_y[:, :], sb_[:, :]),
                         reads=[l1_y, sb_], writes=[l1_y], tag="l1_ya")
            ws = co.WS[d % 2]
            for dd in range(DC):
                p.dma("sync", ws[dd][:, 128:256],
                      s_win.ap()[dd * 128:(dd + 1) * 128, D + d * 128:D + (d + 1) * 128],
                      reads=[s_win], writes=[ws[dd]], tag="l1_wzl")
            for dd in range(DC):
                p.op("tensor", lambda t, dd=dd, qsl=qsl, ws=ws: t.matmul(
                    co.pm[2][0:128, 0:512], ws[dd][:, 128:256], xn[dd][:, qsl],
                    start=(dd == 0), stop=(dd == DC - 1)),
                     reads=[ws[dd], xn[dd]], writes=[co.pm[2]], tag="l1_mmz")
            zs = co.s512[0]
            p.op("scalar", lambda e, zs=zs: e.activation(
                zs[:, :], co.pm[2][0:128, 0:512], AF.Silu),
                 reads=[co.pm[2]], writes=[zs], tag="l1_zs")
            p.op("vector", lambda e, d=d, qsl=qsl: e.scalar_tensor_tensor(
                l1_y[:, :], xconv[d][:, qsl], Dcol[:, d:d + 1], l1_y[:, :],
                ALU.mult, ALU.add),
                 reads=[xconv[d], Dcol, l1_y], writes=[l1_y], tag="l1_Dt")
            p.op("vector", lambda e, d=d, qsl=qsl, zs=zs: e.tensor_tensor(
                du[d][:, qsl], l1_y[:, :], zs[:, :], ALU.mult),
                 reads=[l1_y, zs], writes=[du[d]], tag="l1_yz")
    for c in range(DC):
        ws = co.WS[c % 2]
        for d in range(DC):
            p.dma("sync", ws[d][:, 0:128],
                  s_wout.ap()[d * 128:(d + 1) * 128, c * 128:(c + 1) * 128],
                  reads=[s_wout], writes=[ws[d]], tag="l1_wol")
        for j in range(NT):
            sl = slice(j * 512, (j + 1) * 512)
            for d in range(DC):
                p.op("tensor", lambda t, d=d, sl=sl, ws=ws: t.matmul(
                    co.pm[0][0:128, 0:512], ws[d][:, 0:128], du[d][:, sl],
                    start=(d == 0), stop=(d == DC - 1)),
                     reads=[ws[d], du[d]], writes=[co.pm[0]], tag="l1_mmo")
            xt = co.x[c]
            p.op("vector", lambda e, xt=xt, sl=sl: e.tensor_add(
                xt[:, sl], xt[:, sl], co.pm[0][0:128, 0:512]),
                 reads=[xt, co.pm[0]], writes=[xt], tag="l1_ores")


def emit_l1_ffn(p, co):
    w_ext = co.ein("L1_ffn_weight", [D, D], F32)
    bias = vec_cols(p, co, "L1_ffn_bias", DC, "l1_bias")
    xt_ = p.sb("l1_xt", [128, D], F32)
    wf = xt_
    xq = p.sb("l1_xq", [128, D], BF16)
    sc = p.sb("l1_sc", [128, 32], F32)
    qx = p.sb("l1_qx", [128, 512], F32)
    q = [co.s512[0], co.s512[1], co.s512[2], co.s512[3], qx]
    wq = co.pb3
    xdT = co.pb2
    for oc in range(DC):
        p.dma("sync", wf[:, :], w_ext.ap()[oc * 128:(oc + 1) * 128, :],
              reads=[w_ext], writes=[wf], tag="l1f_wl")
        for h in range(2):
            emit_quant(p, co, wf, h * 512, wq[oc], h * 512, q, sc, f"l1f_w{oc}{h}")
    for oc in range(DC):
        for ic in range(0, DC, 2):
            for k in range(2):
                p.op("tensor", lambda t, oc=oc, ic=ic, k=k: t.transpose(
                    co.ptb[0:128, k * 128:(k + 1) * 128],
                    wq[oc][:, (ic + k) * 128:(ic + k + 1) * 128], co.ident_bf[:, :]),
                     reads=[wq[oc], co.ident_bf], writes=[co.ptb], tag="l1f_wtr")
            for k in range(2):
                p.op("vector", lambda e, oc=oc, ic=ic, k=k: e.tensor_copy(
                    wq[ic + k][:, 1024 + oc * 128:1024 + (oc + 1) * 128],
                    co.ptb[0:128, k * 128:(k + 1) * 128]),
                     reads=[co.ptb], writes=[wq[ic + k]], tag="l1f_wte")
    for tc in range(TC):
        tsl = slice(tc * 128, (tc + 1) * 128)
        for g in range(2):
            for k in range(4):
                d = g * 4 + k
                p.op("tensor", lambda t, d=d, k=k, tsl=tsl: t.transpose(
                    co.paux[0:128, k * 128:(k + 1) * 128], co.x[d][:, tsl],
                    co.ident_f32[:, :]),
                     reads=[co.x[d], co.ident_f32], writes=[co.paux], tag="l1f_xtr")
            p.op("vector", lambda e, g=g, tc=tc: e.tensor_scalar(
                xt_[:, g * 512:(g + 1) * 512], co.paux[0:128, 0:512],
                co.colC[:, tc:tc + 1], co.colA[:, tc:tc + 1],
                ALU.mult, ALU.subtract),
                 reads=[co.paux, co.colC, co.colA], writes=[xt_], tag="l1f_xn")
        for h in range(2):
            emit_quant(p, co, xt_, h * 512, xq, h * 512, q, sc, f"l1f_x{tc}{h}")
        for ic in range(0, DC, 2):
            for k in range(2):
                p.op("tensor", lambda t, ic=ic, k=k: t.transpose(
                    co.ptb[0:128, k * 128:(k + 1) * 128],
                    xq[:, (ic + k) * 128:(ic + k + 1) * 128], co.ident_bf[:, :]),
                     reads=[xq, co.ident_bf], writes=[co.ptb], tag="l1f_btr")
            for k in range(2):
                p.op("vector", lambda e, ic=ic, k=k, tsl=tsl: e.tensor_copy(
                    xdT[ic + k][:, tsl], co.ptb[0:128, k * 128:(k + 1) * 128]),
                     reads=[co.ptb], writes=[xdT[ic + k]], tag="l1f_bte")
    for oc in range(DC):
        for j in range(NT):
            sl = slice(j * 512, (j + 1) * 512)
            for ic in range(DC):
                p.op("tensor", lambda t, oc=oc, ic=ic, sl=sl: t.matmul(
                    co.pm[0][0:128, 0:512],
                    wq[ic][:, 1024 + oc * 128:1024 + (oc + 1) * 128],
                    xdT[ic][:, sl], start=(ic == 0), stop=(ic == DC - 1)),
                     reads=[wq[ic], xdT[ic]], writes=[co.pm[0]], tag="l1f_mm")
            xt2 = co.x[oc]
            p.op("vector", lambda e, xt2=xt2, sl=sl, oc=oc: e.scalar_tensor_tensor(
                xt2[:, sl], co.pm[0][0:128, 0:512], bias[:, oc:oc + 1], xt2[:, sl],
                ALU.add, ALU.add),
                 reads=[co.pm[0], bias, xt2], writes=[xt2], tag="l1f_res")



L2_OFFS = [0] + [1 << k for k in range(11)]


def emit_l2(p, co):
    s_qkv = stage_bf16(p, co, "L2_attn_W_qkv", [D, 3 * D])
    s_proj = stage_bf16(p, co, "L2_attn_W_proj", [D, D])
    s_w1 = stage_bf16(p, co, "L2_ffn_w1_w", [4096, D])
    s_w2 = stage_bf16(p, co, "L2_ffn_w2_w", [4096, D])
    s_h1 = stage_bf16(p, co, "L2_ffn_h1_w", [4096, 4096])
    s_h2 = stage_bf16(p, co, "L2_ffn_h2_w", [4096, 4096])
    s_w3 = stage_bf16(p, co, "L2_ffn_w3_w", [D, 4096])

    emit_stats(p, co, co.x, False)
    emit_normalize(p, co, co.x, co.pb1)
    xn = co.pb1

    import os
    if not os.environ.get("L2_SKIP_FFN"):
        emit_l2_ffn(p, co, xn, s_w1, s_w2, s_h1, s_h2, s_w3)
    if not os.environ.get("L2_SKIP_ATTN"):
        emit_l2_attn(p, co, xn, s_qkv, s_proj)


def emit_l2_ffn(p, co, xn, s_w1, s_w2, s_h1, s_h2, s_w3):
    w1s = vec_cols(p, co, "L2_ffn_w1_s", 32, "l2_w1s")
    w2s = vec_cols(p, co, "L2_ffn_w2_s", 32, "l2_w2s")
    h1s = vec_cols(p, co, "L2_ffn_h1_s", 32, "l2_h1s")
    h2s = vec_cols(p, co, "L2_ffn_h2_s", 32, "l2_h2s")
    w3s = vec_cols(p, co, "L2_ffn_w3_s", DC, "l2_w3s")
    l2w = co.wtb

    def hslot(pool_sel, hh):
        t = pool_sel[hh // 4]
        off = (hh % 4) * 512
        return t, off

    for q in range(NT):
        qsl = slice(q * 512, (q + 1) * 512)
        for hb in range(8):
            wa = co.WT[hb % 2]
            wb = co.WT[(hb + 1) % 2]
            for ic in range(DC):
                p.dma("sync", wa[ic][:, :],
                      s_w1.ap()[hb * 512:(hb + 1) * 512, ic * 128:(ic + 1) * 128],
                      reads=[s_w1], writes=[wa[ic]], tag="l2_w1l", transpose=True)
            for hc in range(4):
                for ic in range(DC):
                    p.op("tensor", lambda t, ic=ic, hc=hc, qsl=qsl, wa=wa: t.matmul(
                        co.pm[hc][0:128, 0:512], wa[ic][:, hc * 128:(hc + 1) * 128],
                        xn[ic][:, qsl], start=(ic == 0), stop=(ic == DC - 1)),
                         reads=[wa[ic], xn[ic]], writes=[co.pm[hc]], tag="l2_mm1")
            for ic in range(DC):
                p.dma("sync", wb[ic][:, :],
                      s_w2.ap()[hb * 512:(hb + 1) * 512, ic * 128:(ic + 1) * 128],
                      reads=[s_w2], writes=[wb[ic]], tag="l2_w2l", transpose=True)
            for hc in range(4):
                hh = 4 * hb + hc
                for ic in range(DC):
                    p.op("tensor", lambda t, ic=ic, hc=hc, qsl=qsl, wb=wb: t.matmul(
                        co.pm[4][0:128, 0:512], wb[ic][:, hc * 128:(hc + 1) * 128],
                        xn[ic][:, qsl], start=(ic == 0), stop=(ic == DC - 1)),
                         reads=[wb[ic], xn[ic]], writes=[co.pm[4]], tag="l2_mm2")
                sg = co.s512[0]
                p.op("scalar", lambda e, hh=hh, hc=hc, sg=sg: e.activation(
                    sg[:, :], co.pm[hc][0:128, 0:512], AF.Silu,
                    scale=w1s[:, hh:hh + 1]),
                     reads=[co.pm[hc], w1s], writes=[sg], tag="l2_silu")
                dstt, doff = hslot(co.pb2, hh)
                p.op("vector", lambda e, hh=hh, dstt=dstt, doff=doff, sg=sg:
                     e.scalar_tensor_tensor(
                         dstt[:, doff:doff + 512], co.pm[4][0:128, 0:512],
                         w2s[:, hh:hh + 1], sg[:, :], ALU.mult, ALU.mult),
                     reads=[co.pm[4], w2s, sg], writes=[dstt], tag="l2_h")
        wtball = list(co.wtb) + list(co.wtb2)
        for (srcp, dstp, s_h, scol) in ((co.pb2, co.pb3, s_h1, h1s),
                                        (co.pb3, co.pb2, s_h2, h2s)):
            for ob in range(8):
                for hh in range(32):
                    wt = wtball[hh % 16]
                    p.dma("sync", wt[:, :],
                          s_h.ap()[ob * 512:(ob + 1) * 512, hh * 128:(hh + 1) * 128],
                          reads=[s_h], writes=[wt], tag="l2_hl", transpose=True)
                    st, soff = hslot(srcp, hh)
                    for oc in range(4):
                        p.op("tensor", lambda t, wt=wt, oc=oc, st=st, soff=soff, hh=hh:
                             t.matmul(
                                 co.pm[oc][0:128, 0:512], wt[:, oc * 128:(oc + 1) * 128],
                                 st[:, soff:soff + 512],
                                 start=(hh == 0), stop=(hh == 31)),
                             reads=[wt, st], writes=[co.pm[oc]], tag="l2_hmm")
                for oc in range(4):
                    oo = 4 * ob + oc
                    dstt, doff = hslot(dstp, oo)
                    p.op("scalar", lambda e, oc=oc, oo=oo, dstt=dstt, doff=doff:
                         e.activation(
                             dstt[:, doff:doff + 512], co.pm[oc][0:128, 0:512],
                             AF.Gelu, scale=scol[:, oo:oo + 1]),
                         reads=[co.pm[oc], scol], writes=[dstt], tag="l2_gelu")
        for cb in range(2):
            for hh in range(32):
                wt = wtball[hh % 16]
                p.dma("sync", wt[:, :],
                      s_w3.ap()[cb * 512:(cb + 1) * 512, hh * 128:(hh + 1) * 128],
                      reads=[s_w3], writes=[wt], tag="l2_w3l", transpose=True)
                st, soff = hslot(co.pb2, hh)
                for oc in range(4):
                    p.op("tensor", lambda t, wt=wt, oc=oc, st=st, soff=soff, hh=hh:
                         t.matmul(
                             co.pm[oc][0:128, 0:512], wt[:, oc * 128:(oc + 1) * 128],
                             st[:, soff:soff + 512],
                             start=(hh == 0), stop=(hh == 31)),
                         reads=[wt, st], writes=[co.pm[oc]], tag="l2_w3mm")
            for oc in range(4):
                cc = 4 * cb + oc
                xt = co.x[cc]
                p.op("vector", lambda e, oc=oc, cc=cc, xt=xt, qsl=qsl:
                     e.scalar_tensor_tensor(
                         xt[:, qsl], co.pm[oc][0:128, 0:512], w3s[:, cc:cc + 1],
                         xt[:, qsl], ALU.mult, ALU.add),
                     reads=[co.pm[oc], w3s, xt], writes=[xt], tag="l2_fres")


def emit_l2_attn(p, co, xn, s_qkv, s_proj):
    qkvb = vec_cols(p, co, "L2_attn_b_qkv", 24, "l2_qkvb")
    projb = vec_cols(p, co, "L2_attn_b_proj", DC, "l2_projb")
    sa = co.cload("c_l2sa")
    lmask = co.cload("c_l2mask")
    lpb = co.cload("c_l2pb")
    lden = co.cload("c_l2den")
    lrb = co.cload("c_l2rb")
    l2q = p.view(co.POOL3, 0, T)
    l2k = p.view(co.POOL3, T, D + T)
    l2v = p.view(co.POOL3, T + D + T, D + T)
    l2E = p.view(co.POOL3, 2 * (D + T) + T, T, nrows=24)
    l2es = p.view(co.POOL3, 2 * (D + T) + 2 * T, 512)
    l2rd = p.view(co.POOL3, 2 * (D + T) + 2 * T + 512, 512, nrows=2)
    l2nc = p.sb("l2nc", [24, 1], F32)
    l2r1 = p.sb("l2r1", [1, 32], F32)
    p.op("vector", lambda e: e.memset(l2k[:, 0:D], 0.0), writes=[l2k], tag="l2_kz")
    p.op("vector", lambda e: e.memset(l2v[:, 0:D], 0.0), writes=[l2v], tag="l2_vz")
    attnout = co.pb2
    for hp in range(8):
        for part, dst, doff, bbase in ((0, l2q, 0, 0), (1, l2k, D, 8), (2, l2v, D, 16)):
            ws = co.WS[part % 2]
            for d in range(DC):
                p.dma("sync", ws[d][:, 0:128],
                      s_qkv.ap()[d * 128:(d + 1) * 128,
                                 part * D + hp * 128:part * D + (hp + 1) * 128],
                      reads=[s_qkv], writes=[ws[d]], tag="l2_wql")
            for j in range(NT):
                sl = slice(j * 512, (j + 1) * 512)
                for d in range(DC):
                    p.op("tensor", lambda t, d=d, sl=sl, ws=ws: t.matmul(
                        co.pm[0][0:128, 0:512], ws[d][:, 0:128], xn[d][:, sl],
                        start=(d == 0), stop=(d == DC - 1)),
                         reads=[ws[d], xn[d]], writes=[co.pm[0]], tag="l2_mmqkv")
                p.op("vector", lambda e, dst=dst, doff=doff, j=j, bbase=bbase, hp=hp:
                     e.tensor_scalar_add(
                         dst[:, doff + j * 512:doff + (j + 1) * 512],
                         co.pm[0][0:128, 0:512], qkvb[:, bbase + hp:bbase + hp + 1]),
                     reads=[co.pm[0], qkvb], writes=[dst], tag="l2_qkve")
        for j in range(4):
            p.op("tensor", lambda t, j=j: t.matmul(
                co.PA.h[0:24, j * 512:(j + 1) * 512], co.ones_bf[0:1, 0:24],
                co.zrow[0:1, 0:512], start=True, stop=False),
                 reads=[co.ones_bf, co.zrow], writes=[co.PA], tag="l2_sz")
        for o in range(12):
            off = L2_OFFS[o]
            for j in range(4):
                sl = slice(j * 512, (j + 1) * 512)
                p.op("vector", lambda e, off=off, j=j: e.tensor_tensor(
                    l2es[:, :], l2q[:, j * 512:(j + 1) * 512],
                    l2k[:, D - off + j * 512:D - off + (j + 1) * 512], ALU.mult),
                     reads=[l2q, l2k], writes=[l2es], tag="l2_eo")
                p.op("tensor", lambda t, o=o, j=j: t.matmul(
                    co.PA.h[0:24, j * 512:(j + 1) * 512], sa[:, 24 * o:24 * (o + 1)],
                    l2es[:, :], start=False, stop=False),
                     reads=[sa, l2es], writes=[co.PA], tag="l2_smm")
        for j in range(4):
            p.op("tensor", lambda t, j=j: t.matmul(
                co.PA.h[0:24, j * 512:(j + 1) * 512], co.ident_bf[0:24, 0:24],
                lmask[0:24, j * 512:(j + 1) * 512], start=False, stop=(j == 3)),
                 reads=[co.ident_bf, lmask], writes=[co.PA], tag="l2_maskmm")
        p.op("vector", lambda e: e.reduce_max(l2nc[:, 0:1], co.PA.h[0:24, 0:T], AX.X),
             reads=[co.PA], writes=[l2nc], tag="l2_m24")
        p.op("tensor", lambda t: t.transpose(
            co.paux[0:1, 0:24], l2nc[:, 0:1], co.ident_f32[0:24, 0:24]),
             reads=[l2nc, co.ident_f32], writes=[co.paux], tag="l2_mtr")
        p.op("vector", lambda e: e.tensor_copy(l2r1[:, 0:24], co.paux[0:1, 0:24]),
             reads=[co.paux], writes=[l2r1], tag="l2_mev")
        p.op("vector", lambda e: e.reduce_max(l2r1[:, 24:25], l2r1[:, 0:24], AX.X),
             reads=[l2r1], writes=[l2r1], tag="l2_mred")
        p.op("vector", lambda e: e.tensor_scalar_mul(l2r1[:, 25:26], l2r1[:, 24:25],
                                                     -0.125),
             reads=[l2r1], writes=[l2r1], tag="l2_mneg")
        p.op("tensor", lambda t: t.matmul(
            co.paux[0:24, 0:1], co.ones[0:1, 0:24], l2r1[:, 25:26],
            start=True, stop=True),
             reads=[co.ones, l2r1], writes=[co.paux], tag="l2_mbc")
        p.op("vector", lambda e: e.tensor_copy(l2nc[:, 0:1], co.paux[0:24, 0:1]),
             reads=[co.paux], writes=[l2nc], tag="l2_mbe")
        p.op("scalar", lambda e: e.activation(
            l2E[:, :], co.PA.h[0:24, 0:T], AF.Exp, scale=co.cv(0.125, 24),
            bias=l2nc[:, 0:1]),
             reads=[co.PA, co.cvt, l2nc], writes=[l2E], tag="l2_exp")
        for j in range(4):
            sl = slice(j * 512, (j + 1) * 512)
            p.op("tensor", lambda t, j=j: t.matmul(
                co.pm[4][0:2, 0:512], lden[0:24, 0:2], l2E[:, j * 512:(j + 1) * 512],
                start=True, stop=True),
                 reads=[lden, l2E], writes=[co.pm[4]], tag="l2_dmm")
            p.op("scalar", lambda e, j=j: e.activation(
                l2rd[:, 0:512], co.pm[4][0:2, 0:512], AF.Ln),
                 reads=[co.pm[4]], writes=[l2rd], tag="l2_dln")
            p.op("scalar", lambda e, j=j: e.activation(
                l2rd[:, 0:512], l2rd[:, 0:512],
                AF.Exp, scale=co.cv(-1.0, 2)),
                 reads=[l2rd, co.cvt], writes=[l2rd], tag="l2_dexp")
            acc, tmp = co.s512[1], co.s512[2]
            for o in range(12):
                off = L2_OFFS[o]
                p.op("tensor", lambda t, o=o, j=j: t.matmul(
                    co.paux[0:128, 0:512], lpb[0:24, 128 * o:128 * (o + 1)],
                    l2E[:, j * 512:(j + 1) * 512], start=True, stop=True),
                     reads=[lpb, l2E], writes=[co.paux], tag="l2_pbm")
                if o == 0:
                    p.op("vector", lambda e, off=off, j=j, acc=acc: e.tensor_tensor(
                        acc[:, :], l2v[:, D - off + j * 512:D - off + (j + 1) * 512],
                        co.paux[0:128, 0:512], ALU.mult),
                         reads=[l2v, co.paux], writes=[acc], tag="l2_oa0")
                else:
                    p.op("vector", lambda e, off=off, j=j, tmp=tmp: e.tensor_tensor(
                        tmp[:, :], l2v[:, D - off + j * 512:D - off + (j + 1) * 512],
                        co.paux[0:128, 0:512], ALU.mult),
                         reads=[l2v, co.paux], writes=[tmp], tag="l2_oam")
                    p.op("vector", lambda e, acc=acc, tmp=tmp: e.tensor_add(
                        acc[:, :], acc[:, :], tmp[:, :]),
                         reads=[acc, tmp], writes=[acc], tag="l2_oaa")
            p.op("tensor", lambda t, j=j: t.matmul(
                co.paux[0:128, 0:512], lrb[0:2, 0:128], l2rd[0:2, 0:512],
                start=True, stop=True),
                 reads=[lrb, l2rd], writes=[co.paux], tag="l2_rbm")
            p.op("vector", lambda e, hp=hp, j=j, acc=acc: e.tensor_tensor(
                attnout[hp][:, j * 512:(j + 1) * 512], acc[:, :],
                co.paux[0:128, 0:512], ALU.mult),
                 reads=[acc, co.paux], writes=[attnout[hp]], tag="l2_onorm")
    for c in range(DC):
        ws = co.WS[c % 2]
        for d in range(DC):
            p.dma("sync", ws[d][:, 0:128],
                  s_proj.ap()[d * 128:(d + 1) * 128, c * 128:(c + 1) * 128],
                  reads=[s_proj], writes=[ws[d]], tag="l2_wpl")
        for j in range(NT):
            sl = slice(j * 512, (j + 1) * 512)
            for d in range(DC):
                p.op("tensor", lambda t, d=d, sl=sl, ws=ws: t.matmul(
                    co.pm[0][0:128, 0:512], ws[d][:, 0:128], attnout[d][:, sl],
                    start=(d == 0), stop=(d == DC - 1)),
                     reads=[ws[d], attnout[d]], writes=[co.pm[0]], tag="l2_mmp")
            xt = co.x[c]
            p.op("vector", lambda e, xt=xt, sl=sl, c=c: e.scalar_tensor_tensor(
                xt[:, sl], co.pm[0][0:128, 0:512], projb[:, c:c + 1], xt[:, sl],
                ALU.add, ALU.add),
                 reads=[co.pm[0], projb, xt], writes=[xt], tag="l2_pres")



def emit_l3(p, co):
    s_wq = stage_bf16(p, co, "L3_attn_Wq", [D, D])
    s_wk = stage_bf16(p, co, "L3_attn_Wk", [D, D])
    s_wv = stage_bf16(p, co, "L3_attn_Wv", [D, D])
    s_wo = stage_bf16(p, co, "L3_attn_Wo", [D, D])
    s_sh1 = stage_bf16(p, co, "L3_ffn_shared_w1", [4096, D])
    s_sh2 = stage_bf16(p, co, "L3_ffn_shared_w2", [D, 4096])
    s_ew1 = stage_bf16(p, co, "L3_ffn_experts_w1", [8, 4096, D])
    s_ew2 = stage_bf16(p, co, "L3_ffn_experts_w2", [8, D, 4096])
    s_gw = stage_bf16(p, co, "L3_ffn_gate_w", [8, D])

    emit_stats(p, co, co.x, False)
    emit_normalize(p, co, co.x, co.pb1)
    xn = co.pb1

    import os
    if not os.environ.get("L3_SKIP_FFN"):
        emit_l3_ffn(p, co, xn, s_sh1, s_sh2, s_ew1, s_ew2, s_gw)
    if not os.environ.get("L3_SKIP_ATTN"):
        emit_l3_attn(p, co, xn, s_wq, s_wk, s_wv, s_wo)


def emit_l3_ffn(p, co, xn, s_sh1, s_sh2, s_ew1, s_ew2, s_gw):
    esel = co.cload("c_esel")
    wr8 = p.sb("l3_wr8", [8, D], BF16)
    p.dma("sync", wr8[:, :], s_gw.ap(), reads=[s_gw], writes=[wr8], tag="l3_gwl")
    wrT = p.sb("l3_wrT", [128, 64], BF16)
    for d in range(DC):
        p.op("tensor", lambda t, d=d: t.transpose(
            co.ptb[0:128, 0:8], wr8[0:8, d * 128:(d + 1) * 128],
            co.ident_bf[0:8, 0:8]),
             reads=[wr8, co.ident_bf], writes=[co.ptb], tag="l3_gtr")
        p.op("vector", lambda e, d=d: e.tensor_copy(
            wrT[:, d * 8:(d + 1) * 8], co.ptb[0:128, 0:8]),
             reads=[co.ptb], writes=[wrT], tag="l3_gte")
    GT = emit_router(p, co, xn, wrT, "l3")

    def hslot(pool_sel, hh):
        t = pool_sel[hh // 4]
        off = (hh % 4) * 512
        return t, off

    gbt = co.s512[3]
    for q in range(NT):
        qsl = slice(q * 512, (q + 1) * 512)
        for hb in range(8):
            wa = co.WT[hb % 2]
            for ic in range(DC):
                p.dma("sync", wa[ic][:, :],
                      s_sh1.ap()[hb * 512:(hb + 1) * 512, ic * 128:(ic + 1) * 128],
                      reads=[s_sh1], writes=[wa[ic]], tag="l3_s1l", transpose=True)
            for hc in range(4):
                hh = 4 * hb + hc
                for ic in range(DC):
                    p.op("tensor", lambda t, ic=ic, hc=hc, qsl=qsl, wa=wa: t.matmul(
                        co.pm[hc][0:128, 0:512], wa[ic][:, hc * 128:(hc + 1) * 128],
                        xn[ic][:, qsl], start=(ic == 0), stop=(ic == DC - 1)),
                         reads=[wa[ic], xn[ic]], writes=[co.pm[hc]], tag="l3_s1mm")
            for hc in range(4):
                hh = 4 * hb + hc
                dstt, doff = hslot(co.pb2, hh)
                p.op("scalar", lambda e, hc=hc, dstt=dstt, doff=doff: e.activation(
                    dstt[:, doff:doff + 512], co.pm[hc][0:128, 0:512], AF.Silu),
                     reads=[co.pm[hc]], writes=[dstt], tag="l3_s1act")
        wtball = list(co.wtb) + list(co.wtb2)
        for cb in range(2):
            for hh in range(32):
                wt = wtball[hh % 16]
                p.dma("sync", wt[:, :],
                      s_sh2.ap()[cb * 512:(cb + 1) * 512, hh * 128:(hh + 1) * 128],
                      reads=[s_sh2], writes=[wt], tag="l3_s2l", transpose=True)
                st, soff = hslot(co.pb2, hh)
                for oc in range(4):
                    p.op("tensor", lambda t, wt=wt, oc=oc, st=st, soff=soff, hh=hh:
                         t.matmul(
                             co.pm[oc][0:128, 0:512], wt[:, oc * 128:(oc + 1) * 128],
                             st[:, soff:soff + 512],
                             start=(hh == 0), stop=(hh == 31)),
                         reads=[wt, st], writes=[co.pm[oc]], tag="l3_s2mm")
            for oc in range(4):
                cc = 4 * cb + oc
                xt = co.x[cc]
                p.op("vector", lambda e, oc=oc, xt=xt, qsl=qsl: e.tensor_add(
                    xt[:, qsl], xt[:, qsl], co.pm[oc][0:128, 0:512]),
                     reads=[xt, co.pm[oc]], writes=[xt], tag="l3_s2res")
        for e_ in range(8):
            for hb in range(8):
                wa = co.WT[hb % 2]
                for ic in range(DC):
                    p.dma("sync", wa[ic][:, :],
                          s_ew1.ap()[e_, hb * 512:(hb + 1) * 512, ic * 128:(ic + 1) * 128],
                          reads=[s_ew1], writes=[wa[ic]], tag="l3_e1l", transpose=True)
                for hc in range(4):
                    for ic in range(DC):
                        p.op("tensor", lambda t, ic=ic, hc=hc, qsl=qsl, wa=wa: t.matmul(
                            co.pm[hc][0:128, 0:512],
                            wa[ic][:, hc * 128:(hc + 1) * 128],
                            xn[ic][:, qsl], start=(ic == 0), stop=(ic == DC - 1)),
                             reads=[wa[ic], xn[ic]], writes=[co.pm[hc]], tag="l3_e1mm")
                for hc in range(4):
                    hh = 4 * hb + hc
                    dstt, doff = hslot(co.pb3, hh)
                    p.op("scalar", lambda e, hc=hc, dstt=dstt, doff=doff: e.activation(
                        dstt[:, doff:doff + 512], co.pm[hc][0:128, 0:512], AF.Silu),
                         reads=[co.pm[hc]], writes=[dstt], tag="l3_e1act")
            p.op("tensor", lambda t, e_=e_, qsl=qsl: t.matmul(
                co.paux[0:128, 0:512], esel[0:8, e_ * 128:(e_ + 1) * 128],
                GT[0:8, qsl], start=True, stop=True),
                 reads=[esel, GT], writes=[co.paux], tag="l3_gbc")
            p.op("vector", lambda e: e.tensor_copy(gbt[:, :], co.paux[0:128, 0:512]),
                 reads=[co.paux], writes=[gbt], tag="l3_gbcc")
            for cb in range(2):
                for hh in range(32):
                    wt = wtball[hh % 16]
                    p.dma("sync", wt[:, :],
                          s_ew2.ap()[e_, cb * 512:(cb + 1) * 512, hh * 128:(hh + 1) * 128],
                          reads=[s_ew2], writes=[wt], tag="l3_e2l", transpose=True)
                    st, soff = hslot(co.pb3, hh)
                    for oc in range(4):
                        p.op("tensor", lambda t, wt=wt, oc=oc, st=st, soff=soff, hh=hh:
                             t.matmul(
                                 co.pm[oc][0:128, 0:512], wt[:, oc * 128:(oc + 1) * 128],
                                 st[:, soff:soff + 512],
                                 start=(hh == 0), stop=(hh == 31)),
                             reads=[wt, st], writes=[co.pm[oc]], tag="l3_e2mm")
                for oc in range(4):
                    cc = 4 * cb + oc
                    xt = co.x[cc]
                    tmp = co.s512[2]
                    p.op("vector", lambda e, oc=oc, tmp=tmp: e.tensor_tensor(
                        tmp[:, :], co.pm[oc][0:128, 0:512], gbt[:, :], ALU.mult),
                         reads=[co.pm[oc], gbt], writes=[tmp], tag="l3_egate")
                    p.op("vector", lambda e, xt=xt, qsl=qsl, tmp=tmp: e.tensor_add(
                        xt[:, qsl], xt[:, qsl], tmp[:, :]),
                         reads=[xt, tmp], writes=[xt], tag="l3_eres")


def emit_router(p, co, xn, wrT, pref):
    GT = p.sb(pref + "_GT", [8, T], BF16)
    rtp = co.s512[0]
    for tc in range(TC):
        tsl = slice(tc * 128, (tc + 1) * 128)
        for d in range(DC):
            p.op("tensor", lambda t, d=d, tsl=tsl: t.matmul(
                co.pm[0][0:128, 0:8], xn[d][:, tsl], wrT[:, d * 8:(d + 1) * 8],
                start=(d == 0), stop=(d == DC - 1)),
                 reads=[xn[d], wrT], writes=[co.pm[0]], tag=pref + "_mmr")
        p.op("vector", lambda e: e.tensor_copy(rtp[:, 0:8], co.pm[0][0:128, 0:8]),
             reads=[co.pm[0]], writes=[rtp], tag=pref + "_rt")
        p.op("vector", lambda e: e.reduce_max(rtp[:, 16:17], rtp[:, 0:8], AX.X),
             reads=[rtp], writes=[rtp], tag=pref + "_m1")
        p.op("vector", lambda e: e.tensor_scalar(
            rtp[:, 24:32], rtp[:, 0:8], rtp[:, 16:17], None, ALU.is_equal),
             reads=[rtp], writes=[rtp], tag=pref + "_eq1")
        p.op("vector", lambda e: e.scalar_tensor_tensor(
            rtp[:, 8:16], rtp[:, 24:32], -1e30, rtp[:, 0:8], ALU.mult, ALU.add),
             reads=[rtp], writes=[rtp], tag=pref + "_mask1")
        p.op("vector", lambda e: e.reduce_max(rtp[:, 17:18], rtp[:, 8:16], AX.X),
             reads=[rtp], writes=[rtp], tag=pref + "_m2")
        p.op("vector", lambda e: e.tensor_scalar(
            rtp[:, 32:40], rtp[:, 8:16], rtp[:, 17:18], None, ALU.is_equal),
             reads=[rtp], writes=[rtp], tag=pref + "_eq2")
        p.op("vector", lambda e: e.tensor_sub(rtp[:, 18:19], rtp[:, 16:17], rtp[:, 17:18]),
             reads=[rtp], writes=[rtp], tag=pref + "_dm")
        p.op("scalar", lambda e: e.activation(rtp[:, 18:19], rtp[:, 18:19], AF.Sigmoid),
             reads=[rtp], writes=[rtp], tag=pref + "_g1")
        p.op("vector", lambda e: e.tensor_scalar(
            rtp[:, 19:20], rtp[:, 18:19], -1.0, 1.0, ALU.mult, ALU.add),
             reads=[rtp], writes=[rtp], tag=pref + "_g2")
        p.op("vector", lambda e: e.tensor_scalar(
            rtp[:, 24:32], rtp[:, 24:32], rtp[:, 18:19], None, ALU.mult),
             reads=[rtp], writes=[rtp], tag=pref + "_p1")
        p.op("vector", lambda e: e.scalar_tensor_tensor(
            rtp[:, 24:32], rtp[:, 32:40], rtp[:, 19:20], rtp[:, 24:32],
            ALU.mult, ALU.add),
             reads=[rtp], writes=[rtp], tag=pref + "_p2")
        gtile = p.sb(f"{pref}_g{tc}", [128, 8], BF16)
        p.op("vector", lambda e, gtile=gtile: e.tensor_copy(gtile[:, :], rtp[:, 24:32]),
             reads=[rtp], writes=[gtile], tag=pref + "_gcast")
        p.op("tensor", lambda t, gtile=gtile: t.transpose(
            co.ptb[0:8, 0:128], gtile[:, :], co.ident_bf[:, :]),
             reads=[gtile, co.ident_bf], writes=[co.ptb], tag=pref + "_gtr2")
        p.op("vector", lambda e, tsl=tsl: e.tensor_copy(GT[0:8, tsl], co.ptb[0:8, 0:128]),
             reads=[co.ptb], writes=[GT], tag=pref + "_gtc")
    return GT


def emit_l3_attn(p, co, xn, s_wq, s_wk, s_wv, s_wo):
    bq = vec_cols(p, co, "L3_attn_bq", DC, "l3_bq")
    bk = vec_cols(p, co, "L3_attn_bk", DC, "l3_bk")
    bo = vec_cols(p, co, "L3_attn_bo", DC, "l3_bo")
    bv_ext = co.ein("L3_attn_bv", [D], F32)
    bvr = p.sb("l3_bvr", [1, D], BF16)
    p.dma("gpsimd", bvr[:, :], bv_ext.ap().rearrange("(o n) -> o n", o=1),
          reads=[bv_ext], writes=[bvr], tag="l3_bvl")
    hsel = co.cload("c_l3hs")
    expb_ext = co.ein("c_l3expb", [T, T], BF16)
    l3q = [p.view(co.POOL3, 0, T), p.view(co.POOL3, T, T)]
    l3k = [p.view(co.POOL3, 2 * T, T), p.view(co.POOL3, 3 * T, T)]
    l3v = p.view(co.POOL3, 4 * T, 4096)
    l3es = p.view(co.POOL3, 4 * T + 4096, 1024)
    l3eb = [p.view(co.POOL3, 4 * T + 4096 + 1024 + i * 1024, 1024) for i in range(2)]
    l3rb = p.sb("l3_rb", [64, 1024], F32)
    l3nc = p.sb("l3_nc", [128, 8], F32)
    l3sm = p.sb("l3_sm", [2, 32], F32)
    attnout = co.pb2
    pm01 = PsumView(co.PA, 0, 1024)
    pm23 = PsumView(co.PA, 1024, 1024)
    pden = PsumView(co.PB, 0, 1024)
    for quad in range(4):
        for part, s_w, bias in ((0, s_wq, bq), (1, s_wk, bk)):
            dstp = l3q if part == 0 else l3k
            for hp in range(2):
                ws = co.WS[0]
                for d in range(DC):
                    p.dma("sync", ws[d][:, 0:128],
                          s_w.ap()[d * 128:(d + 1) * 128,
                                   quad * 256 + hp * 128:quad * 256 + (hp + 1) * 128],
                          reads=[s_w], writes=[ws[d]], tag="l3_wl")
                for j in range(NT):
                    sl = slice(j * 512, (j + 1) * 512)
                    for d in range(DC):
                        p.op("tensor", lambda t, d=d, sl=sl, ws=ws: t.matmul(
                            co.pm[0][0:128, 0:512], ws[d][:, 0:128], xn[d][:, sl],
                            start=(d == 0), stop=(d == DC - 1)),
                             reads=[ws[d], xn[d]], writes=[co.pm[0]], tag="l3_mmqk")
                    p.op("vector", lambda e, hp=hp, sl=sl, quad=quad, bias=bias,
                         dstp=dstp: e.tensor_scalar_add(
                             dstp[hp][:, sl], co.pm[0][0:128, 0:512],
                             bias[:, 2 * quad + hp:2 * quad + hp + 1]),
                         reads=[co.pm[0], bias], writes=[dstp[hp]], tag="l3_qke")
        wsv = co.WS[0]
        for d in range(DC):
            p.dma("sync", wsv[d][:, 0:256],
                  s_wv.ap()[d * 128:(d + 1) * 128, quad * 256:(quad + 1) * 256],
                  reads=[s_wv], writes=[wsv[d]], tag="l3_wvl")
        for s in range(TC):
            ssl = slice(s * 128, (s + 1) * 128)
            for d in range(DC):
                p.op("tensor", lambda t, d=d, ssl=ssl: t.matmul(
                    co.pm[1][0:128, 0:256], xn[d][:, ssl], wsv[d][:, 0:256],
                    start=(d == 0), stop=False),
                     reads=[xn[d], wsv[d]], writes=[co.pm[1]], tag="l3_mmv")
            p.op("tensor", lambda t, quad=quad: t.matmul(
                co.pm[1][0:128, 0:256], co.ones_bf[0:1, 0:128],
                bvr[:, quad * 256:(quad + 1) * 256], start=False, stop=True),
                 reads=[co.ones_bf, bvr], writes=[co.pm[1]], tag="l3_mmvb")
            p.op("vector", lambda e, s=s: e.tensor_copy(
                l3v[:, s * 256:(s + 1) * 256], co.pm[1][0:128, 0:256]),
                 reads=[co.pm[1]], writes=[l3v], tag="l3_vev")
        for hp in range(2):
            for (src, col) in ((l3q[hp], 0), (l3k[hp], 2)):
                sq = co.s512[1]
                for j in range(NT):
                    p.op("scalar", lambda e, src=src, j=j, sq=sq: e.activation(
                        sq[:, :], src[:, j * 512:(j + 1) * 512], AF.Square),
                         reads=[src], writes=[sq], tag="l3_nsq")
                    p.op("tensor", lambda t, j=j, sq=sq: t.matmul(
                        co.pm[4][0:2, 0:512], hsel[0:128, 0:2], sq[:, :],
                        start=True, stop=True),
                         reads=[hsel, sq], writes=[co.pm[4]], tag="l3_nmm")
                    p.op("vector", lambda e, j=j, col=col: e.reduce_max(
                        l3sm[:, col + 16 + j:col + 17 + j], co.pm[4][0:2, 0:512], AX.X),
                         reads=[co.pm[4]], writes=[l3sm], tag="l3_nred")
                p.op("vector", lambda e, col=col: e.reduce_max(
                    l3sm[:, col:col + 1], l3sm[:, col + 16:col + 20], AX.X),
                     reads=[l3sm], writes=[l3sm], tag="l3_nred2")
            p.op("vector", lambda e: e.tensor_mul(l3sm[:, 4:5], l3sm[:, 0:1],
                                                  l3sm[:, 2:3]),
                 reads=[l3sm], writes=[l3sm], tag="l3_nm")
            p.op("scalar", lambda e: e.activation(l3sm[:, 4:5], l3sm[:, 4:5], AF.Ln),
                 reads=[l3sm], writes=[l3sm], tag="l3_nln")
            p.op("scalar", lambda e: e.activation(l3sm[:, 4:5], l3sm[:, 4:5], AF.Exp,
                                                  scale=co.cv(0.5, 2)),
                 reads=[l3sm, co.cvt], writes=[l3sm], tag="l3_nexp")
            p.op("vector", lambda e: e.tensor_scalar_mul(l3sm[:, 5:6], l3sm[:, 4:5],
                                                         -0.125),
                 reads=[l3sm], writes=[l3sm], tag="l3_nneg")
            p.op("tensor", lambda t: t.transpose(
                co.paux[0:1, 0:2], l3sm[0:2, 5:6], co.ident_f32[0:2, 0:2]),
                 reads=[l3sm, co.ident_f32], writes=[co.paux], tag="l3_ntr")
            p.op("vector", lambda e: e.tensor_copy(l3sm[0:1, 6:8], co.paux[0:1, 0:2]),
                 reads=[co.paux], writes=[l3sm], tag="l3_nte")
            for hh in range(2):
                p.op("tensor", lambda t, hh=hh: t.matmul(
                    co.paux[0:128, 0:1], co.ones[0:1, :], l3sm[0:1, 6 + hh:7 + hh],
                    start=True, stop=True),
                     reads=[co.ones, l3sm], writes=[co.paux], tag="l3_nbc")
                p.op("vector", lambda e, hp=hp, hh=hh: e.tensor_copy(
                    l3nc[:, 2 * hp + hh:2 * hp + hh + 1], co.paux[0:128, 0:1]),
                     reads=[co.paux], writes=[l3nc], tag="l3_nbe")
        for h in range(4):
            hp, hq = h // 2, h % 2
            base = hq * 64
            for qh in range(2):
                qof = qh * 1024
                for s in range(TC):
                    ebuf = l3eb[s % 2]
                    p.dma("sync", ebuf[:, 0:1024],
                          expb_ext.ap()[s * 128:(s + 1) * 128, qof:qof + 1024],
                          reads=[expb_ext], writes=[ebuf], tag="l3_ebl")
                    for u in range(2):
                        p.op("tensor", lambda t, s=s, u=u, qof=qof, hp=hp, base=base:
                             t.matmul(
                                 pm23[0 + 0:128, u * 512:(u + 1) * 512] if False else
                                 co.PA.h[0:128, 1024 + u * 512:1024 + (u + 1) * 512],
                                 l3k[hp][base:base + 64, s * 128:(s + 1) * 128],
                                 l3q[hp][base:base + 64, qof + u * 512:qof + (u + 1) * 512],
                                 start=True, stop=True),
                             reads=[l3k[hp], l3q[hp]], writes=[pm23], tag="l3_smm")
                    p.op("scalar", lambda e, hp=hp, hq=hq, h=h: e.activation(
                        l3es[:, 0:1024], co.PA.h[0:128, 1024:2048], AF.Exp,
                        scale=co.cv(0.125), bias=l3nc[:, h:h + 1]),
                         reads=[pm23, co.cvt, l3nc], writes=[l3es], tag="l3_sexp")
                    p.op("vector", lambda e, ebuf=ebuf: e.tensor_mul(
                        l3es[:, 0:1024], l3es[:, 0:1024], ebuf[:, 0:1024]),
                         reads=[l3es, ebuf], writes=[l3es], tag="l3_eb")
                    for u in range(2):
                        p.op("tensor", lambda t, s=s, u=u, h=h: t.matmul(
                            co.PA.h[0:64, u * 512:(u + 1) * 512],
                            l3v[:, s * 256 + h * 64:s * 256 + h * 64 + 64],
                            l3es[:, u * 512:(u + 1) * 512],
                            start=(s == 0), stop=(s == TC - 1)),
                             reads=[l3v, l3es], writes=[pm01], tag="l3_omm")
                        p.op("tensor", lambda t, s=s, u=u: t.matmul(
                            co.PB.h[0:1, u * 512:(u + 1) * 512], co.ones_bf[:, 0:1],
                            l3es[:, u * 512:(u + 1) * 512],
                            start=(s == 0), stop=(s == TC - 1)),
                             reads=[co.ones_bf, l3es], writes=[pden], tag="l3_dmm")
                p.op("scalar", lambda e: e.activation(
                    co.s512[1][0:1, 0:512], co.PB.h[0:1, 0:512], AF.Ln),
                     reads=[pden], writes=[co.s512[1]], tag="l3_dln1")
                p.op("scalar", lambda e: e.activation(
                    co.s512[2][0:1, 0:512], co.PB.h[0:1, 512:1024], AF.Ln),
                     reads=[pden], writes=[co.s512[2]], tag="l3_dln2")
                p.op("scalar", lambda e: e.activation(
                    co.s512[1][0:1, 0:512], co.s512[1][0:1, 0:512], AF.Exp,
                    scale=co.cv(-1.0, 1)),
                     reads=[co.s512[1], co.cvt], writes=[co.s512[1]], tag="l3_dex1")
                p.op("scalar", lambda e: e.activation(
                    co.s512[2][0:1, 0:512], co.s512[2][0:1, 0:512], AF.Exp,
                    scale=co.cv(-1.0, 1)),
                     reads=[co.s512[2], co.cvt], writes=[co.s512[2]], tag="l3_dex2")
                for u in range(2):
                    rsrc = co.s512[1] if u == 0 else co.s512[2]
                    p.op("tensor", lambda t, u=u, rsrc=rsrc: t.matmul(
                        co.PA.h[0:64, 1024 + u * 512:1024 + (u + 1) * 512],
                        co.ones[0:1, 0:64], rsrc[0:1, 0:512],
                        start=True, stop=True),
                         reads=[co.ones, rsrc], writes=[pm23], tag="l3_rbm")
                p.op("vector", lambda e: e.tensor_copy(
                    l3rb[:, 0:1024], co.PA.h[0:64, 1024:2048]),
                     reads=[pm23], writes=[l3rb], tag="l3_rbe")
                dst = attnout[h // 2 + 2 * (quad // 2)] if False else attnout[(quad * 4 + h) // 2]
                rbase = ((quad * 4 + h) % 2) * 64
                p.op("vector", lambda e, dst=dst, rbase=rbase, qof=qof: e.tensor_tensor(
                    dst[rbase:rbase + 64, qof:qof + 1024], co.PA.h[0:64, 0:1024],
                    l3rb[0:64, 0:1024], ALU.mult),
                     reads=[pm01, l3rb], writes=[dst], tag="l3_onorm")
    for c in range(DC):
        ws = co.WS[c % 2]
        for d in range(DC):
            p.dma("sync", ws[d][:, 0:128],
                  s_wo.ap()[d * 128:(d + 1) * 128, c * 128:(c + 1) * 128],
                  reads=[s_wo], writes=[ws[d]], tag="l3_wol")
        for j in range(NT):
            sl = slice(j * 512, (j + 1) * 512)
            for d in range(DC):
                p.op("tensor", lambda t, d=d, sl=sl, ws=ws: t.matmul(
                    co.pm[0][0:128, 0:512], ws[d][:, 0:128], attnout[d][:, sl],
                    start=(d == 0), stop=(d == DC - 1)),
                     reads=[ws[d], attnout[d]], writes=[co.pm[0]], tag="l3_mmo")
            xt = co.x[c]
            p.op("vector", lambda e, xt=xt, sl=sl, c=c: e.scalar_tensor_tensor(
                xt[:, sl], co.pm[0][0:128, 0:512], bo[:, c:c + 1], xt[:, sl],
                ALU.add, ALU.add),
                 reads=[co.pm[0], bo, xt], writes=[xt], tag="l3_pres")



_PROG_CACHE = {}


def build_program(layers):
    nc = bass.Bass("TRN2", target_bir_lowering=False, debug=False)
    p = P(nc)
    co = Core(p)
    x_in = p.dram("x_in", [D, T], F32, kind="ExternalInput")
    x_out = p.dram("x_out", [D, T], F32, kind="ExternalOutput")
    co.ext["x_in"] = x_in
    co.load_consts()
    for d in range(DC):
        p.dma("sync", co.x[d][:, :], x_in.ap()[d * 128:(d + 1) * 128, :],
              reads=[x_in], writes=[co.x[d]], tag="xload")
    for li in layers:
        co.lazy_consts = {}
        with p.scope():
            if li == 0:
                emit_l0(p, co)
            elif li == 1:
                emit_l1(p, co)
            elif li == 2:
                emit_l2(p, co)
            elif li == 3:
                emit_l3(p, co)
            else:
                raise NotImplementedError(f"layer {li}")
    for d in range(DC):
        p.dma("sync", x_out.ap()[d * 128:(d + 1) * 128, :], co.x[d][:, :],
              reads=[co.x[d]], writes=[x_out], tag="xstore")
    names = set(co.ext.keys())
    counts = p.n_ops()
    p.emit_and_close()
    cvarr = np.zeros((128, 16), np.float32)
    for v, col in co.cvals.items():
        cvarr[:, col] = v
    return nc, names, counts, cvarr


def _get_program(layers):
    key = tuple(layers)
    if key not in _PROG_CACHE:
        _PROG_CACHE[key] = build_program(layers)
    return _PROG_CACHE[key]


def kernel(**inputs):
    x = np.asarray(inputs["L0_attn_x"])
    xT = [np.ascontiguousarray(x[b].T) for b in range(NB)]
    for group in GROUPS:
        nc, names, counts, cvarr = _get_program(group)
        in_maps = []
        for b in range(NB):
            m = {"x_in": xT[b]}
            for n in names:
                if n == "x_in":
                    continue
                if n == "c_cvals":
                    m[n] = cvarr
                elif n.startswith("c_"):
                    m[n] = CONSTS[n]
                else:
                    m[n] = np.asarray(inputs[n])
            in_maps.append(m)
        res = run_bass_kernel_spmd(nc, in_maps, core_ids=list(range(NB)))
        xT = [np.asarray(r["x_out"]) for r in res.results]
    return np.stack([t.T for t in xT], axis=0)


# version 05
# speedup vs baseline: 1.9692x; 1.7562x over previous
import contextlib

import numpy as np
import ml_dtypes

import concourse.bass as bass
import concourse.mybir as mybir
from concourse.bass_utils import run_bass_kernel_spmd

F32 = mybir.dt.float32
BF16 = mybir.dt.bfloat16
AF = mybir.ActivationFunctionType
ALU = mybir.AluOpType
AX = mybir.AxisListType

BF = ml_dtypes.bfloat16

D = 1024
T = 2048
NB = 8
DC = D // 128
NT = T // 512
TC = T // 128

ENGINES = ("tensor", "vector", "scalar", "gpsimd", "sync")
DMA_QUEUES = ("sync", "gpsimd", "scalar")
DSEM_K = 24

GROUPS = [(0, 1, 2, 3)]


class _Op:
    __slots__ = ("fn", "waits", "inc", "dma", "tag")

    def __init__(self, fn, waits, dma, tag):
        self.fn = fn
        self.waits = waits
        self.inc = False
        self.dma = dma
        self.tag = tag


class PsumBuf:
    def __init__(self, nc, handle):
        self.h = handle
        m = nc.lookup_mloc(handle)
        nbytes = int(m.dims[1]) if len(m.dims) > 1 else int(m.dims[0])
        b0 = int(m.bank)
        nbanks = max(1, (nbytes + 2047) // 2048)
        self.keys = tuple(("psum", b) for b in range(b0, b0 + nbanks))

    def __getitem__(self, key):
        return self.h[key]


class PsumView:

    def __init__(self, buf, col0, ncols, elsz=4):
        self.buf = buf
        self.col0 = col0
        self.ncols = ncols
        b0 = buf.keys[0][1]
        lo = b0 + (col0 * elsz) // 2048
        hi = b0 + ((col0 + ncols) * elsz - 1) // 2048
        self.keys = tuple(("psum", b) for b in range(lo, hi + 1))

    def __getitem__(self, key):
        rows, cols = key
        if isinstance(cols, slice):
            start = (cols.start or 0) + self.col0
            stop = (cols.stop if cols.stop is not None else self.ncols) + self.col0
            cols = slice(start, stop)
        else:
            raise TypeError(cols)
        return self.buf.h[rows, cols]


class SbView:

    def __init__(self, h, col0, ncols, nrows=128, keys=None):
        self.h = h
        self.col0 = col0
        self.ncols = ncols
        self.nrows = nrows
        self.keys = keys

    def __getitem__(self, key):
        rows, cols = key
        if isinstance(rows, slice) and rows == slice(None):
            rows = slice(0, self.nrows)
        if isinstance(cols, slice):
            start = (cols.start or 0) + self.col0
            stop = (cols.stop if cols.stop is not None else self.ncols) + self.col0
            cols = slice(start, stop)
        else:
            raise TypeError(cols)
        return self.h[rows, cols]


_PINS = []


class P:

    def __init__(self, nc):
        self.nc = nc
        self.stack = contextlib.ExitStack()
        self.cur_stack = self.stack
        self._kcache = {}
        self.stream = {e: [] for e in ENGINES}
        self.lastw = {}
        self.readers = {}
        self.dma_n = {e: 0 for e in ENGINES}
        self.waited = {e: {} for e in ENGINES}

    def sb(self, name, shape, dtype):
        return self.cur_stack.enter_context(self.nc.sbuf_tensor(name, list(shape), dtype))

    @contextlib.contextmanager
    def scope(self):
        st = contextlib.ExitStack()
        old = self.cur_stack
        self.cur_stack = st
        try:
            yield
        finally:
            st.close()
            self.cur_stack = old

    def _keys(self, objs):
        out = []
        for o in objs:
            if isinstance(o, (PsumBuf, PsumView, SbView)):
                out.extend(o.keys)
            elif isinstance(o, tuple):
                out.append(o)
            else:
                kk = self._kcache.get(id(o))
                if kk is None:
                    _PINS.append(o)
                    try:
                        space = getattr(o, "space", None)
                        is_sb = space is not None and "SB" in str(space)
                    except Exception:
                        is_sb = False
                    if is_sb:
                        m = self.nc.lookup_mloc(o)
                        addr = int(m.addr)
                        nbytes = int(m.dims[1]) if len(m.dims) > 1 else int(m.dims[0])
                        g0 = addr // 512
                        g1 = (addr + nbytes - 1) // 512
                        kk = tuple(("sb", g) for g in range(g0, g1 + 1))
                    else:
                        kk = (("t", id(o)),)
                    self._kcache[id(o)] = kk
                out.extend(kk)
        return out

    def sbpool(self, name, ncols, dtype):
        h = self.sb(name, [128, ncols], dtype)
        elsz = 4 if dtype == F32 else 2
        m = self.nc.lookup_mloc(h)
        addr = int(m.addr)
        return h, addr, elsz

    def view(self, pool, col0, ncols, nrows=128):
        h, addr, elsz = pool
        b0 = col0 * elsz
        b1 = (col0 + ncols) * elsz - 1
        keys = tuple(("v", addr, g) for g in range(b0 // 512, b1 // 512 + 1))
        return SbView(h, col0, ncols, nrows, keys)

    def ps(self, name, shape, dtype=F32):
        h = self.stack.enter_context(self.nc.psum_tensor(name, list(shape), dtype))
        return PsumBuf(self.nc, h)

    def dram(self, name, shape, dtype, kind="Internal"):
        return self.nc.dram_tensor(name, list(shape), dtype, kind=kind)

    def _deps(self, reads, writes):
        deps = []
        for b in reads:
            t = self.lastw.get(b)
            if t is not None:
                deps.append(t)
        for b in writes:
            deps.extend(self.readers.get(b, ()))
            t = self.lastw.get(b)
            if t is not None:
                deps.append(t)
        return deps

    def _resolve(self, eng, deps):
        out = []
        w = self.waited[eng]
        for t in deps:
            if t[0] == "E":
                _, peng, idx = t
                if peng == eng and peng == "tensor":
                    continue
                k = ("E", peng)
                if w.get(k, -1) >= idx:
                    continue
                w[k] = idx
                out.append(t)
            else:
                _, qeng, cnt = t
                slot = (cnt - 1) % DSEM_K
                use = (cnt - 1) // DSEM_K + 1
                k = ("D", qeng, slot)
                if w.get(k, 0) >= use:
                    continue
                w[k] = use
                out.append(t)
        return out

    def _record(self, eng, fn, reads, writes, dma, tag):
        reads = self._keys(reads)
        writes = self._keys(writes)
        deps = self._deps(reads, writes)
        waits = self._resolve(eng, deps)
        for t in waits:
            if t[0] == "E":
                self.stream[t[1]][t[2]].inc = True
        if dma:
            self.dma_n[eng] += 1
            tok = ("D", eng, self.dma_n[eng])
        else:
            tok = ("E", eng, len(self.stream[eng]))
        self.stream[eng].append(_Op(fn, waits, dma, tag))
        for b in reads:
            self.readers.setdefault(b, []).append(tok)
        for b in writes:
            self.lastw[b] = tok
            self.readers[b] = []
        return tok

    def op(self, eng, fn, reads=(), writes=(), tag=""):
        return self._record(eng, fn, reads, writes, False, tag)

    def dma(self, qeng, out_ap, in_ap, reads=(), writes=(), tag="", transpose=False):
        if transpose:
            fn = lambda e: e.dma_start_transpose(out_ap, in_ap)
        else:
            fn = lambda e: e.dma_start(out=out_ap, in_=in_ap)
        return self._record(qeng, fn, reads, writes, True, tag)

    def emit_and_close(self):
        nc = self.nc
        sems = {e: self.stack.enter_context(nc.semaphore("sem_" + e)) for e in ENGINES}
        dsems = {q: [self.stack.enter_context(nc.semaphore(f"dsem_{q}{i}"))
                     for i in range(DSEM_K)] for q in DMA_QUEUES}
        rank = {}
        for e in ENGINES:
            c = 0
            r = []
            for o in self.stream[e]:
                if o.inc and not o.dma:
                    c += 1
                r.append(c)
            rank[e] = r

        def make_body(e):
            def body(eng):
                nd = 0
                for o in self.stream[e]:
                    for t in o.waits:
                        if t[0] == "E":
                            eng.wait_ge(sems[t[1]], rank[t[1]][t[2]])
                        else:
                            _, q, cnt = t
                            slot = (cnt - 1) % DSEM_K
                            use = (cnt - 1) // DSEM_K + 1
                            eng.wait_ge(dsems[q][slot], 16 * use)
                    if o.dma:
                        slot = nd % DSEM_K
                        use = nd // DSEM_K + 1
                        if use > 1:
                            eng.wait_ge(dsems[e][slot], 16 * (use - 1))
                        ins = o.fn(eng)
                        ins.then_inc(dsems[e][slot], 16)
                        nd += 1
                    else:
                        ins = o.fn(eng)
                        if o.inc:
                            ins.then_inc(sems[e], 1)
                if e == "sync":
                    for q in DMA_QUEUES:
                        n = self.dma_n[q]
                        for slot in range(min(n, DSEM_K)):
                            last_use = (n - 1 - slot) // DSEM_K + 1
                            eng.wait_ge(dsems[q][slot], 16 * last_use)
            return body

        with nc.Block() as block:
            block.tensor(make_body("tensor"))
            block.vector(make_body("vector"))
            block.scalar(make_body("scalar"))
            block.gpsimd(make_body("gpsimd"))
            block.sync(make_body("sync"))
        self.stack.close()

    def n_ops(self):
        d = {e: len(self.stream[e]) for e in ENGINES}
        d["total"] = sum(d.values())
        return d



def host_constants():
    c = {}
    c["c_ones"] = np.ones((128, 128), np.float32)
    c["c_ones_bf"] = np.ones((128, 256), BF)
    c["c_ident_bf"] = np.eye(128, dtype=BF)
    c["c_ident_f32"] = np.eye(128, dtype=np.float32)
    tsel = np.zeros((4, 512), np.float32)
    for k in range(4):
        tsel[k, k * 128:(k + 1) * 128] = 1.0
    c["c_tsel"] = tsel
    c["c_zero16"] = np.zeros((1, 16), np.float32)
    c["c_zrow"] = np.zeros((1, 512), BF)
    offs = [0] + [1 << k for k in range(11)]
    sa = np.zeros((128, 12 * 24), BF)
    for o in range(12):
        for pp in range(128):
            sa[pp, 24 * o + (pp // 64) * 12 + o] = 1.0
    c["c_l2sa"] = sa
    lm = np.zeros((24, 2048), np.float32)
    for h in range(2):
        for o in range(12):
            lm[h * 12 + o, 0:offs[o]] = -1e30
    c["c_l2mask"] = lm.astype(BF)
    pb = np.zeros((24, 12 * 128), BF)
    for o in range(12):
        for j in range(128):
            pb[(j // 64) * 12 + o, 128 * o + j] = 1.0
    c["c_l2pb"] = pb
    dn = np.zeros((24, 2), BF)
    dn[0:12, 0] = 1.0
    dn[12:24, 1] = 1.0
    c["c_l2den"] = dn
    rb2 = np.zeros((2, 128), BF)
    rb2[0, 0:64] = 1.0
    rb2[1, 64:128] = 1.0
    c["c_l2rb"] = rb2
    hs = np.zeros((128, 2), np.float32)
    hs[0:64, 0] = 1.0
    hs[64:128, 1] = 1.0
    c["c_l3hs"] = hs
    GRID, DECAY = 12, 0.3
    i = np.arange(T)
    xi = (i // GRID) % GRID
    yi = i % GRID
    dxa = np.abs(xi[:, None] - xi[None, :])
    dya = np.abs(yi[:, None] - yi[None, :])
    dx = np.minimum(dxa, GRID - dxa).astype(np.float32)
    dy = np.minimum(dya, GRID - dya).astype(np.float32)
    dist = np.sqrt(dx * dx + dy * dy)
    mask = np.exp(-DECAY * dist) * np.tril(np.ones((T, T), np.float32))
    c["c_l3expb"] = (mask + 1e-10).T.astype(BF)
    esel = np.zeros((8, 1024), BF)
    for e in range(8):
        esel[e, e * 128:(e + 1) * 128] = 1.0
    c["c_esel"] = esel
    sel16 = np.zeros((16, 2048), BF)
    for n in range(16):
        sel16[n, n * 128:(n + 1) * 128] = 1.0
    c["c_sel16"] = sel16
    u = np.arange(896)[None, :]
    pp = np.arange(128)[:, None]
    c["c_l0mask"] = (u >= pp + 384).astype(BF)
    return c


CONSTS = host_constants()



class Core:
    def __init__(self, p):
        self.p = p
        self.x = [p.sb(f"x{i}", [128, T], F32) for i in range(DC)]
        self.POOL1 = p.sbpool("POOL1", DC * T, BF16)
        self.POOL2 = p.sbpool("POOL2", DC * T, BF16)
        self.POOL3 = p.sbpool("POOL3", DC * T, BF16)
        self.pb1 = [p.view(self.POOL1, T * i, T) for i in range(DC)]
        self.pb2 = [p.view(self.POOL2, T * i, T) for i in range(DC)]
        self.pb3 = [p.view(self.POOL3, T * i, T) for i in range(DC)]
        self.s512 = [p.sb(f"s512_{i}", [128, 512], F32) for i in range(4)]
        self.colA = p.sb("colA", [128, TC], F32)
        self.colB = p.sb("colB", [128, TC], F32)
        self.colC = p.sb("colC", [128, TC], F32)
        self.sr4 = p.sb("sr4", [4, 128], F32)
        self.ones = p.sb("ones", [128, 128], F32)
        self.ones_bf = p.sb("ones_bf", [128, 256], BF16)
        self.ident_bf = p.sb("ident_bf", [128, 128], BF16)
        self.ident_f32 = p.sb("ident_f32", [128, 128], F32)
        self.tsel = p.sb("tsel", [4, 512], F32)
        self.WPOOL = p.sbpool("WPOOL", 8 * 512, BF16)
        self.WPOOL2 = p.sbpool("WPOOL2", 8 * 512, BF16)
        self.wtb = [p.view(self.WPOOL, 512 * i, 512) for i in range(8)]
        self.wtb2 = [p.view(self.WPOOL2, 512 * i, 512) for i in range(8)]
        self.wsb = [p.view(self.WPOOL, 512 * i, 256) for i in range(8)]
        self.wsb2 = [p.view(self.WPOOL2, 512 * i, 256) for i in range(8)]
        self.WS = (self.wsb, self.wsb2)
        self.WT = (self.wtb, self.wtb2)
        self.cvt = p.sb("cvt", [128, 16], F32)
        self.zrow = p.sb("zrow", [1, 512], BF16)
        self.cvals = {}
        self.lazy_consts = {}
        self.PA = p.ps("PA", [128, 2048])
        self.PB = p.ps("PB", [128, 1536])
        self.PT = p.ps("PT", [128, 512], BF16)
        self.pm = [PsumView(self.PA, 512 * i, 512) for i in range(4)]
        self.pm.append(PsumView(self.PB, 0, 512))
        self.paux = PsumView(self.PB, 512, 512)
        self.prow = PsumView(self.PB, 1024, 512)
        self.ptb = PsumView(self.PT, 0, 512, elsz=2)
        self.ext = {}

    def cv(self, v, n=128):
        v = float(v)
        if v not in self.cvals:
            assert len(self.cvals) < 16
            self.cvals[v] = len(self.cvals)
        return self.cvt[0:n, self.cvals[v]:self.cvals[v] + 1]

    def ein(self, name, shape, dtype=F32):
        if name not in self.ext:
            self.ext[name] = self.p.dram(name, list(shape), dtype, kind="ExternalInput")
        return self.ext[name]

    def load_consts(self):
        p = self.p
        pairs = [("c_ones", self.ones), ("c_ones_bf", self.ones_bf),
                 ("c_ident_bf", self.ident_bf), ("c_ident_f32", self.ident_f32),
                 ("c_tsel", self.tsel)]
        for name, tile in pairs:
            arr = CONSTS[name]
            d = self.ein(name, list(arr.shape),
                         F32 if arr.dtype == np.float32 else BF16)
            p.dma("sync", tile[:, :], d.ap(), reads=[d], writes=[tile], tag="const")
        d = self.ein("c_cvals", [128, 16], F32)
        p.dma("sync", self.cvt[:, :], d.ap(), reads=[d], writes=[self.cvt], tag="cvals")
        dz = self.ein("c_zrow", [1, 512], BF16)
        p.dma("sync", self.zrow[:, :], dz.ap(), reads=[dz], writes=[self.zrow], tag="zrow")

    def cload(self, name):
        if name in self.lazy_consts:
            return self.lazy_consts[name]
        arr = CONSTS[name]
        dt_ = F32 if arr.dtype == np.float32 else BF16
        d = self.ein(name, list(arr.shape), dt_)
        self._cload_n = getattr(self, "_cload_n", 0) + 1
        t = self.p.sb(f"sb{self._cload_n}" + name, list(arr.shape), dt_)
        self.p.dma("sync", t[:, :], d.ap(), reads=[d], writes=[t], tag="lconst")
        self.lazy_consts[name] = t
        return t


def vec_cols(p, co, in_name, n, name):
    d = co.ein(in_name, [n * 128], F32)
    t = p.sb(name, [128, n], F32)
    for c in range(n):
        p.dma("sync", t[:, c:c + 1],
              d.ap()[c * 128:(c + 1) * 128].rearrange("(p o) -> p o", o=1),
              reads=[d], writes=[t], tag="veccol")
    return t


def stage_bf16(p, co, in_name, shape, sname=None):
    src = co.ein(in_name, list(shape), F32)
    dst = p.dram((sname or ("s_" + in_name)), list(shape), BF16, kind="Internal")
    p.dma("gpsimd", dst.ap(), src.ap(), reads=[src], writes=[dst], tag="stage")
    return dst



def psum_zero(p, co, psum, ncols, tag="pz"):
    p.op("tensor", lambda t: t.matmul(psum[0:128, 0:ncols], co.ones_bf[0:1, 0:128],
                                      co.zrow[:, 0:ncols], start=True, stop=True),
         reads=[co.ones_bf, co.zrow], writes=[psum], tag=tag)


def emit_stats(p, co, src_tiles, src_bf16):
    ones_x = co.ones_bf[:, 0:1] if src_bf16 else co.ones[:, 0:1]
    ones_x_buf = co.ones_bf if src_bf16 else co.ones
    psum_zero(p, co, co.pm[0], 16, "st_z1")
    psum_zero(p, co, co.pm[1], 16, "st_z2")
    for j in range(NT):
        for d in range(DC):
            xt = src_tiles[d]
            sq = co.s512[d % 2]
            p.op("scalar", lambda e, xt=xt, sq=sq, j=j: e.activation(
                sq[:, :], xt[:, j * 512:(j + 1) * 512], AF.Square),
                 reads=[xt], writes=[sq], tag="st_sq")
            for u in range(4):
                tc = 4 * j + u
                p.op("tensor", lambda t, xt=xt, tc=tc: t.matmul(
                    co.pm[0][0:128, tc:tc + 1], xt[:, tc * 128:(tc + 1) * 128],
                    ones_x, start=False, stop=False),
                     reads=[ones_x_buf, xt], writes=[co.pm[0]], tag="st_sum")
                p.op("tensor", lambda t, sq=sq, tc=tc, u=u: t.matmul(
                    co.pm[1][0:128, tc:tc + 1], sq[:, u * 128:(u + 1) * 128],
                    co.ones[:, 0:1], start=False, stop=False),
                     reads=[co.ones, sq], writes=[co.pm[1]], tag="st_sumsq")
    p.op("vector", lambda e: e.tensor_scalar_mul(co.colA[:, :], co.pm[0][0:128, 0:16],
                                                 1.0 / D),
         reads=[co.pm[0]], writes=[co.colA], tag="st_mu")
    p.op("vector", lambda e: e.tensor_scalar_mul(co.colB[:, :], co.pm[1][0:128, 0:16],
                                                 1.0 / D),
         reads=[co.pm[1]], writes=[co.colB], tag="st_msq")
    p.op("vector", lambda e: e.tensor_tensor(co.colC[:, :], co.colA[:, :],
                                             co.colA[:, :], ALU.mult),
         reads=[co.colA], writes=[co.colC], tag="st_mu2")
    p.op("vector", lambda e: e.tensor_sub(co.colC[:, :], co.colB[:, :], co.colC[:, :]),
         reads=[co.colB, co.colC], writes=[co.colC], tag="st_var")
    p.op("scalar", lambda e: e.activation(co.colC[:, :], co.colC[:, :], AF.Ln,
                                          bias=co.cv(1e-5)),
         reads=[co.colC, co.cvt], writes=[co.colC], tag="st_ln")
    p.op("scalar", lambda e: e.activation(co.colC[:, :], co.colC[:, :], AF.Exp,
                                          scale=co.cv(-0.5)),
         reads=[co.colC, co.cvt], writes=[co.colC], tag="st_r")
    p.op("vector", lambda e: e.tensor_tensor(co.colA[:, :], co.colA[:, :],
                                             co.colC[:, :], ALU.mult),
         reads=[co.colA, co.colC], writes=[co.colA], tag="st_mr")


def col_broadcast(p, co, col_ap, col_buf, out_tile, w=512, tag="cb"):
    nchunks = w // 128
    p.op("tensor", lambda t: t.transpose(
        co.paux[0:nchunks, 0:128], col_ap, co.ident_f32[:, :]),
         reads=[col_buf, co.ident_f32], writes=[co.paux], tag=tag + "_tr")
    p.op("vector", lambda e: e.tensor_copy(co.sr4[0:nchunks, :],
                                           co.paux[0:nchunks, 0:128]),
         reads=[co.paux], writes=[co.sr4], tag=tag + "_ev")
    for c in range(nchunks):
        p.op("tensor", lambda t, c=c: t.matmul(
            co.paux[0:128, c * 128:(c + 1) * 128],
            co.tsel[0:nchunks, c * 128:(c + 1) * 128],
            co.sr4[0:nchunks, 0:128], start=True, stop=True),
             reads=[co.tsel, co.sr4], writes=[co.paux], tag=tag + "_mm")
    p.op("vector", lambda e: e.tensor_copy(out_tile[:, 0:w], co.paux[0:128, 0:w]),
         reads=[co.paux], writes=[out_tile], tag=tag + "_cp")


def emit_normalize(p, co, src_tiles, out_tiles, affine=None):
    for j in range(NT):
        sl = slice(j * 512, (j + 1) * 512)
        rb, mb, tmp = co.s512[3], co.s512[0], co.s512[2]
        col_broadcast(p, co, co.colC[:, 4 * j:4 * j + 4], co.colC, rb, tag="nm_r")
        col_broadcast(p, co, co.colA[:, 4 * j:4 * j + 4], co.colA, mb, tag="nm_m")
        for d in range(DC):
            xt, ot = src_tiles[d], out_tiles[d]
            p.op("vector", lambda e, xt=xt, sl=sl, tmp=tmp, rb=rb: e.tensor_tensor(
                tmp[:, :], xt[:, sl], rb[:, :], ALU.mult),
                 reads=[xt, rb], writes=[tmp], tag="nm_n1")
            if affine is None:
                p.op("vector", lambda e, tmp=tmp, ot=ot, sl=sl, mb=mb: e.tensor_sub(
                    ot[:, sl], tmp[:, :], mb[:, :]),
                     reads=[tmp, mb], writes=[ot], tag="nm_n2")
            else:
                g, b = affine
                p.op("vector", lambda e, tmp=tmp, mb=mb: e.tensor_sub(
                    tmp[:, :], tmp[:, :], mb[:, :]),
                     reads=[tmp, mb], writes=[tmp], tag="nm_n2")
                p.op("vector", lambda e, tmp=tmp, ot=ot, sl=sl, d=d: e.tensor_scalar(
                    ot[:, sl], tmp[:, :], g[:, d:d + 1], b[:, d:d + 1],
                    ALU.mult, ALU.add),
                     reads=[tmp, g, b], writes=[ot], tag="nm_aff")



def emit_l0(p, co):
    co.l0mask = co.cload("c_l0mask")
    s_qkv = stage_bf16(p, co, "L0_attn_qkv_w", [D, 3 * D])
    s_gate = stage_bf16(p, co, "L0_attn_gate_w", [D, D])
    s_router = stage_bf16(p, co, "L0_ffn_w_router", [D, 8])
    s_w1 = stage_bf16(p, co, "L0_ffn_w1", [8, D, 4096])
    s_w2 = stage_bf16(p, co, "L0_ffn_w2", [8, 2048, D])

    emit_stats(p, co, co.x, False)
    emit_normalize(p, co, co.x, co.pb1)
    xn = co.pb1

    emit_l0_ffn(p, co, xn, s_router, s_w1, s_w2)

    ln_g = vec_cols(p, co, "L0_attn_ln_g", DC, "l0_lng")
    ln_b = vec_cols(p, co, "L0_attn_ln_b", DC, "l0_lnb")
    qkv_b = vec_cols(p, co, "L0_attn_qkv_b", 3 * DC, "l0_qkvb")
    gate_b = vec_cols(p, co, "L0_attn_gate_b", DC, "l0_gateb")

    emit_stats(p, co, xn, True)
    emit_normalize(p, co, xn, co.pb2, affine=(ln_g, ln_b))
    xn2 = co.pb2

    kT = co.pb1
    for c in range(DC):
        ws = co.WS[c % 2]
        for d in range(DC):
            p.dma("sync", ws[d][:, 0:128],
                  s_gate.ap()[d * 128:(d + 1) * 128, c * 128:(c + 1) * 128],
                  reads=[s_gate], writes=[ws[d]], tag="l0_wgl")
            p.dma("sync", ws[d][:, 128:256],
                  s_qkv.ap()[d * 128:(d + 1) * 128, D + c * 128:D + (c + 1) * 128],
                  reads=[s_qkv], writes=[ws[d]], tag="l0_wkl")
        for j in range(NT):
            sl = slice(j * 512, (j + 1) * 512)
            pg, pk = (co.pm[0], co.pm[1]) if j % 2 == 0 else (co.pm[2], co.pm[3])
            for d in range(DC):
                p.op("tensor", lambda t, d=d, sl=sl, ws=ws, pg=pg: t.matmul(
                    pg[0:128, 0:512], ws[d][:, 0:128], xn2[d][:, sl],
                    start=(d == 0), stop=(d == DC - 1)),
                     reads=[ws[d], xn2[d]], writes=[pg], tag="l0_mmg")
            for d in range(DC):
                p.op("tensor", lambda t, d=d, sl=sl, ws=ws, pk=pk: t.matmul(
                    pk[0:128, 0:512], ws[d][:, 128:256], xn2[d][:, sl],
                    start=(d == 0), stop=(d == DC - 1)),
                     reads=[ws[d], xn2[d]], writes=[pk], tag="l0_mmk")
            sg, k1, km = (co.s512[0], co.s512[1], co.s512[2])
            p.op("scalar", lambda e, c=c, sg=sg, pg=pg: e.activation(
                sg[:, :], pg[0:128, 0:512], AF.Sigmoid, bias=gate_b[:, c:c + 1]),
                 reads=[pg, gate_b], writes=[sg], tag="l0_sig")
            p.op("vector", lambda e, c=c, k1=k1, sg=sg, pk=pk: e.scalar_tensor_tensor(
                k1[:, :], pk[0:128, 0:512], qkv_b[:, DC + c:DC + c + 1], sg[:, :],
                ALU.add, ALU.mult),
                 reads=[pk, qkv_b, sg], writes=[k1], tag="l0_k1")
            p.op("vector", lambda e, k1=k1, km=km: e.tensor_scalar_min(km[:, :], k1[:, :], 0.0),
                 reads=[k1], writes=[km], tag="l0_kmin")
            p.op("scalar", lambda e, km=km: e.activation(km[:, :], km[:, :], AF.Exp),
                 reads=[km], writes=[km], tag="l0_kexp")
            p.op("vector", lambda e, k1=k1: e.tensor_scalar_max(k1[:, :], k1[:, :], 0.0),
                 reads=[k1], writes=[k1], tag="l0_krelu")
            p.op("vector", lambda e, c=c, sl=sl, k1=k1, km=km: e.tensor_add(
                kT[c][:, sl], km[:, :], k1[:, :]),
                 reads=[km, k1], writes=[kT[c]], tag="l0_kout")

    vtok = co.pb3
    bv = p.sb("l0_bv", [1, 1024], BF16)
    p.dma("gpsimd", bv[:, :],
          co.ext["L0_attn_qkv_b"].ap()[2 * D:3 * D].rearrange("(o n) -> o n", o=1),
          reads=[co.ext["L0_attn_qkv_b"]], writes=[bv], tag="l0_bvf")
    for g in range(4):
        gsl = slice(2 * D + g * 256, 2 * D + (g + 1) * 256)
        ws = co.WS[g % 2]
        for d in range(DC):
            p.dma("sync", ws[d][:, :],
                  s_qkv.ap()[d * 128:(d + 1) * 128, gsl],
                  reads=[s_qkv], writes=[ws[d]], tag="l0_wvl")
        for s in range(TC):
            ssl = slice(s * 128, (s + 1) * 128)
            pv = co.pm[2] if s % 2 == 0 else co.pm[3]
            for d in range(DC):
                p.op("tensor", lambda t, d=d, ssl=ssl, ws=ws, pv=pv: t.matmul(
                    pv[0:128, 0:256], xn2[d][:, ssl], ws[d][:, :],
                    start=(d == 0), stop=False),
                     reads=[xn2[d], ws[d]], writes=[pv], tag="l0_mmv")
            p.op("tensor", lambda t, g=g, pv=pv: t.matmul(
                pv[0:128, 0:256], co.ones_bf[0:1, 0:128],
                bv[:, g * 256:(g + 1) * 256], start=False, stop=True),
                 reads=[co.ones_bf, bv], writes=[pv], tag="l0_mmvb")
            dst = vtok[s // 2]
            off = (s % 2) * 1024 + g * 256
            p.op("vector", lambda e, dst=dst, off=off, pv=pv: e.tensor_copy(
                dst[:, off:off + 256], pv[0:128, 0:256]),
                 reads=[pv], writes=[dst], tag="l0_vev")

    W = 256
    NJ = T // W
    qsb = [p.view(co.WPOOL, 512 * d + 256, 256) for d in range(DC)]
    esb = [p.sb(f"l0_e{i}", [128, W], BF16) for i in range(8)] + \
          [p.view(co.WPOOL2, 512 * i + 256, 256) for i in range(8)]
    kcs = p.sb("l0_kcs", [128, DC], F32)
    p.op("vector", lambda e: e.memset(kcs[:, :], 0.0), writes=[kcs], tag="l0_kcs0")
    rden_b = co.s512[3]
    for j in range(NJ):
        sl = slice(j * W, (j + 1) * W)
        for c in range(DC):
            ws = co.WS[c % 2]
            for d in range(DC):
                p.dma("sync", ws[d][:, 0:128],
                      s_qkv.ap()[d * 128:(d + 1) * 128, c * 128:(c + 1) * 128],
                      reads=[s_qkv], writes=[ws[d]], tag="l0_wql")
            pq = co.pm[3] if c % 2 == 0 else co.pm[2]
            for d in range(DC):
                p.op("tensor", lambda t, d=d, sl=sl, ws=ws, pq=pq: t.matmul(
                    pq[0:128, 0:W], ws[d][:, 0:128],
                    xn2[d][:, sl], start=(d == 0), stop=(d == DC - 1)),
                     reads=[ws[d], xn2[d]], writes=[pq], tag="l0_mmq")
            q1, qm = co.s512[0], co.s512[1]
            p.op("vector", lambda e, c=c, q1=q1, pq=pq: e.tensor_scalar_add(
                q1[:, 0:W], pq[0:128, 0:W], qkv_b[:, c:c + 1]),
                 reads=[pq, qkv_b], writes=[q1], tag="l0_qb")
            p.op("vector", lambda e, q1=q1, qm=qm: e.tensor_scalar_min(
                qm[:, 0:W], q1[:, 0:W], 0.0), reads=[q1], writes=[qm], tag="l0_qmin")
            p.op("scalar", lambda e, qm=qm: e.activation(qm[:, 0:W], qm[:, 0:W], AF.Exp),
                 reads=[qm], writes=[qm], tag="l0_qexp")
            p.op("vector", lambda e, q1=q1: e.tensor_scalar_max(q1[:, 0:W], q1[:, 0:W], 0.0),
                 reads=[q1], writes=[q1], tag="l0_qrelu")
            p.op("vector", lambda e, c=c, q1=q1, qm=qm: e.tensor_add(
                qsb[c][:, :], qm[:, 0:W], q1[:, 0:W]),
                 reads=[qm, q1], writes=[qsb[c]], tag="l0_qout")
        psum_zero(p, co, co.pm[4], 2, "l0_dz")
        for d in range(DC):
            kc = co.s512[2]
            p.op("vector", lambda e, d=d, sl=sl, kc=kc: e.tensor_tensor_scan(
                kc[:, 0:W], co.ones_bf[:, 0:W], kT[d][:, sl], kcs[:, d:d + 1],
                ALU.mult, ALU.add),
                 reads=[co.ones_bf, kT[d], kcs], writes=[kc], tag="l0_kscan")
            p.op("vector", lambda e, d=d, kc=kc: e.tensor_copy(
                kcs[:, d:d + 1], kc[:, W - 1:W]),
                 reads=[kc], writes=[kcs], tag="l0_kst")
            p.op("vector", lambda e, d=d, kc=kc: e.tensor_tensor(
                kc[:, 0:W], kc[:, 0:W], qsb[d][:, :], ALU.mult),
                 reads=[kc, qsb[d]], writes=[kc], tag="l0_kq")
            for h in range(2):
                p.op("tensor", lambda t, kc=kc, h=h: t.matmul(
                    co.pm[4][0:128, h:h + 1], kc[:, h * 128:(h + 1) * 128],
                    co.ones[:, 0:1], start=False, stop=False),
                     reads=[co.ones, kc], writes=[co.pm[4]], tag="l0_mmden")
        dcol = co.colB
        p.op("scalar", lambda e: e.activation(
            dcol[:, 0:2], co.pm[4][0:128, 0:2], AF.Ln, bias=co.cv(1e-6)),
             reads=[co.pm[4], co.cvt], writes=[dcol], tag="l0_dln")
        p.op("scalar", lambda e: e.activation(
            dcol[:, 0:2], dcol[:, 0:2], AF.Exp, scale=co.cv(-1.0)),
             reads=[dcol, co.cvt], writes=[dcol], tag="l0_dexp")
        col_broadcast(p, co, dcol[:, 0:2], dcol, rden_b, w=W, tag="l0_db")
        ns = 2 * j + 2
        for i in range(ns):
            isl = slice(i * 128, (i + 1) * 128)
            pe_ = co.pm[4] if i % 2 == 0 else co.pm[3]
            for d in range(DC):
                p.op("tensor", lambda t, d=d, isl=isl, pe_=pe_: t.matmul(
                    pe_[0:128, 0:W], kT[d][:, isl], qsb[d][:, :],
                    start=(d == 0), stop=(d == DC - 1)),
                     reads=[kT[d], qsb[d]], writes=[pe_], tag="l0_mma")
            if i >= 2 * j:
                ko = 384 - 128 * (i - 2 * j)
                p.op("vector", lambda e, i=i, ko=ko, pe_=pe_: e.tensor_tensor(
                    esb[i][:, :], pe_[0:128, 0:W],
                    co.l0mask[0:128, ko:ko + W], ALU.mult),
                     reads=[pe_, co.l0mask], writes=[esb[i]], tag="l0_emask")
            else:
                p.op("vector", lambda e, i=i, pe_=pe_: e.tensor_copy(
                    esb[i][:, :], pe_[0:128, 0:W]),
                     reads=[pe_], writes=[esb[i]], tag="l0_ecp")
        for c in range(DC):
            po = co.pm[0] if c % 2 == 0 else co.pm[1]
            for i in range(ns):
                vt = vtok[i // 2]
                voff = (i % 2) * 1024 + c * 128
                p.op("tensor", lambda t, vt=vt, voff=voff, i=i, ns=ns, po=po: t.matmul(
                    po[0:128, 0:W], vt[:, voff:voff + 128], esb[i][:, :],
                    start=(i == 0), stop=(i == ns - 1)),
                     reads=[vt, esb[i]], writes=[po], tag="l0_mmo")
            tmp = co.s512[2]
            p.op("vector", lambda e, tmp=tmp, po=po: e.tensor_tensor(
                tmp[:, 0:W], po[0:128, 0:W], rden_b[:, 0:W], ALU.mult),
                 reads=[po, rden_b], writes=[tmp], tag="l0_onorm")
            xt = co.x[c]
            p.op("vector", lambda e, xt=xt, sl=sl, tmp=tmp: e.tensor_add(
                xt[:, sl], xt[:, sl], tmp[:, 0:W]),
                 reads=[xt, tmp], writes=[xt], tag="l0_ores")


def emit_l0_ffn(p, co, xn, s_router, s_w1, s_w2):
    esel = co.cload("c_esel")
    wr = p.sb("l0_wr", [128, 64], BF16)
    for d in range(DC):
        p.dma("sync", wr[:, d * 8:(d + 1) * 8],
              s_router.ap()[d * 128:(d + 1) * 128, 0:8],
              reads=[s_router], writes=[wr], tag="l0_wrl")
    GT = p.sb("l0_GT", [8, T], BF16)
    rtp = co.s512[0]
    for tc in range(TC):
        tsl = slice(tc * 128, (tc + 1) * 128)
        for d in range(DC):
            p.op("tensor", lambda t, d=d, tsl=tsl: t.matmul(
                co.pm[0][0:128, 0:8], xn[d][:, tsl], wr[:, d * 8:(d + 1) * 8],
                start=(d == 0), stop=(d == DC - 1)),
                 reads=[xn[d], wr], writes=[co.pm[0]], tag="l0_mmr")
        p.op("vector", lambda e: e.tensor_copy(rtp[:, 0:8], co.pm[0][0:128, 0:8]),
             reads=[co.pm[0]], writes=[rtp], tag="l0_rt")
        p.op("vector", lambda e: e.reduce_max(rtp[:, 16:17], rtp[:, 0:8], AX.X),
             reads=[rtp], writes=[rtp], tag="l0_m1")
        p.op("vector", lambda e: e.tensor_scalar(
            rtp[:, 24:32], rtp[:, 0:8], rtp[:, 16:17], None, ALU.is_equal),
             reads=[rtp], writes=[rtp], tag="l0_eq1")
        p.op("vector", lambda e: e.scalar_tensor_tensor(
            rtp[:, 8:16], rtp[:, 24:32], -1e30, rtp[:, 0:8], ALU.mult, ALU.add),
             reads=[rtp], writes=[rtp], tag="l0_mask1")
        p.op("vector", lambda e: e.reduce_max(rtp[:, 17:18], rtp[:, 8:16], AX.X),
             reads=[rtp], writes=[rtp], tag="l0_m2")
        p.op("vector", lambda e: e.tensor_scalar(
            rtp[:, 32:40], rtp[:, 8:16], rtp[:, 17:18], None, ALU.is_equal),
             reads=[rtp], writes=[rtp], tag="l0_eq2")
        p.op("vector", lambda e: e.tensor_sub(rtp[:, 18:19], rtp[:, 16:17], rtp[:, 17:18]),
             reads=[rtp], writes=[rtp], tag="l0_dm")
        p.op("scalar", lambda e: e.activation(rtp[:, 18:19], rtp[:, 18:19], AF.Sigmoid),
             reads=[rtp], writes=[rtp], tag="l0_g1")
        p.op("vector", lambda e: e.tensor_scalar(
            rtp[:, 19:20], rtp[:, 18:19], -1.0, 1.0, ALU.mult, ALU.add),
             reads=[rtp], writes=[rtp], tag="l0_g2")
        p.op("vector", lambda e: e.tensor_scalar(
            rtp[:, 24:32], rtp[:, 24:32], rtp[:, 18:19], None, ALU.mult),
             reads=[rtp], writes=[rtp], tag="l0_p1")
        p.op("vector", lambda e: e.scalar_tensor_tensor(
            rtp[:, 24:32], rtp[:, 32:40], rtp[:, 19:20], rtp[:, 24:32],
            ALU.mult, ALU.add),
             reads=[rtp], writes=[rtp], tag="l0_p2")
        gtile = p.sb(f"l0_g{tc}", [128, 8], BF16)
        p.op("vector", lambda e, gtile=gtile: e.tensor_copy(gtile[:, :], rtp[:, 24:32]),
             reads=[rtp], writes=[gtile], tag="l0_gcast")
        p.op("tensor", lambda t, gtile=gtile: t.transpose(
            co.ptb[0:8, 0:128], gtile[:, :], co.ident_bf[:, :]),
             reads=[gtile, co.ident_bf], writes=[co.ptb], tag="l0_gtr")
        p.op("vector", lambda e, tsl=tsl: e.tensor_copy(GT[0:8, tsl], co.ptb[0:8, 0:128]),
             reads=[co.ptb], writes=[GT], tag="l0_gtc")

    actbuf = co.pb3
    w2sb = co.pb2
    gbt = co.s512[3]
    for e_ in range(8):
        for i in range(16):
            p.dma("sync", w2sb[i // 2][:, (i % 2) * 1024:(i % 2) * 1024 + 1024],
                  s_w2.ap()[e_, i * 128:(i + 1) * 128, 0:D],
                  reads=[s_w2], writes=[w2sb[i // 2]], tag="l0_w2l")
        for th in range(2):
            for hc in range(16):
                ws = co.WS[hc % 2]
                for d in range(DC):
                    p.dma("sync", ws[d][:, 0:128],
                          s_w1.ap()[e_, d * 128:(d + 1) * 128, hc * 128:(hc + 1) * 128],
                          reads=[s_w1], writes=[ws[d]], tag="l0_w1la")
                    p.dma("sync", ws[d][:, 128:256],
                          s_w1.ap()[e_, d * 128:(d + 1) * 128,
                                    2048 + hc * 128:2048 + (hc + 1) * 128],
                          reads=[s_w1], writes=[ws[d]], tag="l0_w1lg")
                for tj in range(2):
                    j = th * 2 + tj
                    sl = slice(j * 512, (j + 1) * 512)
                    par = (2 * hc + tj) % 2
                    pa_, pg_ = (co.pm[0], co.pm[1]) if par == 0 else (co.pm[2], co.pm[3])
                    for d in range(DC):
                        p.op("tensor", lambda t, d=d, sl=sl, ws=ws, pa_=pa_: t.matmul(
                            pa_[0:128, 0:512], ws[d][:, 0:128], xn[d][:, sl],
                            start=(d == 0), stop=(d == DC - 1)),
                             reads=[ws[d], xn[d]], writes=[pa_], tag="l0_mme1")
                    for d in range(DC):
                        p.op("tensor", lambda t, d=d, sl=sl, ws=ws, pg_=pg_: t.matmul(
                            pg_[0:128, 0:512], ws[d][:, 128:256], xn[d][:, sl],
                            start=(d == 0), stop=(d == DC - 1)),
                             reads=[ws[d], xn[d]], writes=[pg_], tag="l0_mme2")
                    sg = co.s512[1] if par == 0 else co.s512[3]
                    p.op("scalar", lambda e, sg=sg, pg_=pg_: e.activation(
                        sg[:, :], pg_[0:128, 0:512], AF.Silu),
                         reads=[pg_], writes=[sg], tag="l0_esilu")
                    dst = actbuf[hc // 2]
                    off = (hc % 2) * 1024 + tj * 512
                    p.op("vector", lambda e, dst=dst, off=off, sg=sg, pa_=pa_: e.tensor_tensor(
                        dst[:, off:off + 512], pa_[0:128, 0:512], sg[:, :], ALU.mult),
                         reads=[pa_, sg], writes=[dst], tag="l0_eact")
            for tj in range(2):
                j = th * 2 + tj
                sl = slice(j * 512, (j + 1) * 512)
                p.op("tensor", lambda t, e_=e_, sl=sl: t.matmul(
                    co.paux[0:128, 0:512], esel[0:8, e_ * 128:(e_ + 1) * 128],
                    GT[0:8, sl], start=True, stop=True),
                     reads=[esel, GT], writes=[co.paux], tag="l0_gbc")
                p.op("vector", lambda e: e.tensor_copy(gbt[:, :], co.paux[0:128, 0:512]),
                     reads=[co.paux], writes=[gbt], tag="l0_gbcc")
                for c in range(DC):
                    po2 = co.pm[4] if c % 2 == 0 else co.paux
                    for hc in range(16):
                        src = w2sb[hc // 2]
                        woff = (hc % 2) * 1024 + c * 128
                        asrc = actbuf[hc // 2]
                        aoff = (hc % 2) * 1024 + tj * 512
                        p.op("tensor", lambda t, src=src, woff=woff, asrc=asrc,
                             aoff=aoff, hc=hc, po2=po2: t.matmul(
                                 po2[0:128, 0:512], src[:, woff:woff + 128],
                                 asrc[:, aoff:aoff + 512],
                                 start=(hc == 0), stop=(hc == 15)),
                             reads=[src, asrc], writes=[po2], tag="l0_mmo2")
                    tmp = co.s512[2]
                    p.op("vector", lambda e, tmp=tmp, po2=po2: e.tensor_tensor(
                        tmp[:, :], po2[0:128, 0:512], gbt[:, :], ALU.mult),
                         reads=[po2, gbt], writes=[tmp], tag="l0_egate")
                    xt = co.x[c]
                    p.op("vector", lambda e, xt=xt, sl=sl, tmp=tmp: e.tensor_add(
                        xt[:, sl], xt[:, sl], tmp[:, :]),
                         reads=[xt, tmp], writes=[xt], tag="l0_eres")



C8 = float(1.5 * 2 ** 23 / 8)
C4 = float(1.5 * 2 ** 23 / 4)
C2 = float(1.5 * 2 ** 23 / 2)


def emit_quant(p, co, src_h, s_off, dst_h, d_off, q, sc, tag):
    q1, q2, q3, q4, q5 = q
    sv = src_h[:, s_off:s_off + 512].rearrange("p (g k) -> p g k", k=32)
    p.op("vector", lambda e: e.tensor_reduce(
        sc[:, 0:16], sv, AX.X, ALU.max, apply_absolute_value=True),
         reads=[src_h], writes=[sc], tag=tag + "_amax")
    p.op("vector", lambda e: e.tensor_scalar(
        sc[:, 0:16], sc[:, 0:16], 1.0 / 7.5, 1e-30, ALU.mult, ALU.max),
         reads=[sc], writes=[sc], tag=tag + "_sc")
    p.op("scalar", lambda e: e.activation(sc[:, 16:32], sc[:, 0:16], AF.Ln),
         reads=[sc], writes=[sc], tag=tag + "_ln")
    p.op("scalar", lambda e: e.activation(sc[:, 16:32], sc[:, 16:32], AF.Exp,
                                          scale=co.cv(-1.0)),
         reads=[sc, co.cvt], writes=[sc], tag=tag + "_inv")
    inv_b = sc[:, 16:32].rearrange("p (g o) -> p g o", o=1).broadcast_to([128, 16, 32])
    scl_b = sc[:, 0:16].rearrange("p (g o) -> p g o", o=1).broadcast_to([128, 16, 32])
    qv = lambda t: t[:, 0:512].rearrange("p (g k) -> p g k", k=32)
    p.op("vector", lambda e: e.tensor_tensor(qv(q1), sv, inv_b, ALU.mult),
         reads=[src_h, sc], writes=[q1], tag=tag + "_y")
    p.op("scalar", lambda e: e.activation(q2[:, :], q1[:, :], AF.Abs),
         reads=[q1], writes=[q2], tag=tag + "_u")
    p.op("vector", lambda e: e.tensor_scalar(q3[:, :], q1[:, :], C8, C8,
                                             ALU.add, ALU.subtract),
         reads=[q1], writes=[q3], tag=tag + "_r1")
    p.op("vector", lambda e: e.tensor_scalar(q4[:, :], q1[:, :], C4, C4,
                                             ALU.add, ALU.subtract),
         reads=[q1], writes=[q4], tag=tag + "_r2")
    p.op("vector", lambda e: e.tensor_sub(q4[:, :], q4[:, :], q3[:, :]),
         reads=[q4, q3], writes=[q4], tag=tag + "_d21")
    p.op("vector", lambda e: e.tensor_single_scalar(q5[:, :], q2[:, :], 2.0, ALU.is_ge),
         reads=[q2], writes=[q5], tag=tag + "_m2")
    p.op("vector", lambda e: e.tensor_mul(q4[:, :], q4[:, :], q5[:, :]),
         reads=[q4, q5], writes=[q4], tag=tag + "_md")
    p.op("vector", lambda e: e.tensor_add(q3[:, :], q3[:, :], q4[:, :]),
         reads=[q3, q4], writes=[q3], tag=tag + "_a1")
    p.op("vector", lambda e: e.tensor_scalar(q4[:, :], q1[:, :], C2, C2,
                                             ALU.add, ALU.subtract),
         reads=[q1], writes=[q4], tag=tag + "_r4")
    p.op("vector", lambda e: e.tensor_scalar(q1[:, :], q1[:, :], C4, C4,
                                             ALU.add, ALU.subtract),
         reads=[q1], writes=[q1], tag=tag + "_r2b")
    p.op("vector", lambda e: e.tensor_sub(q4[:, :], q4[:, :], q1[:, :]),
         reads=[q4, q1], writes=[q4], tag=tag + "_d42")
    p.op("vector", lambda e: e.tensor_single_scalar(q5[:, :], q2[:, :], 4.0, ALU.is_ge),
         reads=[q2], writes=[q5], tag=tag + "_m4")
    p.op("vector", lambda e: e.tensor_mul(q4[:, :], q4[:, :], q5[:, :]),
         reads=[q4, q5], writes=[q4], tag=tag + "_md4")
    p.op("vector", lambda e: e.tensor_add(q3[:, :], q3[:, :], q4[:, :]),
         reads=[q3, q4], writes=[q3], tag=tag + "_a2")
    dv = dst_h[:, d_off:d_off + 512].rearrange("p (g k) -> p g k", k=32)
    p.op("vector", lambda e: e.tensor_tensor(dv, qv(q3), scl_b, ALU.mult),
         reads=[q3, sc], writes=[dst_h], tag=tag + "_dq")


def emit_l1(p, co):
    s_win = stage_bf16(p, co, "L1_attn_W_in", [D, 2 * D])
    s_xprj = stage_bf16(p, co, "L1_attn_W_xproj", [D, 33])
    s_wout = stage_bf16(p, co, "L1_attn_W_out", [D, D])

    emit_stats(p, co, co.x, False)
    emit_normalize(p, co, co.x, co.pb1)
    xn = co.pb1

    with p.scope():
        emit_l1_ffn(p, co)

    convb = vec_cols(p, co, "L1_attn_conv_b", DC, "l1_convb")
    Dcol = vec_cols(p, co, "L1_attn_D_param", DC, "l1_D")
    cw_ext = co.ein("L1_attn_conv_w", [D, 1, 4], F32)
    l1_cw = p.sb("l1_cw", [128, 32], F32)
    for d in range(DC):
        p.dma("sync", l1_cw[:, 4 * d:4 * d + 4],
              cw_ext.ap()[d * 128:(d + 1) * 128, 0, :],
              reads=[cw_ext], writes=[l1_cw], tag="l1_cwl")
    la_ext = co.ein("L1_attn_log_A", [D, 16], F32)
    l1_A = p.sb("l1_A", [128, 128], F32)
    for d in range(DC):
        p.dma("sync", l1_A[:, 16 * d:16 * d + 16],
              la_ext.ap()[d * 128:(d + 1) * 128, :],
              reads=[la_ext], writes=[l1_A], tag="l1_lal")
    p.op("scalar", lambda e: e.activation(l1_A[:, :], l1_A[:, :], AF.Exp),
         reads=[l1_A], writes=[l1_A], tag="l1_Aexp")
    p.op("vector", lambda e: e.tensor_scalar_mul(l1_A[:, :], l1_A[:, :], -1.0),
         reads=[l1_A], writes=[l1_A], tag="l1_Aneg")

    xconv = co.pb2
    L1P = p.sbpool("L1P", 4096, BF16)
    xm_pad = p.view(L1P, 0, T + 3)
    p.op("vector", lambda e: e.memset(xm_pad[:, 0:3], 0.0),
         writes=[xm_pad], tag="l1_pz")
    for c in range(DC):
        ws = co.WS[c % 2]
        for d in range(DC):
            p.dma("sync", ws[d][:, 0:128],
                  s_win.ap()[d * 128:(d + 1) * 128, c * 128:(c + 1) * 128],
                  reads=[s_win], writes=[ws[d]], tag="l1_wxl")
        for j in range(NT):
            sl = slice(j * 512, (j + 1) * 512)
            pp = co.pm[j % 2]
            for d in range(DC):
                p.op("tensor", lambda t, d=d, sl=sl, ws=ws, pp=pp: t.matmul(
                    pp[0:128, 0:512], ws[d][:, 0:128], xn[d][:, sl],
                    start=(d == 0), stop=(d == DC - 1)),
                     reads=[ws[d], xn[d]], writes=[pp], tag="l1_mmx")
            p.op("vector", lambda e, j=j, pp=pp: e.tensor_copy(
                xm_pad[:, 3 + j * 512:3 + (j + 1) * 512], pp[0:128, 0:512]),
                 reads=[pp], writes=[xm_pad], tag="l1_xev")
        for j in range(NT):
            acc = co.s512[0]
            p.op("vector", lambda e, c=c, j=j, acc=acc: e.tensor_scalar(
                acc[:, :], xm_pad[:, j * 512:j * 512 + 512],
                l1_cw[:, 4 * c:4 * c + 1], None, ALU.mult),
                 reads=[xm_pad, l1_cw], writes=[acc], tag="l1_cv0")
            for jj in range(1, 4):
                p.op("vector", lambda e, c=c, j=j, jj=jj, acc=acc: e.scalar_tensor_tensor(
                    acc[:, :], xm_pad[:, j * 512 + jj:j * 512 + jj + 512],
                    l1_cw[:, 4 * c + jj:4 * c + jj + 1], acc[:, :], ALU.mult, ALU.add),
                     reads=[xm_pad, l1_cw, acc], writes=[acc], tag="l1_cv")
            p.op("scalar", lambda e, c=c, j=j, acc=acc: e.activation(
                xconv[c][:, j * 512:(j + 1) * 512], acc[:, :], AF.Silu,
                bias=convb[:, c:c + 1]),
                 reads=[acc, convb], writes=[xconv[c]], tag="l1_silu")

    bcdB = p.view(L1P, 0, T, nrows=16)
    bcdC = p.view(L1P, T, T, nrows=16)
    l1xp = p.sb("l1_xp", [128, 264], BF16)
    for d in range(DC):
        p.dma("sync", l1xp[:, 33 * d:33 * d + 33],
              s_xprj.ap()[d * 128:(d + 1) * 128, 0:33],
              reads=[s_xprj], writes=[l1xp], tag="l1_wpl")
    for j in range(NT):
        sl = slice(j * 512, (j + 1) * 512)
        for d in range(DC):
            p.op("tensor", lambda t, d=d, sl=sl: t.matmul(
                co.pm[0][0:16, 0:512], l1xp[:, 33 * d:33 * d + 16], xconv[d][:, sl],
                start=(d == 0), stop=(d == DC - 1)),
                 reads=[l1xp, xconv[d]], writes=[co.pm[0]], tag="l1_mmB")
        for d in range(DC):
            p.op("tensor", lambda t, d=d, sl=sl: t.matmul(
                co.pm[1][0:16, 0:512], l1xp[:, 33 * d + 16:33 * d + 32], xconv[d][:, sl],
                start=(d == 0), stop=(d == DC - 1)),
                 reads=[l1xp, xconv[d]], writes=[co.pm[1]], tag="l1_mmC")
        p.op("vector", lambda e, sl=sl: e.tensor_copy(bcdB[:, sl], co.pm[0][0:16, 0:512]),
             reads=[co.pm[0]], writes=[bcdB], tag="l1_evB")
        p.op("vector", lambda e, sl=sl: e.tensor_copy(bcdC[:, sl], co.pm[1][0:16, 0:512]),
             reads=[co.pm[1]], writes=[bcdC], tag="l1_evC")
    psum_zero(p, co, co.pm[2], 16, "l1_dz")
    for tc in range(TC):
        for d in range(DC):
            p.op("tensor", lambda t, tc=tc, d=d: t.matmul(
                co.pm[2][0:128, tc:tc + 1], xconv[d][:, tc * 128:(tc + 1) * 128],
                l1xp[:, 33 * d + 32:33 * d + 33], start=False, stop=False),
                 reads=[xconv[d], l1xp], writes=[co.pm[2]], tag="l1_mmd")
    p.op("vector", lambda e: e.tensor_copy(co.colB[:, :], co.pm[2][0:128, 0:16]),
         reads=[co.pm[2]], writes=[co.colB], tag="l1_dcp")
    p.op("scalar", lambda e: e.activation(co.colA[:, :], co.colB[:, :], AF.Abs),
         reads=[co.colB], writes=[co.colA], tag="l1_dabs")
    p.op("scalar", lambda e: e.activation(co.colA[:, :], co.colA[:, :], AF.Exp,
                                          scale=co.cv(-1.0)),
         reads=[co.colA, co.cvt], writes=[co.colA], tag="l1_dexp2")
    p.op("scalar", lambda e: e.activation(co.colA[:, :], co.colA[:, :], AF.Ln,
                                          bias=co.cv(1.0)),
         reads=[co.colA, co.cvt], writes=[co.colA], tag="l1_dln2")
    p.op("vector", lambda e: e.tensor_scalar_max(co.colC[:, :], co.colB[:, :], 0.0),
         reads=[co.colB], writes=[co.colC], tag="l1_drelu")
    p.op("vector", lambda e: e.tensor_add(co.colB[:, :], co.colA[:, :], co.colC[:, :]),
         reads=[co.colA, co.colC], writes=[co.colB], tag="l1_dsp")

    du = co.pb3
    sel16 = co.cload("c_sel16")
    states = p.sb("l1_states", [128, 128], F32)
    p.op("vector", lambda e: e.memset(states[:, :], 0.0), writes=[states], tag="l1_st0")
    l1_y = p.sb("l1_y", [128, 512], F32)
    for q in range(NT):
        qsl = slice(q * 512, (q + 1) * 512)
        delta_b = co.s512[3]
        col_broadcast(p, co, co.colB[:, 4 * q:4 * q + 4], co.colB, delta_b, tag="l1_db")
        for d in range(DC):
            p.op("vector", lambda e, d=d, qsl=qsl: e.tensor_tensor(
                du[d][:, qsl], xconv[d][:, qsl], delta_b[:, :], ALU.mult),
                 reads=[xconv[d], delta_b], writes=[du[d]], tag="l1_du")
        for d in range(DC):
            for n in range(16):
                p.op("tensor", lambda t, n=n, qsl=qsl: t.matmul(
                    co.pm[0][0:128, 0:512], sel16[0:16, n * 128:(n + 1) * 128],
                    bcdB[:, qsl], start=True, stop=True),
                     reads=[sel16, bcdB], writes=[co.pm[0]], tag="l1_bB")
                p.op("tensor", lambda t, n=n, qsl=qsl: t.matmul(
                    co.pm[1][0:128, 0:512], sel16[0:16, n * 128:(n + 1) * 128],
                    bcdC[:, qsl], start=True, stop=True),
                     reads=[sel16, bcdC], writes=[co.pm[1]], tag="l1_bC")
                sa, sb_, sh = co.s512[0], co.s512[1], co.s512[2]
                scol = 16 * d + n
                p.op("scalar", lambda e, scol=scol, sa=sa: e.activation(
                    sa[:, :], delta_b[:, :], AF.Exp,
                    scale=l1_A[:, scol:scol + 1]),
                     reads=[delta_b, l1_A], writes=[sa], tag="l1_dA")
                p.op("vector", lambda e, d=d, qsl=qsl, sb_=sb_: e.tensor_tensor(
                    sb_[:, :], du[d][:, qsl], co.pm[0][0:128, 0:512], ALU.mult),
                     reads=[du[d], co.pm[0]], writes=[sb_], tag="l1_bu")
                p.op("vector", lambda e, sa=sa, sb_=sb_, sh=sh, scol=scol:
                     e.tensor_tensor_scan(
                         sh[:, :], sa[:, :], sb_[:, :], states[:, scol:scol + 1],
                         ALU.mult, ALU.add),
                     reads=[sa, sb_, states], writes=[sh], tag="l1_scan")
                p.op("vector", lambda e, sh=sh, scol=scol: e.tensor_copy(
                    states[:, scol:scol + 1], sh[:, 511:512]),
                     reads=[sh], writes=[states], tag="l1_stu")
                p.op("vector", lambda e, sh=sh, sb_=sb_: e.tensor_tensor(
                    sb_[:, :], sh[:, :], co.pm[1][0:128, 0:512], ALU.mult),
                     reads=[sh, co.pm[1]], writes=[sb_], tag="l1_yt")
                if n == 0:
                    p.op("vector", lambda e, sb_=sb_: e.tensor_copy(
                        l1_y[:, :], sb_[:, :]),
                         reads=[sb_], writes=[l1_y], tag="l1_y0")
                else:
                    p.op("vector", lambda e, sb_=sb_: e.tensor_add(
                        l1_y[:, :], l1_y[:, :], sb_[:, :]),
                         reads=[l1_y, sb_], writes=[l1_y], tag="l1_ya")
            ws = co.WS[d % 2]
            for dd in range(DC):
                p.dma("sync", ws[dd][:, 128:256],
                      s_win.ap()[dd * 128:(dd + 1) * 128, D + d * 128:D + (d + 1) * 128],
                      reads=[s_win], writes=[ws[dd]], tag="l1_wzl")
            for dd in range(DC):
                p.op("tensor", lambda t, dd=dd, qsl=qsl, ws=ws: t.matmul(
                    co.pm[2][0:128, 0:512], ws[dd][:, 128:256], xn[dd][:, qsl],
                    start=(dd == 0), stop=(dd == DC - 1)),
                     reads=[ws[dd], xn[dd]], writes=[co.pm[2]], tag="l1_mmz")
            zs = co.s512[0]
            p.op("scalar", lambda e, zs=zs: e.activation(
                zs[:, :], co.pm[2][0:128, 0:512], AF.Silu),
                 reads=[co.pm[2]], writes=[zs], tag="l1_zs")
            p.op("vector", lambda e, d=d, qsl=qsl: e.scalar_tensor_tensor(
                l1_y[:, :], xconv[d][:, qsl], Dcol[:, d:d + 1], l1_y[:, :],
                ALU.mult, ALU.add),
                 reads=[xconv[d], Dcol, l1_y], writes=[l1_y], tag="l1_Dt")
            p.op("vector", lambda e, d=d, qsl=qsl, zs=zs: e.tensor_tensor(
                du[d][:, qsl], l1_y[:, :], zs[:, :], ALU.mult),
                 reads=[l1_y, zs], writes=[du[d]], tag="l1_yz")
    for c in range(DC):
        ws = co.WS[c % 2]
        for d in range(DC):
            p.dma("sync", ws[d][:, 0:128],
                  s_wout.ap()[d * 128:(d + 1) * 128, c * 128:(c + 1) * 128],
                  reads=[s_wout], writes=[ws[d]], tag="l1_wol")
        for j in range(NT):
            sl = slice(j * 512, (j + 1) * 512)
            pp = co.pm[j % 2]
            for d in range(DC):
                p.op("tensor", lambda t, d=d, sl=sl, ws=ws, pp=pp: t.matmul(
                    pp[0:128, 0:512], ws[d][:, 0:128], du[d][:, sl],
                    start=(d == 0), stop=(d == DC - 1)),
                     reads=[ws[d], du[d]], writes=[pp], tag="l1_mmo")
            xt = co.x[c]
            p.op("vector", lambda e, xt=xt, sl=sl, pp=pp: e.tensor_add(
                xt[:, sl], xt[:, sl], pp[0:128, 0:512]),
                 reads=[xt, pp], writes=[xt], tag="l1_ores")


def emit_l1_ffn(p, co):
    w_ext = co.ein("L1_ffn_weight", [D, D], F32)
    bias = vec_cols(p, co, "L1_ffn_bias", DC, "l1_bias")
    xt_ = p.sb("l1_xt", [128, D], F32)
    wf = xt_
    xq = p.sb("l1_xq", [128, D], BF16)
    sc = p.sb("l1_sc", [128, 32], F32)
    qx = p.sb("l1_qx", [128, 512], F32)
    q = [co.s512[0], co.s512[1], co.s512[2], co.s512[3], qx]
    wq = co.pb3
    xdT = co.pb2
    for oc in range(DC):
        p.dma("sync", wf[:, :], w_ext.ap()[oc * 128:(oc + 1) * 128, :],
              reads=[w_ext], writes=[wf], tag="l1f_wl")
        for h in range(2):
            emit_quant(p, co, wf, h * 512, wq[oc], h * 512, q, sc, f"l1f_w{oc}{h}")
    for oc in range(DC):
        for ic in range(0, DC, 2):
            for k in range(2):
                p.op("tensor", lambda t, oc=oc, ic=ic, k=k: t.transpose(
                    co.ptb[0:128, k * 128:(k + 1) * 128],
                    wq[oc][:, (ic + k) * 128:(ic + k + 1) * 128], co.ident_bf[:, :]),
                     reads=[wq[oc], co.ident_bf], writes=[co.ptb], tag="l1f_wtr")
            for k in range(2):
                p.op("vector", lambda e, oc=oc, ic=ic, k=k: e.tensor_copy(
                    wq[ic + k][:, 1024 + oc * 128:1024 + (oc + 1) * 128],
                    co.ptb[0:128, k * 128:(k + 1) * 128]),
                     reads=[co.ptb], writes=[wq[ic + k]], tag="l1f_wte")
    for tc in range(TC):
        tsl = slice(tc * 128, (tc + 1) * 128)
        for g in range(2):
            for k in range(4):
                d = g * 4 + k
                p.op("tensor", lambda t, d=d, k=k, tsl=tsl: t.transpose(
                    co.paux[0:128, k * 128:(k + 1) * 128], co.x[d][:, tsl],
                    co.ident_f32[:, :]),
                     reads=[co.x[d], co.ident_f32], writes=[co.paux], tag="l1f_xtr")
            p.op("vector", lambda e, g=g, tc=tc: e.tensor_scalar(
                xt_[:, g * 512:(g + 1) * 512], co.paux[0:128, 0:512],
                co.colC[:, tc:tc + 1], co.colA[:, tc:tc + 1],
                ALU.mult, ALU.subtract),
                 reads=[co.paux, co.colC, co.colA], writes=[xt_], tag="l1f_xn")
        for h in range(2):
            emit_quant(p, co, xt_, h * 512, xq, h * 512, q, sc, f"l1f_x{tc}{h}")
        for ic in range(0, DC, 2):
            for k in range(2):
                p.op("tensor", lambda t, ic=ic, k=k: t.transpose(
                    co.ptb[0:128, k * 128:(k + 1) * 128],
                    xq[:, (ic + k) * 128:(ic + k + 1) * 128], co.ident_bf[:, :]),
                     reads=[xq, co.ident_bf], writes=[co.ptb], tag="l1f_btr")
            for k in range(2):
                p.op("vector", lambda e, ic=ic, k=k, tsl=tsl: e.tensor_copy(
                    xdT[ic + k][:, tsl], co.ptb[0:128, k * 128:(k + 1) * 128]),
                     reads=[co.ptb], writes=[xdT[ic + k]], tag="l1f_bte")
    for oc in range(DC):
        for j in range(NT):
            sl = slice(j * 512, (j + 1) * 512)
            for ic in range(DC):
                p.op("tensor", lambda t, oc=oc, ic=ic, sl=sl: t.matmul(
                    co.pm[0][0:128, 0:512],
                    wq[ic][:, 1024 + oc * 128:1024 + (oc + 1) * 128],
                    xdT[ic][:, sl], start=(ic == 0), stop=(ic == DC - 1)),
                     reads=[wq[ic], xdT[ic]], writes=[co.pm[0]], tag="l1f_mm")
            xt2 = co.x[oc]
            p.op("vector", lambda e, xt2=xt2, sl=sl, oc=oc: e.scalar_tensor_tensor(
                xt2[:, sl], co.pm[0][0:128, 0:512], bias[:, oc:oc + 1], xt2[:, sl],
                ALU.add, ALU.add),
                 reads=[co.pm[0], bias, xt2], writes=[xt2], tag="l1f_res")



L2_OFFS = [0] + [1 << k for k in range(11)]


def emit_l2(p, co):
    s_qkv = stage_bf16(p, co, "L2_attn_W_qkv", [D, 3 * D])
    s_proj = stage_bf16(p, co, "L2_attn_W_proj", [D, D])
    s_w1 = stage_bf16(p, co, "L2_ffn_w1_w", [4096, D])
    s_w2 = stage_bf16(p, co, "L2_ffn_w2_w", [4096, D])
    s_h1 = stage_bf16(p, co, "L2_ffn_h1_w", [4096, 4096])
    s_h2 = stage_bf16(p, co, "L2_ffn_h2_w", [4096, 4096])
    s_w3 = stage_bf16(p, co, "L2_ffn_w3_w", [D, 4096])

    emit_stats(p, co, co.x, False)
    emit_normalize(p, co, co.x, co.pb1)
    xn = co.pb1

    import os
    if not os.environ.get("L2_SKIP_FFN"):
        emit_l2_ffn(p, co, xn, s_w1, s_w2, s_h1, s_h2, s_w3)
    if not os.environ.get("L2_SKIP_ATTN"):
        emit_l2_attn(p, co, xn, s_qkv, s_proj)


def emit_l2_ffn(p, co, xn, s_w1, s_w2, s_h1, s_h2, s_w3):
    w1s = vec_cols(p, co, "L2_ffn_w1_s", 32, "l2_w1s")
    w2s = vec_cols(p, co, "L2_ffn_w2_s", 32, "l2_w2s")
    h1s = vec_cols(p, co, "L2_ffn_h1_s", 32, "l2_h1s")
    h2s = vec_cols(p, co, "L2_ffn_h2_s", 32, "l2_h2s")
    w3s = vec_cols(p, co, "L2_ffn_w3_s", DC, "l2_w3s")
    l2w = co.wtb

    def hslot(pool_sel, hh):
        t = pool_sel[hh // 4]
        off = (hh % 4) * 512
        return t, off

    for q in range(NT):
        qsl = slice(q * 512, (q + 1) * 512)
        for hb in range(8):
            wa = co.WT[hb % 2]
            wb = co.WT[(hb + 1) % 2]
            for ic in range(DC):
                p.dma("sync", wa[ic][:, :],
                      s_w1.ap()[hb * 512:(hb + 1) * 512, ic * 128:(ic + 1) * 128],
                      reads=[s_w1], writes=[wa[ic]], tag="l2_w1l", transpose=True)
            for hc in range(4):
                for ic in range(DC):
                    p.op("tensor", lambda t, ic=ic, hc=hc, qsl=qsl, wa=wa: t.matmul(
                        co.pm[hc][0:128, 0:512], wa[ic][:, hc * 128:(hc + 1) * 128],
                        xn[ic][:, qsl], start=(ic == 0), stop=(ic == DC - 1)),
                         reads=[wa[ic], xn[ic]], writes=[co.pm[hc]], tag="l2_mm1")
            for ic in range(DC):
                p.dma("sync", wb[ic][:, :],
                      s_w2.ap()[hb * 512:(hb + 1) * 512, ic * 128:(ic + 1) * 128],
                      reads=[s_w2], writes=[wb[ic]], tag="l2_w2l", transpose=True)
            for hc in range(4):
                hh = 4 * hb + hc
                p2 = co.pm[4] if hc % 2 == 0 else co.paux
                for ic in range(DC):
                    p.op("tensor", lambda t, ic=ic, hc=hc, qsl=qsl, wb=wb, p2=p2: t.matmul(
                        p2[0:128, 0:512], wb[ic][:, hc * 128:(hc + 1) * 128],
                        xn[ic][:, qsl], start=(ic == 0), stop=(ic == DC - 1)),
                         reads=[wb[ic], xn[ic]], writes=[p2], tag="l2_mm2")
                sg = co.s512[0] if hc % 2 == 0 else co.s512[1]
                p.op("scalar", lambda e, hh=hh, hc=hc, sg=sg: e.activation(
                    sg[:, :], co.pm[hc][0:128, 0:512], AF.Silu,
                    scale=w1s[:, hh:hh + 1]),
                     reads=[co.pm[hc], w1s], writes=[sg], tag="l2_silu")
                dstt, doff = hslot(co.pb2, hh)
                p.op("vector", lambda e, hh=hh, dstt=dstt, doff=doff, sg=sg, p2=p2:
                     e.scalar_tensor_tensor(
                         dstt[:, doff:doff + 512], p2[0:128, 0:512],
                         w2s[:, hh:hh + 1], sg[:, :], ALU.mult, ALU.mult),
                     reads=[p2, w2s, sg], writes=[dstt], tag="l2_h")
        wtball = list(co.wtb) + list(co.wtb2)
        for (srcp, dstp, s_h, scol) in ((co.pb2, co.pb3, s_h1, h1s),
                                        (co.pb3, co.pb2, s_h2, h2s)):
            for ob in range(8):
                for hh in range(32):
                    wt = wtball[hh % 16]
                    p.dma("sync", wt[:, :],
                          s_h.ap()[ob * 512:(ob + 1) * 512, hh * 128:(hh + 1) * 128],
                          reads=[s_h], writes=[wt], tag="l2_hl", transpose=True)
                    st, soff = hslot(srcp, hh)
                    for oc in range(4):
                        p.op("tensor", lambda t, wt=wt, oc=oc, st=st, soff=soff, hh=hh:
                             t.matmul(
                                 co.pm[oc][0:128, 0:512], wt[:, oc * 128:(oc + 1) * 128],
                                 st[:, soff:soff + 512],
                                 start=(hh == 0), stop=(hh == 31)),
                             reads=[wt, st], writes=[co.pm[oc]], tag="l2_hmm")
                for oc in range(4):
                    oo = 4 * ob + oc
                    dstt, doff = hslot(dstp, oo)
                    p.op("scalar", lambda e, oc=oc, oo=oo, dstt=dstt, doff=doff:
                         e.activation(
                             dstt[:, doff:doff + 512], co.pm[oc][0:128, 0:512],
                             AF.Gelu, scale=scol[:, oo:oo + 1]),
                         reads=[co.pm[oc], scol], writes=[dstt], tag="l2_gelu")
        for cb in range(2):
            for hh in range(32):
                wt = wtball[hh % 16]
                p.dma("sync", wt[:, :],
                      s_w3.ap()[cb * 512:(cb + 1) * 512, hh * 128:(hh + 1) * 128],
                      reads=[s_w3], writes=[wt], tag="l2_w3l", transpose=True)
                st, soff = hslot(co.pb2, hh)
                for oc in range(4):
                    p.op("tensor", lambda t, wt=wt, oc=oc, st=st, soff=soff, hh=hh:
                         t.matmul(
                             co.pm[oc][0:128, 0:512], wt[:, oc * 128:(oc + 1) * 128],
                             st[:, soff:soff + 512],
                             start=(hh == 0), stop=(hh == 31)),
                         reads=[wt, st], writes=[co.pm[oc]], tag="l2_w3mm")
            for oc in range(4):
                cc = 4 * cb + oc
                xt = co.x[cc]
                p.op("vector", lambda e, oc=oc, cc=cc, xt=xt, qsl=qsl:
                     e.scalar_tensor_tensor(
                         xt[:, qsl], co.pm[oc][0:128, 0:512], w3s[:, cc:cc + 1],
                         xt[:, qsl], ALU.mult, ALU.add),
                     reads=[co.pm[oc], w3s, xt], writes=[xt], tag="l2_fres")


def emit_l2_attn(p, co, xn, s_qkv, s_proj):
    qkvb = vec_cols(p, co, "L2_attn_b_qkv", 24, "l2_qkvb")
    projb = vec_cols(p, co, "L2_attn_b_proj", DC, "l2_projb")
    sa = co.cload("c_l2sa")
    lmask = co.cload("c_l2mask")
    lpb = co.cload("c_l2pb")
    lden = co.cload("c_l2den")
    lrb = co.cload("c_l2rb")
    l2q = p.view(co.POOL3, 0, T)
    l2k = p.view(co.POOL3, T, D + T)
    l2v = p.view(co.POOL3, T + D + T, D + T)
    l2E = p.view(co.POOL3, 2 * (D + T) + T, T, nrows=24)
    l2esb = [p.view(co.POOL3, 2 * (D + T) + 2 * T + 512 * i, 512) for i in range(4)]
    l2es = l2esb[0]
    l2rd = p.view(co.POOL3, 2 * (D + T) + 2 * T + 2048, 512, nrows=2)
    l2nc = p.sb("l2nc", [24, 1], F32)
    l2r1 = p.sb("l2r1", [1, 32], F32)
    p.op("vector", lambda e: e.memset(l2k[:, 0:D], 0.0), writes=[l2k], tag="l2_kz")
    p.op("vector", lambda e: e.memset(l2v[:, 0:D], 0.0), writes=[l2v], tag="l2_vz")
    attnout = co.pb2
    for hp in range(8):
        for part, dst, doff, bbase in ((0, l2q, 0, 0), (1, l2k, D, 8), (2, l2v, D, 16)):
            ws = co.WS[part % 2]
            for d in range(DC):
                p.dma("sync", ws[d][:, 0:128],
                      s_qkv.ap()[d * 128:(d + 1) * 128,
                                 part * D + hp * 128:part * D + (hp + 1) * 128],
                      reads=[s_qkv], writes=[ws[d]], tag="l2_wql")
            for j in range(NT):
                sl = slice(j * 512, (j + 1) * 512)
                pp = co.pm[j % 2]
                for d in range(DC):
                    p.op("tensor", lambda t, d=d, sl=sl, ws=ws, pp=pp: t.matmul(
                        pp[0:128, 0:512], ws[d][:, 0:128], xn[d][:, sl],
                        start=(d == 0), stop=(d == DC - 1)),
                         reads=[ws[d], xn[d]], writes=[pp], tag="l2_mmqkv")
                p.op("vector", lambda e, dst=dst, doff=doff, j=j, bbase=bbase, hp=hp,
                     pp=pp: e.tensor_scalar_add(
                         dst[:, doff + j * 512:doff + (j + 1) * 512],
                         pp[0:128, 0:512], qkvb[:, bbase + hp:bbase + hp + 1]),
                     reads=[pp, qkvb], writes=[dst], tag="l2_qkve")
        for j in range(4):
            p.op("tensor", lambda t, j=j: t.matmul(
                co.PA.h[0:24, j * 512:(j + 1) * 512], co.ones_bf[0:1, 0:24],
                co.zrow[0:1, 0:512], start=True, stop=False),
                 reads=[co.ones_bf, co.zrow], writes=[co.PA], tag="l2_sz")
        for o in range(12):
            off = L2_OFFS[o]
            for j in range(4):
                sl = slice(j * 512, (j + 1) * 512)
                es = l2esb[(4 * o + j) % 4]
                p.op("vector", lambda e, off=off, j=j, es=es: e.tensor_tensor(
                    es[:, 0:512], l2q[:, j * 512:(j + 1) * 512],
                    l2k[:, D - off + j * 512:D - off + (j + 1) * 512], ALU.mult),
                     reads=[l2q, l2k], writes=[es], tag="l2_eo")
                p.op("tensor", lambda t, o=o, j=j, es=es: t.matmul(
                    co.PA.h[0:24, j * 512:(j + 1) * 512], sa[0:128, 24 * o:24 * (o + 1)],
                    es[:, 0:512], start=False, stop=False),
                     reads=[sa, es], writes=[co.PA], tag="l2_smm")
        for j in range(4):
            p.op("tensor", lambda t, j=j: t.matmul(
                co.PA.h[0:24, j * 512:(j + 1) * 512], co.ident_bf[0:24, 0:24],
                lmask[0:24, j * 512:(j + 1) * 512], start=False, stop=(j == 3)),
                 reads=[co.ident_bf, lmask], writes=[co.PA], tag="l2_maskmm")
        p.op("vector", lambda e: e.reduce_max(l2nc[:, 0:1], co.PA.h[0:24, 0:T], AX.X),
             reads=[co.PA], writes=[l2nc], tag="l2_m24")
        p.op("tensor", lambda t: t.transpose(
            co.paux[0:1, 0:24], l2nc[:, 0:1], co.ident_f32[0:24, 0:24]),
             reads=[l2nc, co.ident_f32], writes=[co.paux], tag="l2_mtr")
        p.op("vector", lambda e: e.tensor_copy(l2r1[:, 0:24], co.paux[0:1, 0:24]),
             reads=[co.paux], writes=[l2r1], tag="l2_mev")
        p.op("vector", lambda e: e.reduce_max(l2r1[:, 24:25], l2r1[:, 0:24], AX.X),
             reads=[l2r1], writes=[l2r1], tag="l2_mred")
        p.op("vector", lambda e: e.tensor_scalar_mul(l2r1[:, 25:26], l2r1[:, 24:25],
                                                     -0.125),
             reads=[l2r1], writes=[l2r1], tag="l2_mneg")
        p.op("tensor", lambda t: t.matmul(
            co.paux[0:24, 0:1], co.ones[0:1, 0:24], l2r1[:, 25:26],
            start=True, stop=True),
             reads=[co.ones, l2r1], writes=[co.paux], tag="l2_mbc")
        p.op("vector", lambda e: e.tensor_copy(l2nc[:, 0:1], co.paux[0:24, 0:1]),
             reads=[co.paux], writes=[l2nc], tag="l2_mbe")
        p.op("scalar", lambda e: e.activation(
            l2E[:, :], co.PA.h[0:24, 0:T], AF.Exp, scale=co.cv(0.125, 24),
            bias=l2nc[:, 0:1]),
             reads=[co.PA, co.cvt, l2nc], writes=[l2E], tag="l2_exp")
        for j in range(4):
            sl = slice(j * 512, (j + 1) * 512)
            p.op("tensor", lambda t, j=j: t.matmul(
                co.pm[4][0:2, 0:512], lden[0:24, 0:2], l2E[:, j * 512:(j + 1) * 512],
                start=True, stop=True),
                 reads=[lden, l2E], writes=[co.pm[4]], tag="l2_dmm")
            p.op("scalar", lambda e, j=j: e.activation(
                l2rd[:, 0:512], co.pm[4][0:2, 0:512], AF.Ln),
                 reads=[co.pm[4]], writes=[l2rd], tag="l2_dln")
            p.op("scalar", lambda e, j=j: e.activation(
                l2rd[:, 0:512], l2rd[:, 0:512],
                AF.Exp, scale=co.cv(-1.0, 2)),
                 reads=[l2rd, co.cvt], writes=[l2rd], tag="l2_dexp")
            acc, tmp = co.s512[1], co.s512[2]
            for o in range(12):
                off = L2_OFFS[o]
                pb_ = co.paux if o % 2 == 0 else co.pm[0]
                p.op("tensor", lambda t, o=o, j=j, pb_=pb_: t.matmul(
                    pb_[0:128, 0:512], lpb[0:24, 128 * o:128 * (o + 1)],
                    l2E[:, j * 512:(j + 1) * 512], start=True, stop=True),
                     reads=[lpb, l2E], writes=[pb_], tag="l2_pbm")
                if o == 0:
                    p.op("vector", lambda e, off=off, j=j, acc=acc, pb_=pb_: e.tensor_tensor(
                        acc[:, :], l2v[:, D - off + j * 512:D - off + (j + 1) * 512],
                        pb_[0:128, 0:512], ALU.mult),
                         reads=[l2v, pb_], writes=[acc], tag="l2_oa0")
                else:
                    p.op("vector", lambda e, off=off, j=j, tmp=tmp, pb_=pb_: e.tensor_tensor(
                        tmp[:, :], l2v[:, D - off + j * 512:D - off + (j + 1) * 512],
                        pb_[0:128, 0:512], ALU.mult),
                         reads=[l2v, pb_], writes=[tmp], tag="l2_oam")
                    p.op("vector", lambda e, acc=acc, tmp=tmp: e.tensor_add(
                        acc[:, :], acc[:, :], tmp[:, :]),
                         reads=[acc, tmp], writes=[acc], tag="l2_oaa")
            p.op("tensor", lambda t, j=j: t.matmul(
                co.paux[0:128, 0:512], lrb[0:2, 0:128], l2rd[0:2, 0:512],
                start=True, stop=True),
                 reads=[lrb, l2rd], writes=[co.paux], tag="l2_rbm")
            p.op("vector", lambda e, hp=hp, j=j, acc=acc: e.tensor_tensor(
                attnout[hp][:, j * 512:(j + 1) * 512], acc[:, :],
                co.paux[0:128, 0:512], ALU.mult),
                 reads=[acc, co.paux], writes=[attnout[hp]], tag="l2_onorm")
    for c in range(DC):
        ws = co.WS[c % 2]
        for d in range(DC):
            p.dma("sync", ws[d][:, 0:128],
                  s_proj.ap()[d * 128:(d + 1) * 128, c * 128:(c + 1) * 128],
                  reads=[s_proj], writes=[ws[d]], tag="l2_wpl")
        for j in range(NT):
            sl = slice(j * 512, (j + 1) * 512)
            pp = co.pm[j % 2]
            for d in range(DC):
                p.op("tensor", lambda t, d=d, sl=sl, ws=ws, pp=pp: t.matmul(
                    pp[0:128, 0:512], ws[d][:, 0:128], attnout[d][:, sl],
                    start=(d == 0), stop=(d == DC - 1)),
                     reads=[ws[d], attnout[d]], writes=[pp], tag="l2_mmp")
            xt = co.x[c]
            p.op("vector", lambda e, xt=xt, sl=sl, c=c, pp=pp: e.scalar_tensor_tensor(
                xt[:, sl], pp[0:128, 0:512], projb[:, c:c + 1], xt[:, sl],
                ALU.add, ALU.add),
                 reads=[pp, projb, xt], writes=[xt], tag="l2_pres")



def emit_l3(p, co):
    s_wq = stage_bf16(p, co, "L3_attn_Wq", [D, D])
    s_wk = stage_bf16(p, co, "L3_attn_Wk", [D, D])
    s_wv = stage_bf16(p, co, "L3_attn_Wv", [D, D])
    s_wo = stage_bf16(p, co, "L3_attn_Wo", [D, D])
    s_sh1 = stage_bf16(p, co, "L3_ffn_shared_w1", [4096, D])
    s_sh2 = stage_bf16(p, co, "L3_ffn_shared_w2", [D, 4096])
    s_ew1 = stage_bf16(p, co, "L3_ffn_experts_w1", [8, 4096, D])
    s_ew2 = stage_bf16(p, co, "L3_ffn_experts_w2", [8, D, 4096])
    s_gw = stage_bf16(p, co, "L3_ffn_gate_w", [8, D])

    emit_stats(p, co, co.x, False)
    emit_normalize(p, co, co.x, co.pb1)
    xn = co.pb1

    import os
    if not os.environ.get("L3_SKIP_FFN"):
        emit_l3_ffn(p, co, xn, s_sh1, s_sh2, s_ew1, s_ew2, s_gw)
    if not os.environ.get("L3_SKIP_ATTN"):
        emit_l3_attn(p, co, xn, s_wq, s_wk, s_wv, s_wo)


def emit_l3_ffn(p, co, xn, s_sh1, s_sh2, s_ew1, s_ew2, s_gw):
    esel = co.cload("c_esel")
    wr8 = p.sb("l3_wr8", [8, D], BF16)
    p.dma("sync", wr8[:, :], s_gw.ap(), reads=[s_gw], writes=[wr8], tag="l3_gwl")
    wrT = p.sb("l3_wrT", [128, 64], BF16)
    for d in range(DC):
        p.op("tensor", lambda t, d=d: t.transpose(
            co.ptb[0:128, 0:8], wr8[0:8, d * 128:(d + 1) * 128],
            co.ident_bf[0:8, 0:8]),
             reads=[wr8, co.ident_bf], writes=[co.ptb], tag="l3_gtr")
        p.op("vector", lambda e, d=d: e.tensor_copy(
            wrT[:, d * 8:(d + 1) * 8], co.ptb[0:128, 0:8]),
             reads=[co.ptb], writes=[wrT], tag="l3_gte")
    GT = emit_router(p, co, xn, wrT, "l3")

    def hslot(pool_sel, hh):
        t = pool_sel[hh // 4]
        off = (hh % 4) * 512
        return t, off

    gbt = co.s512[3]
    for q in range(NT):
        qsl = slice(q * 512, (q + 1) * 512)
        for hb in range(8):
            wa = co.WT[hb % 2]
            for ic in range(DC):
                p.dma("sync", wa[ic][:, :],
                      s_sh1.ap()[hb * 512:(hb + 1) * 512, ic * 128:(ic + 1) * 128],
                      reads=[s_sh1], writes=[wa[ic]], tag="l3_s1l", transpose=True)
            for hc in range(4):
                hh = 4 * hb + hc
                for ic in range(DC):
                    p.op("tensor", lambda t, ic=ic, hc=hc, qsl=qsl, wa=wa: t.matmul(
                        co.pm[hc][0:128, 0:512], wa[ic][:, hc * 128:(hc + 1) * 128],
                        xn[ic][:, qsl], start=(ic == 0), stop=(ic == DC - 1)),
                         reads=[wa[ic], xn[ic]], writes=[co.pm[hc]], tag="l3_s1mm")
            for hc in range(4):
                hh = 4 * hb + hc
                dstt, doff = hslot(co.pb2, hh)
                p.op("scalar", lambda e, hc=hc, dstt=dstt, doff=doff: e.activation(
                    dstt[:, doff:doff + 512], co.pm[hc][0:128, 0:512], AF.Silu),
                     reads=[co.pm[hc]], writes=[dstt], tag="l3_s1act")
        wtball = list(co.wtb) + list(co.wtb2)
        for cb in range(2):
            for hh in range(32):
                wt = wtball[hh % 16]
                p.dma("sync", wt[:, :],
                      s_sh2.ap()[cb * 512:(cb + 1) * 512, hh * 128:(hh + 1) * 128],
                      reads=[s_sh2], writes=[wt], tag="l3_s2l", transpose=True)
                st, soff = hslot(co.pb2, hh)
                for oc in range(4):
                    p.op("tensor", lambda t, wt=wt, oc=oc, st=st, soff=soff, hh=hh:
                         t.matmul(
                             co.pm[oc][0:128, 0:512], wt[:, oc * 128:(oc + 1) * 128],
                             st[:, soff:soff + 512],
                             start=(hh == 0), stop=(hh == 31)),
                         reads=[wt, st], writes=[co.pm[oc]], tag="l3_s2mm")
            for oc in range(4):
                cc = 4 * cb + oc
                xt = co.x[cc]
                p.op("vector", lambda e, oc=oc, xt=xt, qsl=qsl: e.tensor_add(
                    xt[:, qsl], xt[:, qsl], co.pm[oc][0:128, 0:512]),
                     reads=[xt, co.pm[oc]], writes=[xt], tag="l3_s2res")
        for e_ in range(8):
            for hb in range(8):
                wa = co.WT[hb % 2]
                for ic in range(DC):
                    p.dma("sync", wa[ic][:, :],
                          s_ew1.ap()[e_, hb * 512:(hb + 1) * 512, ic * 128:(ic + 1) * 128],
                          reads=[s_ew1], writes=[wa[ic]], tag="l3_e1l", transpose=True)
                for hc in range(4):
                    for ic in range(DC):
                        p.op("tensor", lambda t, ic=ic, hc=hc, qsl=qsl, wa=wa: t.matmul(
                            co.pm[hc][0:128, 0:512],
                            wa[ic][:, hc * 128:(hc + 1) * 128],
                            xn[ic][:, qsl], start=(ic == 0), stop=(ic == DC - 1)),
                             reads=[wa[ic], xn[ic]], writes=[co.pm[hc]], tag="l3_e1mm")
                for hc in range(4):
                    hh = 4 * hb + hc
                    dstt, doff = hslot(co.pb3, hh)
                    p.op("scalar", lambda e, hc=hc, dstt=dstt, doff=doff: e.activation(
                        dstt[:, doff:doff + 512], co.pm[hc][0:128, 0:512], AF.Silu),
                         reads=[co.pm[hc]], writes=[dstt], tag="l3_e1act")
            p.op("tensor", lambda t, e_=e_, qsl=qsl: t.matmul(
                co.paux[0:128, 0:512], esel[0:8, e_ * 128:(e_ + 1) * 128],
                GT[0:8, qsl], start=True, stop=True),
                 reads=[esel, GT], writes=[co.paux], tag="l3_gbc")
            p.op("vector", lambda e: e.tensor_copy(gbt[:, :], co.paux[0:128, 0:512]),
                 reads=[co.paux], writes=[gbt], tag="l3_gbcc")
            for cb in range(2):
                for hh in range(32):
                    wt = wtball[hh % 16]
                    p.dma("sync", wt[:, :],
                          s_ew2.ap()[e_, cb * 512:(cb + 1) * 512, hh * 128:(hh + 1) * 128],
                          reads=[s_ew2], writes=[wt], tag="l3_e2l", transpose=True)
                    st, soff = hslot(co.pb3, hh)
                    for oc in range(4):
                        p.op("tensor", lambda t, wt=wt, oc=oc, st=st, soff=soff, hh=hh:
                             t.matmul(
                                 co.pm[oc][0:128, 0:512], wt[:, oc * 128:(oc + 1) * 128],
                                 st[:, soff:soff + 512],
                                 start=(hh == 0), stop=(hh == 31)),
                             reads=[wt, st], writes=[co.pm[oc]], tag="l3_e2mm")
                for oc in range(4):
                    cc = 4 * cb + oc
                    xt = co.x[cc]
                    tmp = co.s512[2]
                    p.op("vector", lambda e, oc=oc, tmp=tmp: e.tensor_tensor(
                        tmp[:, :], co.pm[oc][0:128, 0:512], gbt[:, :], ALU.mult),
                         reads=[co.pm[oc], gbt], writes=[tmp], tag="l3_egate")
                    p.op("vector", lambda e, xt=xt, qsl=qsl, tmp=tmp: e.tensor_add(
                        xt[:, qsl], xt[:, qsl], tmp[:, :]),
                         reads=[xt, tmp], writes=[xt], tag="l3_eres")


def emit_router(p, co, xn, wrT, pref):
    GT = p.sb(pref + "_GT", [8, T], BF16)
    rtp = co.s512[0]
    for tc in range(TC):
        tsl = slice(tc * 128, (tc + 1) * 128)
        for d in range(DC):
            p.op("tensor", lambda t, d=d, tsl=tsl: t.matmul(
                co.pm[0][0:128, 0:8], xn[d][:, tsl], wrT[:, d * 8:(d + 1) * 8],
                start=(d == 0), stop=(d == DC - 1)),
                 reads=[xn[d], wrT], writes=[co.pm[0]], tag=pref + "_mmr")
        p.op("vector", lambda e: e.tensor_copy(rtp[:, 0:8], co.pm[0][0:128, 0:8]),
             reads=[co.pm[0]], writes=[rtp], tag=pref + "_rt")
        p.op("vector", lambda e: e.reduce_max(rtp[:, 16:17], rtp[:, 0:8], AX.X),
             reads=[rtp], writes=[rtp], tag=pref + "_m1")
        p.op("vector", lambda e: e.tensor_scalar(
            rtp[:, 24:32], rtp[:, 0:8], rtp[:, 16:17], None, ALU.is_equal),
             reads=[rtp], writes=[rtp], tag=pref + "_eq1")
        p.op("vector", lambda e: e.scalar_tensor_tensor(
            rtp[:, 8:16], rtp[:, 24:32], -1e30, rtp[:, 0:8], ALU.mult, ALU.add),
             reads=[rtp], writes=[rtp], tag=pref + "_mask1")
        p.op("vector", lambda e: e.reduce_max(rtp[:, 17:18], rtp[:, 8:16], AX.X),
             reads=[rtp], writes=[rtp], tag=pref + "_m2")
        p.op("vector", lambda e: e.tensor_scalar(
            rtp[:, 32:40], rtp[:, 8:16], rtp[:, 17:18], None, ALU.is_equal),
             reads=[rtp], writes=[rtp], tag=pref + "_eq2")
        p.op("vector", lambda e: e.tensor_sub(rtp[:, 18:19], rtp[:, 16:17], rtp[:, 17:18]),
             reads=[rtp], writes=[rtp], tag=pref + "_dm")
        p.op("scalar", lambda e: e.activation(rtp[:, 18:19], rtp[:, 18:19], AF.Sigmoid),
             reads=[rtp], writes=[rtp], tag=pref + "_g1")
        p.op("vector", lambda e: e.tensor_scalar(
            rtp[:, 19:20], rtp[:, 18:19], -1.0, 1.0, ALU.mult, ALU.add),
             reads=[rtp], writes=[rtp], tag=pref + "_g2")
        p.op("vector", lambda e: e.tensor_scalar(
            rtp[:, 24:32], rtp[:, 24:32], rtp[:, 18:19], None, ALU.mult),
             reads=[rtp], writes=[rtp], tag=pref + "_p1")
        p.op("vector", lambda e: e.scalar_tensor_tensor(
            rtp[:, 24:32], rtp[:, 32:40], rtp[:, 19:20], rtp[:, 24:32],
            ALU.mult, ALU.add),
             reads=[rtp], writes=[rtp], tag=pref + "_p2")
        gtile = p.sb(f"{pref}_g{tc}", [128, 8], BF16)
        p.op("vector", lambda e, gtile=gtile: e.tensor_copy(gtile[:, :], rtp[:, 24:32]),
             reads=[rtp], writes=[gtile], tag=pref + "_gcast")
        p.op("tensor", lambda t, gtile=gtile: t.transpose(
            co.ptb[0:8, 0:128], gtile[:, :], co.ident_bf[:, :]),
             reads=[gtile, co.ident_bf], writes=[co.ptb], tag=pref + "_gtr2")
        p.op("vector", lambda e, tsl=tsl: e.tensor_copy(GT[0:8, tsl], co.ptb[0:8, 0:128]),
             reads=[co.ptb], writes=[GT], tag=pref + "_gtc")
    return GT


def emit_l3_attn(p, co, xn, s_wq, s_wk, s_wv, s_wo):
    bq = vec_cols(p, co, "L3_attn_bq", DC, "l3_bq")
    bk = vec_cols(p, co, "L3_attn_bk", DC, "l3_bk")
    bo = vec_cols(p, co, "L3_attn_bo", DC, "l3_bo")
    bv_ext = co.ein("L3_attn_bv", [D], F32)
    bvr = p.sb("l3_bvr", [1, D], BF16)
    p.dma("gpsimd", bvr[:, :], bv_ext.ap().rearrange("(o n) -> o n", o=1),
          reads=[bv_ext], writes=[bvr], tag="l3_bvl")
    hsel = co.cload("c_l3hs")
    expb_ext = co.ein("c_l3expb", [T, T], BF16)
    l3q = [p.view(co.POOL3, 0, T), p.view(co.POOL3, T, T)]
    l3k = [p.view(co.POOL3, 2 * T, T), p.view(co.POOL3, 3 * T, T)]
    l3v = p.view(co.POOL3, 4 * T, 4096)
    l3es = p.view(co.POOL3, 4 * T + 4096, 512)
    l3es2 = p.view(co.POOL3, 4 * T + 4096 + 512, 512)
    l3eb = [p.view(co.POOL3, 4 * T + 4096 + 1024 + i * 512, 512) for i in range(2)]
    l3rb = p.sb("l3_rb", [64, 512], F32)
    l3nc = p.sb("l3_nc", [128, 8], F32)
    l3sm = p.sb("l3_sm", [2, 32], F32)
    attnout = co.pb2
    pm01 = PsumView(co.PA, 0, 1024)
    pm23 = PsumView(co.PA, 1024, 1024)
    pden = PsumView(co.PB, 0, 1024)
    for quad in range(4):
        for part, s_w, bias in ((0, s_wq, bq), (1, s_wk, bk)):
            dstp = l3q if part == 0 else l3k
            for hp in range(2):
                ws = co.WS[0]
                for d in range(DC):
                    p.dma("sync", ws[d][:, 0:128],
                          s_w.ap()[d * 128:(d + 1) * 128,
                                   quad * 256 + hp * 128:quad * 256 + (hp + 1) * 128],
                          reads=[s_w], writes=[ws[d]], tag="l3_wl")
                for j in range(NT):
                    sl = slice(j * 512, (j + 1) * 512)
                    pp = co.pm[j % 2]
                    for d in range(DC):
                        p.op("tensor", lambda t, d=d, sl=sl, ws=ws, pp=pp: t.matmul(
                            pp[0:128, 0:512], ws[d][:, 0:128], xn[d][:, sl],
                            start=(d == 0), stop=(d == DC - 1)),
                             reads=[ws[d], xn[d]], writes=[pp], tag="l3_mmqk")
                    p.op("vector", lambda e, hp=hp, sl=sl, quad=quad, bias=bias,
                         dstp=dstp, pp=pp: e.tensor_scalar_add(
                             dstp[hp][:, sl], pp[0:128, 0:512],
                             bias[:, 2 * quad + hp:2 * quad + hp + 1]),
                         reads=[pp, bias], writes=[dstp[hp]], tag="l3_qke")
        wsv = co.WS[0]
        for d in range(DC):
            p.dma("sync", wsv[d][:, 0:256],
                  s_wv.ap()[d * 128:(d + 1) * 128, quad * 256:(quad + 1) * 256],
                  reads=[s_wv], writes=[wsv[d]], tag="l3_wvl")
        for s in range(TC):
            ssl = slice(s * 128, (s + 1) * 128)
            pv = co.pm[1] if s % 2 == 0 else co.pm[2]
            for d in range(DC):
                p.op("tensor", lambda t, d=d, ssl=ssl, pv=pv: t.matmul(
                    pv[0:128, 0:256], xn[d][:, ssl], wsv[d][:, 0:256],
                    start=(d == 0), stop=False),
                     reads=[xn[d], wsv[d]], writes=[pv], tag="l3_mmv")
            p.op("tensor", lambda t, quad=quad, pv=pv: t.matmul(
                pv[0:128, 0:256], co.ones_bf[0:1, 0:128],
                bvr[:, quad * 256:(quad + 1) * 256], start=False, stop=True),
                 reads=[co.ones_bf, bvr], writes=[pv], tag="l3_mmvb")
            p.op("vector", lambda e, s=s, pv=pv: e.tensor_copy(
                l3v[:, s * 256:(s + 1) * 256], pv[0:128, 0:256]),
                 reads=[pv], writes=[l3v], tag="l3_vev")
        for hp in range(2):
            for (src, col) in ((l3q[hp], 0), (l3k[hp], 2)):
                sq = co.s512[1]
                for j in range(NT):
                    p.op("scalar", lambda e, src=src, j=j, sq=sq: e.activation(
                        sq[:, :], src[:, j * 512:(j + 1) * 512], AF.Square),
                         reads=[src], writes=[sq], tag="l3_nsq")
                    p.op("tensor", lambda t, j=j, sq=sq: t.matmul(
                        co.pm[4][0:2, 0:512], hsel[0:128, 0:2], sq[:, :],
                        start=True, stop=True),
                         reads=[hsel, sq], writes=[co.pm[4]], tag="l3_nmm")
                    p.op("vector", lambda e, j=j, col=col: e.reduce_max(
                        l3sm[:, col + 16 + j:col + 17 + j], co.pm[4][0:2, 0:512], AX.X),
                         reads=[co.pm[4]], writes=[l3sm], tag="l3_nred")
                p.op("vector", lambda e, col=col: e.reduce_max(
                    l3sm[:, col:col + 1], l3sm[:, col + 16:col + 20], AX.X),
                     reads=[l3sm], writes=[l3sm], tag="l3_nred2")
            p.op("vector", lambda e: e.tensor_mul(l3sm[:, 4:5], l3sm[:, 0:1],
                                                  l3sm[:, 2:3]),
                 reads=[l3sm], writes=[l3sm], tag="l3_nm")
            p.op("scalar", lambda e: e.activation(l3sm[:, 4:5], l3sm[:, 4:5], AF.Ln),
                 reads=[l3sm], writes=[l3sm], tag="l3_nln")
            p.op("scalar", lambda e: e.activation(l3sm[:, 4:5], l3sm[:, 4:5], AF.Exp,
                                                  scale=co.cv(0.5, 2)),
                 reads=[l3sm, co.cvt], writes=[l3sm], tag="l3_nexp")
            p.op("vector", lambda e: e.tensor_scalar_mul(l3sm[:, 5:6], l3sm[:, 4:5],
                                                         -0.125),
                 reads=[l3sm], writes=[l3sm], tag="l3_nneg")
            p.op("tensor", lambda t: t.transpose(
                co.paux[0:1, 0:2], l3sm[0:2, 5:6], co.ident_f32[0:2, 0:2]),
                 reads=[l3sm, co.ident_f32], writes=[co.paux], tag="l3_ntr")
            p.op("vector", lambda e: e.tensor_copy(l3sm[0:1, 6:8], co.paux[0:1, 0:2]),
                 reads=[co.paux], writes=[l3sm], tag="l3_nte")
            for hh in range(2):
                p.op("tensor", lambda t, hh=hh: t.matmul(
                    co.paux[0:128, 0:1], co.ones[0:1, :], l3sm[0:1, 6 + hh:7 + hh],
                    start=True, stop=True),
                     reads=[co.ones, l3sm], writes=[co.paux], tag="l3_nbc")
                p.op("vector", lambda e, hp=hp, hh=hh: e.tensor_copy(
                    l3nc[:, 2 * hp + hh:2 * hp + hh + 1], co.paux[0:128, 0:1]),
                     reads=[co.paux], writes=[l3nc], tag="l3_nbe")
        for h in range(4):
            hp, hq = h // 2, h % 2
            base = hq * 64
            for qq in range(4):
                qof = qq * 512
                obank = PsumView(co.PA, (qq % 2) * 512, 512)
                for s in range(TC):
                    sbank = PsumView(co.PA, 1024 + (s % 2) * 512, 512)
                    ebuf = l3eb[s % 2]
                    p.dma("sync", ebuf[:, 0:512],
                          expb_ext.ap()[s * 128:(s + 1) * 128, qof:qof + 512],
                          reads=[expb_ext], writes=[ebuf], tag="l3_ebl")
                    p.op("tensor", lambda t, s=s, qof=qof, hp=hp, base=base,
                         sbank=sbank: t.matmul(
                             sbank[0:128, 0:512],
                             l3k[hp][base:base + 64, s * 128:(s + 1) * 128],
                             l3q[hp][base:base + 64, qof:qof + 512],
                             start=True, stop=True),
                         reads=[l3k[hp], l3q[hp]], writes=[sbank], tag="l3_smm")
                    esl = l3es if s % 2 == 0 else l3es2
                    p.op("scalar", lambda e, h=h, sbank=sbank, esl=esl: e.activation(
                        esl[:, 0:512], sbank[0:128, 0:512], AF.Exp,
                        scale=co.cv(0.125), bias=l3nc[:, h:h + 1]),
                         reads=[sbank, co.cvt, l3nc], writes=[esl], tag="l3_sexp")
                    p.op("vector", lambda e, ebuf=ebuf, esl=esl: e.tensor_mul(
                        esl[:, 0:512], esl[:, 0:512], ebuf[:, 0:512]),
                         reads=[esl, ebuf], writes=[esl], tag="l3_eb")
                    p.op("tensor", lambda t, s=s, h=h, obank=obank, esl=esl: t.matmul(
                        obank[0:64, 0:512],
                        l3v[:, s * 256 + h * 64:s * 256 + h * 64 + 64],
                        esl[:, 0:512],
                        start=(s == 0), stop=(s == TC - 1)),
                         reads=[l3v, esl], writes=[obank], tag="l3_omm")
                    p.op("tensor", lambda t, s=s, qq=qq, esl=esl: t.matmul(
                        co.PB.h[0:1, (qq % 2) * 512:(qq % 2) * 512 + 512],
                        co.ones_bf[:, 0:1], esl[:, 0:512],
                        start=(s == 0), stop=(s == TC - 1)),
                         reads=[co.ones_bf, esl], writes=[pden], tag="l3_dmm")
                rsrc = co.s512[1]
                p.op("scalar", lambda e, qq=qq, rsrc=rsrc: e.activation(
                    rsrc[0:1, 0:512], co.PB.h[0:1, (qq % 2) * 512:(qq % 2) * 512 + 512],
                    AF.Ln),
                     reads=[pden], writes=[rsrc], tag="l3_dln1")
                p.op("scalar", lambda e, rsrc=rsrc: e.activation(
                    rsrc[0:1, 0:512], rsrc[0:1, 0:512], AF.Exp,
                    scale=co.cv(-1.0, 1)),
                     reads=[rsrc, co.cvt], writes=[rsrc], tag="l3_dex1")
                p.op("tensor", lambda t, rsrc=rsrc: t.matmul(
                    co.paux[0:64, 0:512], co.ones[0:1, 0:64], rsrc[0:1, 0:512],
                    start=True, stop=True),
                     reads=[co.ones, rsrc], writes=[co.paux], tag="l3_rbm")
                p.op("vector", lambda e: e.tensor_copy(
                    l3rb[:, 0:512], co.paux[0:64, 0:512]),
                     reads=[co.paux], writes=[l3rb], tag="l3_rbe")
                dst = attnout[(quad * 4 + h) // 2]
                rbase = ((quad * 4 + h) % 2) * 64
                p.op("vector", lambda e, dst=dst, rbase=rbase, qof=qof,
                     obank=obank: e.tensor_tensor(
                         dst[rbase:rbase + 64, qof:qof + 512], obank[0:64, 0:512],
                         l3rb[0:64, 0:512], ALU.mult),
                     reads=[obank, l3rb], writes=[dst], tag="l3_onorm")
    for c in range(DC):
        ws = co.WS[c % 2]
        for d in range(DC):
            p.dma("sync", ws[d][:, 0:128],
                  s_wo.ap()[d * 128:(d + 1) * 128, c * 128:(c + 1) * 128],
                  reads=[s_wo], writes=[ws[d]], tag="l3_wol")
        for j in range(NT):
            sl = slice(j * 512, (j + 1) * 512)
            pp = co.pm[j % 2]
            for d in range(DC):
                p.op("tensor", lambda t, d=d, sl=sl, ws=ws, pp=pp: t.matmul(
                    pp[0:128, 0:512], ws[d][:, 0:128], attnout[d][:, sl],
                    start=(d == 0), stop=(d == DC - 1)),
                     reads=[ws[d], attnout[d]], writes=[pp], tag="l3_mmo")
            xt = co.x[c]
            p.op("vector", lambda e, xt=xt, sl=sl, c=c, pp=pp: e.scalar_tensor_tensor(
                xt[:, sl], pp[0:128, 0:512], bo[:, c:c + 1], xt[:, sl],
                ALU.add, ALU.add),
                 reads=[pp, bo, xt], writes=[xt], tag="l3_pres")



_PROG_CACHE = {}


def build_program(layers):
    nc = bass.Bass("TRN2", target_bir_lowering=False, debug=False)
    p = P(nc)
    co = Core(p)
    x_in = p.dram("x_in", [D, T], F32, kind="ExternalInput")
    x_out = p.dram("x_out", [D, T], F32, kind="ExternalOutput")
    co.ext["x_in"] = x_in
    co.load_consts()
    for d in range(DC):
        p.dma("sync", co.x[d][:, :], x_in.ap()[d * 128:(d + 1) * 128, :],
              reads=[x_in], writes=[co.x[d]], tag="xload")
    for li in layers:
        co.lazy_consts = {}
        with p.scope():
            if li == 0:
                emit_l0(p, co)
            elif li == 1:
                emit_l1(p, co)
            elif li == 2:
                emit_l2(p, co)
            elif li == 3:
                emit_l3(p, co)
            else:
                raise NotImplementedError(f"layer {li}")
    for d in range(DC):
        p.dma("sync", x_out.ap()[d * 128:(d + 1) * 128, :], co.x[d][:, :],
              reads=[co.x[d]], writes=[x_out], tag="xstore")
    names = set(co.ext.keys())
    counts = p.n_ops()
    p.emit_and_close()
    cvarr = np.zeros((128, 16), np.float32)
    for v, col in co.cvals.items():
        cvarr[:, col] = v
    return nc, names, counts, cvarr


def _get_program(layers):
    key = tuple(layers)
    if key not in _PROG_CACHE:
        _PROG_CACHE[key] = build_program(layers)
    return _PROG_CACHE[key]


def kernel(**inputs):
    x = np.asarray(inputs["L0_attn_x"])
    xT = [np.ascontiguousarray(x[b].T) for b in range(NB)]
    for group in GROUPS:
        nc, names, counts, cvarr = _get_program(group)
        in_maps = []
        for b in range(NB):
            m = {"x_in": xT[b]}
            for n in names:
                if n == "x_in":
                    continue
                if n == "c_cvals":
                    m[n] = cvarr
                elif n.startswith("c_"):
                    m[n] = CONSTS[n]
                else:
                    m[n] = np.asarray(inputs[n])
            in_maps.append(m)
        res = run_bass_kernel_spmd(nc, in_maps, core_ids=list(range(NB)))
        xT = [np.asarray(r["x_out"]) for r in res.results]
    return np.stack([t.T for t in xT], axis=0)
